# version 01
# baseline (speedup 1.0000x reference)
import numpy as np
import ml_dtypes
import concourse.bass as bass
import concourse.mybir as mybir
from concourse.bass_utils import run_bass_kernel_spmd

F32 = mybir.dt.float32
BF16 = mybir.dt.bfloat16
I32 = mybir.dt.int32
U8 = mybir.dt.uint8
OP = mybir.AluOpType
AF = mybir.ActivationFunctionType
AX = mybir.AxisListType
BF = ml_dtypes.bfloat16

D = 4096
NT = 1024
NC = 8
DK = 128
KVH = 8
GRP = 4
CH = 128
NCH = 8
HID = 11008
HID_PAD = 11264
SH12 = HID_PAD // NC
SH3 = D // NC
THETA = 10000.0
MSK = 30000.0
ISQD = 0.08838834764831845
NF4 = np.array([-1.0, -0.6961928009986877, -0.5250730514526367, -0.39491748809814453,
                -0.28444138169288635, -0.18477343022823334, -0.09105003625154495, 0.0,
                0.07958029955625534, 0.16093020141124725, 0.24611230194568634,
                0.33791524171829224, 0.44070982933044434, 0.5626170039176941,
                0.7229568362236023, 1.0], dtype=np.float32)
NF4_BF = NF4.astype(BF)

CP_D1 = 16; CP_2STEP = 17; CP_EPS = 18; CP_INVD = 19; CP_HALF = 20
CP_1P5 = 21; CP_NEG1 = 22; CP_ISQD = 23; CP_NHALF = 24; CP_G16 = 25; CP_P16 = 26


def _host_constants():
    c = {}
    c["ident"] = np.eye(128, dtype=np.float32).astype(BF)
    c["identf"] = np.eye(128, dtype=np.float32)
    rev = np.zeros((128, 128), np.float32)
    rev[np.arange(128), 127 - np.arange(128)] = 1.0
    c["rev"] = rev.astype(BF)
    c["triu_ji"] = np.triu(np.ones((128, 128), np.float32)).astype(BF)
    jj = np.arange(128)[:, None]
    ii = np.arange(128)[None, :]
    c["mask2"] = np.where((jj <= ii) & (jj >= ii - 64), 0.0, -MSK).astype(np.float32)
    c["mask1e"] = np.where(jj >= ii + 64, 0.0, -MSK).astype(np.float32)
    bs = np.zeros((8, 128), np.float32)
    for e in range(8):
        bs[e, e * 16:(e + 1) * 16] = 1.0
    c["blocksel"] = bs
    c["ones_col"] = np.ones((128, 1), np.float32).astype(BF)
    c["ones_row"] = np.ones((1, 128), np.float32).astype(BF)
    pm = np.zeros((128, 128), np.float32)
    for d in range(128):
        pm[(d + 64) % 128, d] = 1.0
    c["perm"] = pm.astype(BF)
    k2 = np.arange(64)
    W64 = np.exp(2j * np.pi * np.outer(k2, k2) / 64.0)
    c["W64re"] = W64.real.astype(np.float32)
    c["W64im"] = W64.imag.astype(np.float32)
    Tw = np.exp(2j * np.pi * np.outer(k2, k2) / 4096.0)
    c["Twre"] = Tw.real.astype(np.float32)
    c["Twim"] = Tw.imag.astype(np.float32)
    W64b = np.exp(2j * np.pi * np.outer(k2, k2) / 64.0) / 4096.0
    c["W64bre"] = W64b.real.astype(np.float32)
    c["W64bim"] = W64b.imag.astype(np.float32)
    t32 = (np.arange(4096).astype(np.float32) / np.float32(4096.0)).astype(np.float32)
    ang = (np.float32(-2.0 * np.pi) * t32).astype(np.float32)
    om = (np.cos(ang.astype(np.float64)) + 1j * np.sin(ang.astype(np.float64)))
    om = om.astype(np.complex64).astype(np.complex128)
    g0 = (1.0 - om) / (1.0 + om)
    cc2 = 2.0 / (1.0 + om)
    c["g0re"] = g0.real.astype(np.float32).reshape(64, 64)
    c["g0im"] = g0.imag.astype(np.float32).reshape(64, 64)
    c["ccre"] = cc2.real.astype(np.float32).reshape(64, 64)
    c["ccim"] = cc2.imag.astype(np.float32).reshape(64, 64)
    return c


def _host_rope(pos0):
    pos = np.arange(pos0 - 64, pos0 + NT).astype(np.float64)
    half = np.arange(64, dtype=np.float64)
    inv = 1.0 / THETA ** (2.0 * half / DK)
    fr = np.maximum(pos, 0.0)[:, None] * inv[None, :]
    emb = np.concatenate([fr, fr], axis=1)
    return np.cos(emb).astype(np.float32), np.sin(emb).astype(np.float32)


class Prog:

    def __init__(self, nc):
        self.nc = nc
        self.ops = {k: [] for k in "gscvt"}
        self.sems = {}
        self.cnt = {}
        self.pending = {k: [] for k in "gscvt"}
        self.last = {k: None for k in "gscvt"}

    def sem(self, name):
        if name not in self.sems:
            self.sems[name] = self.nc.ctx.enter_context(self.nc.semaphore(name))
            self.cnt[name] = 0
        return self.sems[name]

    def on(self, eng, fn):
        self.ops[eng].append(fn)

    def wait(self, eng, name, val=None):
        s = self.sem(name)
        v = self.cnt[name] if val is None else val
        self.ops[eng].append(lambda e, s=s, v=v: e.wait_ge(s, v))
        return v

    def mark(self, eng, name, by=1):
        s = self.sem(name)
        self.cnt[name] += by
        self.pending[eng].append((s, by))
        return self.cnt[name]

    def _take_marks(self, eng):
        m = self.pending[eng]
        self.pending[eng] = []
        return m

    def chained(self, eng):
        return Chained(self, eng)

    def seq(self, eng, fn, name=None, by=1):
        marks = self._take_marks(eng)
        if name is not None:
            s = self.sem(name)
            self.cnt[name] += by
            marks = marks + [(s, by)]
        def run(e, fn=fn, marks=marks):
            i = fn(e)
            for ms, mv in marks:
                i = i.then_inc(ms, mv)
        self.ops[eng].append(run)
        if name is not None:
            self.last[eng] = (name, self.cnt[name])

    def dma(self, eng, out, in_, sem_name, **kw):
        s = self.sem(sem_name)
        self.cnt[sem_name] += 16
        marks = self._take_marks(eng)
        nc = self.nc
        def run(e, out=out, in_=in_, kw=kw, s=s, marks=marks):
            n0 = nc.n_instructions()
            i = e.dma_start(out=out, in_=in_, **kw)
            assert nc.n_instructions() - n0 == 1, "dma split"
            i = i.then_inc(s, 16)
            for ms, mv in marks:
                i = i.then_inc(ms, mv)
        self.ops[eng].append(run)
        self.last[eng] = (sem_name, self.cnt[sem_name])
        return self.cnt[sem_name]

    def barrier(self):
        lasts = {e: self.last[e] for e in "gscvt" if self.last[e] is not None}
        for e in "gscvt":
            if not self.ops[e]:
                continue
            for e2, (nm, vv) in lasts.items():
                if e2 != e:
                    self.wait(e, nm, vv)

    def emit(self, block):
        for tag, attr in (("g", "gpsimd"), ("s", "sync"), ("c", "scalar"),
                          ("v", "vector"), ("t", "tensor")):
            ops = self.ops[tag]
            if ops:
                getattr(block, attr)(lambda e, ops=ops: [f(e) for f in ops])


class Chained:

    def __init__(self, pr, eng):
        object.__setattr__(self, "_pr", pr)
        object.__setattr__(self, "_eng", eng)

    def __getattr__(self, opname):
        pr, eng = self._pr, self._eng
        sname = eng + "_self"
        s = pr.sem(sname)
        def call(*args, **kw):
            prev = pr.cnt[sname]
            pr.cnt[sname] += 1
            marks = pr._take_marks(eng)
            def run(e, prev=prev, opname=opname, args=args, kw=kw, marks=marks):
                if prev > 0:
                    e.wait_ge(s, prev)
                i = getattr(e, opname)(*args, **kw).then_inc(s, 1)
                for ms, mv in marks:
                    i = i.then_inc(ms, mv)
            pr.ops[eng].append(run)
            pr.last[eng] = (sname, pr.cnt[sname])
            return pr.cnt[sname]
        return call


def ap3(t, extra_off, dims):
    a = t if isinstance(t, bass.AP) else t.ap()
    return bass.AP(a.tensor, a.offset + extra_off, [list(a.ap[0])] + [list(d) for d in dims])


def build_program(debug=None):
    debug = debug or {}
    nc = bass.Bass("TRN2", target_bir_lowering=False)
    pr = Prog(nc)
    C = _host_constants()

    ein = lambda n, s, dt=F32: nc.dram_tensor(n, s, dt, kind="ExternalInput")
    dint = lambda n, s, dt=F32: nc.dram_tensor(n, s, dt, kind="Internal")

    xT = ein("xT", [D, NT]); xTh = ein("xTh", [D, 64])
    Wq = ein("Wq", [D, D]); Wk = ein("Wk", [D, 1024]); Wv = ein("Wv", [D, 1024])
    Wo = ein("Wo", [D, D])
    routeT = ein("routeT", [D, 8]); Aflat = ein("Aflat", [D, 128]); BwT = ein("BwT", [128, D])
    prow_i = ein("prow", [1, 128]); qrow_i = ein("qrow", [1, 128]); lamr = ein("lamr", [2, 64])
    VcTre = ein("VcTre", [64, 64]); VcTim = ein("VcTim", [64, 64])
    Ctr = ein("Ctr", [1, 128])
    Bp = ein("Bp", [64, 1]); Dlog = ein("Dlog", [1, 4])
    w1qT = ein("w1qT", [D, SH12], I32); w2qT = ein("w2qT", [D, SH12], I32)
    w3qT = ein("w3qT", [HID, SH3], I32)
    s1T = ein("s1T", [64, SH12]); s2T = ein("s2T", [64, SH12]); s3T = ein("s3T", [172, SH3])
    cosTt = ein("cosTt", [128, NT + 64], BF16); sinTt = ein("sinTt", [128, NT + 64], BF16)
    cin = {k: ein("c_" + k, list(v.shape), BF16 if v.dtype == BF else F32) for k, v in C.items()}
    mask1c0 = ein("mask1c0", [64, 128])
    scanm = ein("scanm", [128, 16])
    out = nc.dram_tensor("out", [D, NT], F32, kind="ExternalOutput")
    dbg_outs = {}
    if "Kout" in debug:
        dbg_outs["Kout"] = nc.dram_tensor("Kout", [1, 4352], F32, kind="ExternalOutput")
        dbg_outs["Kacc"] = nc.dram_tensor("Kacc", [64, 512], F32, kind="ExternalOutput")
        dbg_outs["Kar"] = nc.dram_tensor("Kar", [64, 128], F32, kind="ExternalOutput")
        dbg_outs["Krows"] = nc.dram_tensor("Krows", [1, 642], F32, kind="ExternalOutput")
    if "x1out" in debug:
        dbg_outs["x1out"] = nc.dram_tensor("x1out", [D, NT], F32, kind="ExternalOutput")
    if "x2out" in debug:
        dbg_outs["x2out"] = nc.dram_tensor("x2out", [D, NT], F32, kind="ExternalOutput")
        dbg_outs["rwout"] = nc.dram_tensor("rwout", [8, NT], F32, kind="ExternalOutput")
    if "x3out" in debug:
        dbg_outs["x3out"] = nc.dram_tensor("x3out", [D, NT], F32, kind="ExternalOutput")
    if "aout" in debug:
        dbg_outs["qout"] = nc.dram_tensor("qout", [D, NT], BF16, kind="ExternalOutput")
        dbg_outs["oout"] = nc.dram_tensor("oout", [D, NT], BF16, kind="ExternalOutput")
        dbg_outs["attnout"] = nc.dram_tensor("attnout", [D, NT], F32, kind="ExternalOutput")
        dbg_outs["adiag"] = nc.dram_tensor("adiag", [128, 256], F32, kind="ExternalOutput")
        dbg_outs["adiag2"] = nc.dram_tensor("adiag2", [3, 3072], F32, kind="ExternalOutput")

    x1d = dint("x1d", [D, NT]); x2d = dint("x2d", [D, NT]); x3d = dint("x3d", [D, NT])
    attnTd = dint("attnTd", [D, NT])
    qdr = dint("qdr", [D, NT], BF16)
    odr = dint("odr", [D, NT], BF16)
    hdr = dint("hdr", [HID, NT], BF16)
    Kpad = dint("Kpad", [1, 4352])
    rowsd = dint("rowsd", [4, 8192])
    ag_s_in = dint("ag_s_in", [KVH, 130, 128])
    ag_s_out = dint("ag_s_out", [NC, KVH, 130, 128])
    agw_in = {1: dint("agw1i", [D, SH12], BF16), 2: dint("agw2i", [D, SH12], BF16),
              3: dint("agw3i", [HID, SH3], BF16)}
    agw_out = {1: dint("agw1o", [NC, D, SH12], BF16), 2: dint("agw2o", [NC, D, SH12], BF16),
               3: dint("agw3o", [NC, HID, SH3], BF16)}

    glob = nc.ctx
    sbg = lambda n, s, dt: glob.enter_context(nc.sbuf_tensor(n, s, dt))

    cpool = sbg("cpool", [128, 32], F32)
    tpool = sbg("tpool", [128, 16], BF16)
    ct = {k: sbg("ct_" + k, list(v.shape), BF16 if v.dtype == BF else F32).ap() for k, v in C.items()}
    m1c0_t = sbg("m1c0_t", [64, 128], F32)
    scanm_t = sbg("scanm_t", [128, 16], F32)
    global_stat = [None]
    global_statB = [None]
    wkD = sbg("wkD", [128, 2048], BF16)
    PS = [glob.enter_context(nc.psum_tensor(f"p{i}", [128, 512], F32)) for i in range(7)]
    PSB = glob.enter_context(nc.psum_tensor("pb7", [128, 1024], BF16))

    def setup():
        def f(g):
            for v in range(16):
                g.memset(cpool[:, v:v + 1], float(1 - v))
                g.memset(tpool[:, v:v + 1], float(NF4_BF[v]))
            g.memset(cpool[:, CP_EPS:CP_EPS + 1], 1e-5)
            g.memset(cpool[:, CP_INVD:CP_INVD + 1], 1.0 / D)
            g.memset(cpool[:, CP_HALF:CP_HALF + 1], 0.5)
            g.memset(cpool[:, CP_1P5:CP_1P5 + 1], 1.5)
            g.memset(cpool[:, CP_NEG1:CP_NEG1 + 1], -1.0)
            g.memset(cpool[:, CP_ISQD:CP_ISQD + 1], ISQD)
            g.memset(cpool[:, CP_NHALF:CP_NHALF + 1], -0.5)
            g.memset(cpool[:, CP_G16:CP_G16 + 1], -1.0 / 16.0)
            g.memset(cpool[:, CP_P16:CP_P16 + 1], 1.0 / 16.0)
        pr.on("g", f)
        for k in C:
            pr.dma("g", ct[k][:, :], cin[k].ap(), "cld")
        pr.dma("g", m1c0_t[:, :], mask1c0.ap(), "cld")
        pr.dma("g", scanm_t[:, :], scanm.ap(), "cld")
        for e in "vcts":
            pr.wait(e, "cld")

    def phaseK(sk):
        wkK = sk["wkK"]
        V = pr.chained("v")
        Cc = pr.chained("c")
        pr.dma("g", sk["p"][:, :], prow_i.ap(), "kld")
        pr.dma("g", sk["q"][:, :], qrow_i.ap(), "kld")
        pr.dma("g", sk["vctre"][:, :], VcTre.ap(), "kld")
        pr.dma("g", sk["vctim"][:, :], VcTim.ap(), "kld")
        pr.dma("g", sk["ct"][:, :], Ctr.ap(), "kld")
        pr.dma("g", sk["bp"][:, :], Bp.ap(), "kld")
        pr.dma("g", sk["dl"][:, :], Dlog.ap(), "kld")
        pr.dma("g", sk["lsr"][0:1, 0:64], bass.AP(Dlog, 1, [[1, 1], [0, 64], [1, 1]]), "kld")
        pr.wait("t", "kld")
        def f(t):
            t.matmul(PS[0][0:1, 0:64], sk["bp"][:, :], sk["vctre"][:, :], start=True, stop=True)
            return t.matmul(PS[1][0:1, 0:64], sk["bp"][:, :], sk["vctim"][:, :], start=True, stop=True)
        pr.seq("t", f, "kmmB")
        pr.wait("c", "kld")
        kexp = Cc.activation(sk["lsr"][0:1, 64:128], sk["lsr"][0:1, 0:64], AF.Exp)
        pr.wait("v", "kmmB")
        pr.wait("v", "c_self", kexp)
        row = sk["rows"]
        t0 = wkK[0:1, 5120:5184]
        t1 = wkK[0:1, 5184:5248]
        V.tensor_copy(row[0:1, 0:64], PS[0][0:1, 0:64])
        V.tensor_copy(row[0:1, 64:128], PS[1][0:1, 0:64])
        def cmc(ar, ai, br, bi, orr, oi):
            V.tensor_tensor(t0, ar, br, OP.mult)
            V.tensor_tensor(t1, ai, bi, OP.mult)
            V.tensor_tensor(orr, t0, t1, OP.add)
            V.tensor_tensor(t0, ar, bi, OP.mult)
            V.tensor_tensor(t1, ai, br, OP.mult)
            V.tensor_tensor(oi, t0, t1, OP.subtract)
        ctR, ctI = sk["ct"][0:1, 0:64], sk["ct"][0:1, 64:128]
        pR, pI = sk["p"][0:1, 0:64], sk["p"][0:1, 64:128]
        qR, qI = sk["q"][0:1, 0:64], sk["q"][0:1, 64:128]
        bR, bI = row[0:1, 0:64], row[0:1, 64:128]
        cmc(ctR, ctI, bR, bI, row[0:1, 128:192], row[0:1, 192:256])
        cmc(ctR, ctI, pR, pI, row[0:1, 256:320], row[0:1, 320:384])
        cmc(qR, qI, bR, bI, row[0:1, 384:448], row[0:1, 448:512])
        cmc(qR, qI, pR, pI, row[0:1, 512:576], row[0:1, 576:640])
        V.reciprocal(sk["lsr"][0:1, 128:192], sk["lsr"][0:1, 64:128])
        V.tensor_scalar(row[0:1, 640:641], sk["lsr"][0:1, 128:129], 2.0, None, OP.mult)
        krow = V.tensor_scalar(row[0:1, 641:642], sk["dl"][0:1, 0:1], 1.0, None, OP.add)
        pr.wait("g", "v_self", krow)
        pr.dma("g", rowsd.ap()[0:1, 0:642], row[0:1, 0:642], "krd")
        pr.wait("g", "krd")
        lamb = sk["lamb"]; v4b = sk["v4b"]
        pr.dma("g", lamb[:, :], bass.AP(lamr, 0, [[0, 64], [1, 128]]), "kld2")
        pr.dma("g", v4b[:, :], bass.AP(rowsd, 128, [[0, 64], [1, 512]]), "kld2")
        pr.dma("g", cpool[:, CP_2STEP:CP_2STEP + 1], bass.AP(rowsd, 640, [[0, 128], [1, 1]]), "kld2")
        pr.dma("g", cpool[:, CP_D1:CP_D1 + 1], bass.AP(rowsd, 641, [[0, 128], [1, 1]]), "kld2")
        pr.wait("v", "kld2")
        acc = {nm: (sk["acc"][0:64, i * 128:i * 128 + 64],
                    sk["acc"][0:64, i * 128 + 64:(i + 1) * 128])
               for i, nm in enumerate(["k00", "k01", "k10", "k11"])}
        dre = wkK[0:64, 0:1024]; dim = wkK[0:64, 1024:2048]
        inv = wkK[0:64, 2048:3072]; ta = wkK[0:64, 3072:4096]
        tb = wkK[0:64, 4096:5120]
        gre_q = wkK[0:64, 5120:5136]; gim_q = wkK[0:64, 5136:5152]
        red = wkK[0:64, 5152:5216]
        for nm in ["k00", "k01", "k10", "k11"]:
            V.memset(acc[nm][0], 0.0)
            V.memset(acc[nm][1], 0.0)
        for qtr in range(4):
            k1o = qtr * 16
            V.tensor_scalar(gre_q, ct["g0re"][:, k1o:k1o + 16],
                            cpool[0:64, CP_2STEP:CP_2STEP + 1], None, OP.mult)
            V.tensor_scalar(gim_q, ct["g0im"][:, k1o:k1o + 16],
                            cpool[0:64, CP_2STEP:CP_2STEP + 1], None, OP.mult)
            g3r = ap3(gre_q, 0, [[1, 16], [0, 64]])
            g3i = ap3(gim_q, 0, [[1, 16], [0, 64]])
            l3r = ap3(lamb, 0, [[0, 16], [1, 64]])
            l3i = ap3(lamb, 64, [[0, 16], [1, 64]])
            V.tensor_tensor(dre, g3r, l3r, OP.subtract)
            V.tensor_tensor(dim, g3i, l3i, OP.subtract)
            V.tensor_tensor(inv, dre, dre, OP.mult)
            V.tensor_tensor(ta, dim, dim, OP.mult)
            V.tensor_tensor(ta, inv, ta, OP.add)
            V.reciprocal(inv, ta)
            for i, nm in enumerate(["k00", "k01", "k10", "k11"]):
                vr3 = ap3(v4b, i * 128, [[0, 16], [1, 64]])
                vi3 = ap3(v4b, i * 128 + 64, [[0, 16], [1, 64]])
                ar, ai = acc[nm]
                V.tensor_tensor(ta, vr3, dre, OP.mult)
                V.tensor_tensor(tb, vi3, dim, OP.mult)
                V.tensor_tensor(ta, ta, tb, OP.add)
                V.tensor_tensor(ta, ta, inv, OP.mult)
                V.reduce_sum(red[:, 0:16], ap3(ta, 0, [[64, 16], [1, 64]]), axis=AX.X)
                V.tensor_copy(ar[:, k1o:k1o + 16], red[:, 0:16])
                V.tensor_tensor(ta, vi3, dre, OP.mult)
                V.tensor_tensor(tb, vr3, dim, OP.mult)
                V.tensor_tensor(ta, ta, tb, OP.subtract)
                V.tensor_tensor(ta, ta, inv, OP.mult)
                V.reduce_sum(red[:, 0:16], ap3(ta, 0, [[64, 16], [1, 64]]), axis=AX.X)
                V.tensor_copy(ai[:, k1o:k1o + 16], red[:, 0:16])
        arre = sk["arre"]; arim = sk["arim"]; arimn = sk["arimn"]
        w1 = wkK[0:64, 0:64]; w2 = wkK[0:64, 64:128]
        w3 = wkK[0:64, 128:192]; w4 = wkK[0:64, 192:256]
        w5 = wkK[0:64, 256:320]
        a_re, a_im = acc["k11"]
        V.tensor_scalar(w1, a_re, 1.0, None, OP.add)
        V.tensor_tensor(w2, w1, w1, OP.mult)
        V.tensor_tensor(w3, a_im, a_im, OP.mult)
        V.tensor_tensor(w3, w2, w3, OP.add)
        V.reciprocal(w2, w3)
        br_, bi_ = acc["k01"]; cr_, ci_ = acc["k10"]
        V.tensor_tensor(w3, br_, cr_, OP.mult)
        V.tensor_tensor(w4, bi_, ci_, OP.mult)
        V.tensor_tensor(w3, w3, w4, OP.subtract)
        V.tensor_tensor(w4, br_, ci_, OP.mult)
        V.tensor_tensor(w5, bi_, cr_, OP.mult)
        V.tensor_tensor(w4, w4, w5, OP.add)
        V.tensor_tensor(w5, w3, w1, OP.mult)
        V.tensor_tensor(arre, w4, a_im, OP.mult)
        V.tensor_tensor(w5, w5, arre, OP.add)
        V.tensor_tensor(w5, w5, w2, OP.mult)
        V.tensor_tensor(arre, w4, w1, OP.mult)
        V.tensor_tensor(arim, w3, a_im, OP.mult)
        V.tensor_tensor(arre, arre, arim, OP.subtract)
        V.tensor_tensor(arre, arre, w2, OP.mult)
        k0r, k0i = acc["k00"]
        V.tensor_tensor(w3, k0r, w5, OP.subtract)
        V.tensor_tensor(w4, k0i, arre, OP.subtract)
        V.tensor_tensor(arre, ct["ccre"], w3, OP.mult)
        V.tensor_tensor(w5, ct["ccim"], w4, OP.mult)
        V.tensor_tensor(arre, arre, w5, OP.subtract)
        V.tensor_tensor(arim, ct["ccre"], w4, OP.mult)
        V.tensor_tensor(w5, ct["ccim"], w3, OP.mult)
        V.tensor_tensor(arim, arim, w5, OP.add)
        kar = V.tensor_scalar(arimn, arim, -1.0, None, OP.mult)
        pr.wait("t", "v_self", kar)
        def f(t):
            t.matmul(PS[0][0:64, 0:64], ct["W64re"], arre, start=True, stop=False)
            t.matmul(PS[0][0:64, 0:64], ct["W64im"], arimn, start=False, stop=True)
            t.matmul(PS[1][0:64, 0:64], ct["W64re"], arim, start=True, stop=False)
            return t.matmul(PS[1][0:64, 0:64], ct["W64im"], arre, start=False, stop=True)
        pr.seq("t", f, "kmm1")
        pr.wait("v", "kmm1")
        gre_s = wkK[0:64, 0:64]; gim_s = wkK[0:64, 64:128]
        hre = sk["hre"]; him = sk["him"]
        V.tensor_copy(gre_s, PS[0][0:64, 0:64])
        V.tensor_copy(gim_s, PS[1][0:64, 0:64])
        V.tensor_tensor(hre, gre_s, ct["Twre"], OP.mult)
        V.tensor_tensor(w5, gim_s, ct["Twim"], OP.mult)
        V.tensor_tensor(hre, hre, w5, OP.subtract)
        V.tensor_tensor(him, gre_s, ct["Twim"], OP.mult)
        V.tensor_tensor(w5, gim_s, ct["Twre"], OP.mult)
        kh = V.tensor_tensor(him, him, w5, OP.add)
        pr.wait("t", "v_self", kh)
        def f(t):
            t.transpose(PS[2][0:64, 0:64], sk["hre"], ct["identf"][0:64, 0:64])
            return t.transpose(PS[3][0:64, 0:64], sk["him"], ct["identf"][0:64, 0:64])
        pr.seq("t", f, "kt1")
        pr.wait("v", "kt1")
        V.tensor_copy(sk["hreT"], PS[2][0:64, 0:64])
        V.tensor_copy(sk["himT"], PS[3][0:64, 0:64])
        kt2 = V.tensor_scalar(sk["himT"], sk["himT"], -1.0, None, OP.mult)
        pr.wait("t", "v_self", kt2)
        def f(t):
            t.matmul(PS[4][0:64, 0:64], ct["W64bre"], sk["hreT"], start=True, stop=False)
            return t.matmul(PS[4][0:64, 0:64], ct["W64bim"], sk["himT"], start=False, stop=True)
        pr.seq("t", f, "kmm2")
        pr.wait("v", "kmm2")
        V.tensor_copy(sk["k2d"][:, :], PS[4][0:64, 0:64])
        kk2 = V.memset(wkK[0:1, 0:129], 0.0)
        pr.wait("g", "v_self", kk2)
        pr.dma("g", Kpad.ap()[0:1, 0:127], wkK[0:1, 0:127], "kpd")
        pr.dma("g", Kpad.ap()[0:1, 4223:4352], wkK[0:1, 0:129], "kpd")
        pr.dma("g", bass.AP(Kpad, 127, [[64, 64], [1, 64]]), sk["k2d"][:, :], "kpd")
        pr.wait("g", "kpd")
        if "Kout" in dbg_outs:
            pr.dma("g", dbg_outs["Kout"].ap(), Kpad.ap(), "kdbg")
            pr.dma("g", dbg_outs["Kacc"].ap(), sk["acc"][:, :], "kdbg")
            pr.dma("g", dbg_outs["Kar"].ap()[:, 0:64], arre, "kdbg")
            pr.dma("g", dbg_outs["Kar"].ap()[:, 64:128], arim, "kdbg")
            pr.dma("g", dbg_outs["Krows"].ap(), sk["rows"][0:1, :], "kdbg")
            pr.wait("g", "kdbg")

    setup()
    with nc.sbuf_tensor("skA", [1, 128], F32) as _p, \
         nc.sbuf_tensor("skB", [1, 128], F32) as _q, \
         nc.sbuf_tensor("skC", [64, 64], F32) as _vr, \
         nc.sbuf_tensor("skD", [64, 64], F32) as _vi, \
         nc.sbuf_tensor("skE", [1, 128], F32) as _ct, \
         nc.sbuf_tensor("skF", [64, 1], F32) as _bp, \
         nc.sbuf_tensor("skG", [1, 4], F32) as _dl, \
         nc.sbuf_tensor("skGl", [1, 192], F32) as _lsr, \
         nc.sbuf_tensor("skH", [1, 642], F32) as _rows, \
         nc.sbuf_tensor("skI", [64, 128], F32) as _lamb, \
         nc.sbuf_tensor("skJ", [64, 512], F32) as _v4b, \
         nc.sbuf_tensor("skK", [64, 512], F32) as _acc, \
         nc.sbuf_tensor("skL", [64, 192], F32) as _ar, \
         nc.sbuf_tensor("skM", [64, 128], F32) as _h, \
         nc.sbuf_tensor("skN", [64, 128], F32) as _hT, \
         nc.sbuf_tensor("skO", [64, 64], F32) as _k2d, \
         nc.sbuf_tensor("skW", [128, 5248], F32) as _wkK:
        sk = dict(p=_p, q=_q, vctre=_vr, vctim=_vi, ct=_ct, bp=_bp, dl=_dl, lsr=_lsr, wkK=_wkK,
                  rows=_rows, lamb=_lamb, v4b=_v4b, acc=_acc,
                  arre=_ar[:, 0:64], arim=_ar[:, 64:128], arimn=_ar[:, 128:192],
                  hre=_h[:, 0:64], him=_h[:, 64:128],
                  hreT=_hT[:, 0:64], himT=_hT[:, 64:128], k2d=_k2d)
        phaseK(sk)


    def phaseA(sa):
        V = pr.chained("v")
        Cc = pr.chained("c")
        xb = sa["xb"]; kvT = sa["kvT"]; wcol = sa["wcol"]; qtmp = sa["qtmp"]
        kst = sa["kst"]; kbart = sa["kbart"]
        ropekt = sa["ropekt"]; vnat = sa["vnat"]; ebt = sa["ebt"]
        haloT = sa["haloT"]; vhalo = sa["vhalo"]; rkhalo = sa["rkhalo"]
        Sst = sa["Sst"]; Zrun = sa["Zrun"]; Drun = sa["Drun"]
        crow = sa["crow"]; rkcols = sa["rkcols"]
        ebendF = sa["ebendF"]; wkF = sa["wkF"]
        cosb = sa["cosb"]; sinb = sa["sinb"]

        pr.dma("g", cosb[:, :], cosTt.ap(), "ctb")
        pr.dma("g", sinb[:, :], sinTt.ap(), "ctb")
        V.memset(Zrun[:, :], 0.0)
        V.memset(Drun[:, :], 1.0)
        V.memset(Sst[:, :], 0.0)
        pr.wait("v", "ctb")

        def proj_col(wsrc, coff, b, n_tok, xsrc, nm):
            pr.wait("g", nm)
            pr.dma("g", ap3(wcol[:, :], b * 4096, [[128, 32], [1, 128]]),
                   bass.AP(wsrc, coff, [[wsrc.shape[1], 128], [128 * wsrc.shape[1], 32], [1, 128]]),
                   "wcA")
            pr.wait("t", "wcA")
            def f(t, b=b, n_tok=n_tok):
                last = None
                for k in range(32):
                    last = t.matmul(PS[b][:, 0:n_tok],
                                    wcol[:, b * 4096 + k * 128:b * 4096 + (k + 1) * 128],
                                    xsrc[:, k * n_tok:(k + 1) * n_tok],
                                    start=(k == 0), stop=(k == 31))
                return last
            return pr.seq("t", f, nm)

        for H in range(2):
            t0 = H * 512
            if H == 1:
                pr.wait("g", "qmm")
            pr.dma("g", ap3(xb[:, :], 0, [[512, 32], [1, 512]]),
                   bass.AP(xT, t0, [[NT, 128], [128 * NT, 32], [1, 512]]), "xbA")
            if H == 0:
                pr.dma("g", ap3(sa["xbh"][:, :], 0, [[64, 32], [1, 64]]),
                       bass.AP(xTh, 0, [[64, 128], [128 * 64, 32], [1, 64]]), "xbA")
            pr.wait("t", "xbA")
            for m in range(16):
                wsrc = Wk if m < 8 else Wv
                b = m % 2
                if m >= 2:
                    pr.wait("t", "v_self")
                proj_col(wsrc, (m % 8) * 128, b, 512, xb, "kvmm")
                pr.wait("v", "kvmm")
                V.tensor_copy(kvT[:, m * 512:(m + 1) * 512], PS[b][:, 0:512])
            if H == 0:
                for m in range(16):
                    wsrc = Wk if m < 8 else Wv
                    b = m % 2
                    if m >= 2:
                        pr.wait("t", "v_self")
                    pr.wait("g", "kvmm")
                    pr.dma("g", ap3(wcol[:, :], b * 4096, [[128, 32], [1, 128]]),
                           bass.AP(wsrc, (m % 8) * 128, [[1024, 128], [128 * 1024, 32], [1, 128]]), "wcA")
                    pr.wait("t", "wcA")
                    def f(t, b=b):
                        last = None
                        for k in range(32):
                            last = t.matmul(PS[2 + b][:, 0:64],
                                            wcol[:, b * 4096 + k * 128:b * 4096 + (k + 1) * 128],
                                            sa["xbh"][:, k * 64:(k + 1) * 64],
                                            start=(k == 0), stop=(k == 31))
                        return last
                    pr.seq("t", f, "kvmm")
                    pr.wait("v", "kvmm")
                    V.tensor_copy(haloT[:, m * 64:(m + 1) * 64], PS[2 + b][:, 0:64])
                for h in range(KVH):
                    pr.wait("t", "v_self")
                    pr.seq("t", lambda t, h=h: t.transpose(PSB[0:64, 0:128],
                                                           haloT[:, (8 + h) * 64:(9 + h) * 64],
                                                           ct["ident"]), "hvt")
                    pr.wait("v", "hvt")
                    V.tensor_copy(vhalo[0:64, h * 128:(h + 1) * 128], PSB[0:64, 0:128])
                    kTh = haloT[:, h * 64:(h + 1) * 64]
                    pr.wait("t", "v_self")
                    pr.seq("t", lambda t, kTh=kTh: t.matmul(PS[5][:, 0:64], ct["perm"], kTh,
                                                            start=True, stop=True), "hkp")
                    pr.wait("v", "hkp")
                    V.tensor_copy(wkD[:, 0:64], PS[5][:, 0:64])
                    V.tensor_tensor(wkD[:, 64:128], kTh, cosb[:, 0:64], OP.mult)
                    V.tensor_tensor(wkD[:, 0:64], wkD[:, 0:64], sinb[:, 0:64], OP.mult)
                    V.tensor_tensor(rkhalo[:, h * 64:(h + 1) * 64],
                                    wkD[:, 64:128], wkD[:, 0:64], OP.add)
            for m in range(32):
                b = m % 2
                if m >= 2:
                    pr.wait("t", "v_self")
                proj_col(Wq, m * 128, b, 512, xb, "qmm")
                pr.wait("v", "qmm")
                if H or m >= 2:
                    pr.wait("v", "qst%d" % b)
                qev = V.tensor_copy(qtmp[:, b * 512:(b + 1) * 512], PS[b][:, 0:512])
                pr.wait("s", "v_self", qev)
                pr.dma("s", bass.AP(qdr, m * 128 * NT + t0, [[NT, 128], [1, 512]]),
                       qtmp[:, b * 512:(b + 1) * 512], "qst%d" % b)
            for ci4 in range(4):
                ci = H * 4 + ci4
                for h in range(KVH):
                    kTh = kvT[:, h * 512 + ci4 * 128: h * 512 + (ci4 + 1) * 128]
                    vTh = kvT[:, (8 + h) * 512 + ci4 * 128: (8 + h) * 512 + (ci4 + 1) * 128]
                    pr.wait("t", "v_self")
                    pr.seq("t", lambda t, vTh=vTh: t.transpose(PSB[:, 0:128], vTh, ct["ident"]), "vtr")
                    pr.wait("v", "vtr")
                    V.tensor_copy(vnat[:, ci * 1024 + h * 128: ci * 1024 + (h + 1) * 128], PSB[:, 0:128])
                    lg = wkF[:, 0:128]; Lc = wkF[:, 128:256]; emb = wkF[:, 256:384]
                    pr.wait("c", "v_self")
                    e1c = Cc.activation(lg, kTh, AF.Exp, scale=cpool[:, CP_NEG1:CP_NEG1 + 1])
                    pr.wait("v", "c_self", e1c)
                    lnc = V.tensor_scalar(lg, lg, 1.0, None, OP.add)
                    pr.wait("c", "v_self", lnc)
                    lnd = Cc.activation(lg, lg, AF.Ln)
                    pr.wait("v", "c_self", lnd)
                    scn = V.tensor_tensor_scan(Lc, lg, lg, 0.0, OP.add, OP.bypass)
                    pr.wait("c", "v_self", scn)
                    ebc = Cc.activation(ebt[:, (ci * 8 + h) * 128:(ci * 8 + h + 1) * 128], Lc,
                                        AF.Exp, scale=cpool[:, CP_G16:CP_G16 + 1])
                    emc = Cc.activation(emb, Lc, AF.Exp, scale=cpool[:, CP_P16:CP_P16 + 1])
                    exk = wkD[:, 256:384]
                    pr.wait("c", "kcs")
                    if ci4 % 2 == 0 and h == 0 and (H or ci4):
                        pr.wait("v", "rkb")
                    xkc = Cc.activation(exk, kTh, AF.Exp)
                    pr.wait("t", "c_self", xkc)
                    pr.seq("t", lambda t, exk=exk: t.matmul(PS[5][0:1, 0:128], ct["ones_col"], exk,
                                                            start=True, stop=True), "kcs")
                    pr.wait("v", "kcs")
                    pr.wait("v", "c_self", emc)
                    V.tensor_copy(crow[0:1, ((ci4 % 2) * 8 + h) * 128:((ci4 % 2) * 8 + h + 1) * 128],
                                  PS[5][0:1, 0:128])
                    V.tensor_tensor(kst[:, (ci * 8 + h) * 128:(ci * 8 + h + 1) * 128], exk, emb, OP.mult)
                    pr.wait("t", "v_self")
                    pr.seq("t", lambda t, kTh=kTh: t.matmul(PS[5][:, 0:128], ct["perm"], kTh,
                                                            start=True, stop=True), "rkp")
                    pr.wait("v", "rkp")
                    V.tensor_copy(wkD[:, 0:128], PS[5][:, 0:128])
                    V.tensor_tensor(wkD[:, 128:256], kTh, cosb[:, 64 + ci * 128:64 + (ci + 1) * 128], OP.mult)
                    V.tensor_tensor(wkD[:, 0:128], wkD[:, 0:128], sinb[:, 64 + ci * 128:64 + (ci + 1) * 128], OP.mult)
                    V.tensor_tensor(ropekt[:, (ci * 8 + h) * 128:(ci * 8 + h + 1) * 128],
                                    wkD[:, 128:256], wkD[:, 0:128], OP.add)
                if ci4 % 2 == 1:
                    qtr = H * 2 + ci4 // 2
                    rr = pr.cnt["v_self"]
                    pr.wait("g", "v_self", rr)
                    pr.dma("g", rowsd.ap()[0:1, 0:2048], crow[0:1, 0:2048], "rkb")
                    pr.wait("g", "rkb")
                    pr.dma("g", ap3(sa["rkraw"][:, 0:16], 0, [[1, 16], [1, 1]]),
                           bass.AP(rowsd, 0, [[1, 128], [128, 16], [1, 1]]), "rkc",
                           allow_slow_non_contiguous=True)
                    pr.wait("v", "rkc")
                    V.reciprocal(rkcols[:, qtr * 16:(qtr + 1) * 16], sa["rkraw"][:, 0:16])
            V.tensor_copy(ebendF[:, H * 32:H * 32 + 32],
                          ap3(ebt[:, :], H * 4096 + 127, [[128, 32], [1, 1]]))
            for ci4 in range(4):
                ci = H * 4 + ci4
                for h in range(KVH):
                    g = ci * 8 + h
                    kb = wkD[:, 256:384]
                    pr.wait("v", "kbt")
                    V.tensor_scalar(kb, kst[:, g * 128:(g + 1) * 128],
                                    ebendF[:, g:g + 1], None, OP.mult)
                    kbv = pr.cnt["v_self"]
                    pr.wait("t", "v_self", kbv)
                    pr.seq("t", lambda t, kb=kb: t.transpose(PSB[:, 0:128], kb, ct["ident"]), "kbt")
                    pr.wait("v", "kbt")
                    V.tensor_scalar(kbart[:, ci * 1024 + h * 128: ci * 1024 + (h + 1) * 128],
                                    PSB[:, 0:128], rkcols[:, g:g + 1], None, OP.mult)
                    zn = pr.cnt["v_self"]
                    pr.wait("t", "v_self", zn)
                    def f(t, ci=ci, h=h):
                        return t.matmul(PS[6][:, 0:128],
                                        kbart[:, ci * 1024 + h * 128: ci * 1024 + (h + 1) * 128],
                                        vnat[:, ci * 1024 + h * 128: ci * 1024 + (h + 1) * 128],
                                        start=True, stop=True)
                    pr.seq("t", f, "zmm")
                    pr.wait("v", "zmm")
                    V.scalar_tensor_tensor(Zrun[:, h * 128:(h + 1) * 128],
                                           Zrun[:, h * 128:(h + 1) * 128],
                                           ebendF[:, g:g + 1],
                                           PS[6][:, 0:128], OP.mult, OP.add)
                    V.tensor_scalar(Drun[:, h:h + 1], Drun[:, h:h + 1],
                                    ebendF[:, g:g + 1], None, OP.mult)
        wr = pr.cnt["v_self"]
        pr.wait("g", "v_self", wr)
        for h in range(KVH):
            pr.dma("g", bass.AP(ag_s_in, h * 130 * 128, [[128, 128], [1, 128]]),
                   Zrun[:, h * 128:(h + 1) * 128], "agw")
            pr.dma("g", bass.AP(ag_s_in, h * 130 * 128 + 128 * 128, [[1, 128], [1, 1]]),
                   Drun[:, h:h + 1], "agw", allow_slow_non_contiguous=True)
        pr.wait("g", "agw")
        def f(g):
            return g.collective_compute(
                "AllGather", OP.bypass, replica_groups=[list(range(NC))],
                ins=[ag_s_in.ap().opt()], outs=[ag_s_out.ap().opt()])
        pr.seq("g", f, "agd")
        pr.wait("s", "agd")
        for j in range(NC):
            for h in range(KVH):
                zf = pr.dma("s", wkF[:, 0:128],
                            bass.AP(ag_s_out, (j * KVH + h) * 130 * 128, [[128, 128], [1, 128]]), "sfz")
                df = pr.dma("s", wkF[:, 128:129],
                            bass.AP(ag_s_out, (j * KVH + h) * 130 * 128 + 128 * 128, [[1, 128], [1, 1]]),
                            "sfd", allow_slow_non_contiguous=True)
                pr.wait("v", "sfz", zf)
                pr.wait("v", "sfd", df)
                V.tensor_scalar(wkF[:, 129:130], wkF[:, 128:129], scanm_t[:, j:j + 1], None, OP.mult)
                V.tensor_scalar(wkF[:, 129:130], wkF[:, 129:130], scanm_t[:, 8 + j:9 + j], None, OP.add)
                V.tensor_scalar(wkF[:, 0:128], wkF[:, 0:128], scanm_t[:, j:j + 1], None, OP.mult)
                fs = V.scalar_tensor_tensor(Sst[:, h * 128:(h + 1) * 128],
                                            Sst[:, h * 128:(h + 1) * 128],
                                            wkF[:, 129:130], wkF[:, 0:128], OP.mult, OP.add)
                pr.wait("s", "v_self", fs)


    def phaseA2(sa):
        V = pr.chained("v")
        Cc = pr.chained("c")
        kst = sa["kst"]; kbart = sa["kbart"]; ropekt = sa["ropekt"]
        vnat = sa["vnat"]; ebt = sa["ebt"]; vhalo = sa["vhalo"]; rkhalo = sa["rkhalo"]
        Sst = sa["Sst"]; rkcols = sa["rkcols"]; ebendF = sa["ebendF"]
        wkF = sa["wkF"]; cosb = sa["cosb"]; sinb = sa["sinb"]
        qch = sa["qch"]; oT = sa["oT"]; Scast = sa["Scast"]
        pr.wait("g", "qst0")
        pr.wait("g", "qst1")
        for ci in range(NCH):
            qs_ = (ci % 2) * 4096
            if ci >= 2:
                pr.wait("g", "a2rp")
                pr.wait("g", "c_self")
            pr.dma("g", ap3(qch[:, :], qs_, [[128, 32], [1, 128]]),
                   bass.AP(qdr, ci * 128, [[NT, 128], [128 * NT, 32], [1, 128]]), "a2q%d" % (ci % 2))
            pr.wait("v", "a2o")
            for h in range(KVH):
                g = ci * 8 + h
                pr.wait("v", "a2ol")
                sc_ = V.tensor_copy(Scast[:, :], Sst[:, h * 128:(h + 1) * 128])
                pr.wait("t", "v_self", sc_)
                def f(t, ci=ci, h=h):
                    return t.matmul(PS[0][:, 0:128],
                                    kbart[:, ci * 1024 + h * 128:ci * 1024 + (h + 1) * 128],
                                    vnat[:, ci * 1024 + h * 128:ci * 1024 + (h + 1) * 128],
                                    start=True, stop=True)
                pr.seq("t", f, "a2z")
                pr.wait("v", "a2z")
                V.scalar_tensor_tensor(Sst[:, h * 128:(h + 1) * 128],
                                       Sst[:, h * 128:(h + 1) * 128],
                                       ebendF[:, g:g + 1], PS[0][:, 0:128], OP.mult, OP.add)
                for qi in range(GRP):
                    hq = h * GRP + qi
                    qTh = qch[:, qs_ + hq * 128:qs_ + (hq + 1) * 128]
                    exq = wkD[:, 0:128]
                    rqt = wkD[:, 128:256]
                    qst_ = wkD[:, 256:384]
                    Am = wkD[:, 384:512]
                    e1 = wkD[:, 512:640]
                    e2 = wkD[:, 640:768]
                    rqb = wkD[:, 768:896]
                    pr.wait("c", "a2q%d" % (ci % 2))
                    pr.wait("c", "v_self")
                    xqc = Cc.activation(exq, qTh, AF.Exp)
                    pr.wait("t", "c_self", xqc)
                    pr.wait("t", "v_self")
                    pr.seq("t", lambda t, exq=exq: t.matmul(PS[4][0:1, 0:128], ct["ones_col"], exq,
                                                            start=True, stop=True), "a2cq")
                    pr.wait("v", "a2cq")
                    V.reciprocal(wkF[0:1, 512:640], PS[4][0:1, 0:128])
                    rqc = V.tensor_copy(rqb[0:1, 0:128], wkF[0:1, 512:640])
                    pr.wait("t", "v_self", rqc)
                    pr.seq("t", lambda t, rqb=rqb: t.matmul(PS[5][:, 0:128], ct["ones_row"], rqb[0:1, 0:128],
                                                            start=True, stop=True), "a2xq")
                    pr.seq("t", lambda t, qTh=qTh: t.matmul(PS[2][:, 0:128], ct["perm"], qTh,
                                                            start=True, stop=True), "a2rp")
                    pr.wait("v", "a2rp")
                    V.tensor_copy(rqt, PS[2][:, 0:128])
                    V.tensor_tensor(qst_, qTh, cosb[:, 64 + ci * 128:64 + (ci + 1) * 128], OP.mult)
                    V.tensor_tensor(rqt, rqt, sinb[:, 64 + ci * 128:64 + (ci + 1) * 128], OP.mult)
                    V.tensor_tensor(rqt, qst_, rqt, OP.add)
                    qsv = V.tensor_tensor(qst_, exq, ebt[:, g * 128:(g + 1) * 128], OP.mult)
                    pr.wait("t", "v_self", qsv)
                    pr.seq("t", lambda t, g=g, qst_=qst_: t.matmul(PS[0][:, 0:128],
                                                                   kst[:, g * 128:(g + 1) * 128], qst_,
                                                                   start=True, stop=True), "a2at")
                    pr.wait("v", "a2at")
                    amv = V.scalar_tensor_tensor(Am, PS[0][:, 0:128], rkcols[:, g:g + 1],
                                                 ct["triu_ji"], OP.mult, OP.mult)
                    pr.wait("t", "v_self", amv)
                    def f(t, ci=ci, h=h, Am=Am, qst_=qst_):
                        t.matmul(PS[1][:, 0:128],
                                 vnat[:, ci * 1024 + h * 128:ci * 1024 + (h + 1) * 128], Am,
                                 start=True, stop=False)
                        return t.matmul(PS[1][:, 0:128], Scast[:, :], qst_, start=False, stop=True)
                    pr.seq("t", f, "a2ol")
                    if ci == 0:
                        pk = rkhalo[:, h * 64:(h + 1) * 64]
                        pv_ = vhalo[0:64, h * 128:(h + 1) * 128]
                        msk1 = m1c0_t
                        W1 = 64
                    else:
                        pk = ropekt[:, ((ci - 1) * 8 + h) * 128:((ci - 1) * 8 + h + 1) * 128]
                        pv_ = vnat[:, (ci - 1) * 1024 + h * 128:(ci - 1) * 1024 + (h + 1) * 128]
                        msk1 = ct["mask1e"]
                        W1 = 128
                    def f(t, pk=pk, rqt=rqt, W1=W1):
                        return t.matmul(PS[2][0:W1, 0:128], pk, rqt, start=True, stop=True)
                    pr.seq("t", f, "a2s1")
                    def f(t, g=g, rqt=rqt):
                        return t.matmul(PS[3][:, 0:128],
                                        ropekt[:, g * 128:(g + 1) * 128], rqt, start=True, stop=True)
                    pr.seq("t", f, "a2s2")
                    pr.wait("v", "a2s1")
                    pr.wait("v", "a2s2")
                    m1v = V.tensor_tensor(wkF[0:W1, 0:128], PS[2][0:W1, 0:128],
                                          msk1[0:W1, :] if W1 == 64 else msk1, OP.add)
                    m2v = V.tensor_tensor(wkF[:, 128:256], PS[3][:, 0:128], ct["mask2"], OP.add)
                    pr.wait("c", "v_self", m2v)
                    ex1 = Cc.activation(e1[0:W1, 0:128], wkF[0:W1, 0:128], AF.Exp,
                                        scale=cpool[0:W1, CP_ISQD:CP_ISQD + 1])
                    ex2 = Cc.activation(e2, wkF[:, 128:256], AF.Exp,
                                        scale=cpool[:, CP_ISQD:CP_ISQD + 1])
                    pr.wait("t", "c_self", ex2)
                    pr.wait("t", "v_self")
                    def f(t, e1=e1, e2=e2, W1=W1):
                        t.matmul(PS[4][0:1, 0:128], ct["ones_col"][0:W1, :], e1[0:W1, 0:128],
                                 start=True, stop=False)
                        return t.matmul(PS[4][0:1, 0:128], ct["ones_col"], e2, start=False, stop=True)
                    pr.seq("t", f, "a2cs")
                    def f(t, e1=e1, e2=e2, ci=ci, h=h, pv_=pv_, W1=W1):
                        t.matmul(PS[6][:, 0:128], pv_, e1[0:W1, 0:128], start=True, stop=False)
                        return t.matmul(PS[6][:, 0:128],
                                        vnat[:, ci * 1024 + h * 128:ci * 1024 + (h + 1) * 128], e2,
                                        start=False, stop=True)
                    pr.seq("t", f, "a2pv")
                    pr.wait("v", "a2cs")
                    V.reciprocal(wkF[0:1, 512:640], PS[4][0:1, 0:128])
                    rsc = V.tensor_copy(rqb[0:1, 0:128], wkF[0:1, 512:640])
                    pr.wait("t", "v_self", rsc)
                    pr.seq("t", lambda t, rqb=rqb: t.matmul(PS[4][:, 0:128], ct["ones_row"],
                                                            rqb[0:1, 0:128], start=True, stop=True), "a2xs")
                    pr.wait("v", "a2xs")
                    pr.wait("v", "a2xq")
                    pr.wait("v", "a2pv")
                    pr.wait("v", "a2ol")
                    V.tensor_copy(wkF[:, 128:256], PS[5][:, 0:128])
                    V.tensor_copy(wkF[:, 256:384], PS[4][:, 0:128])
                    V.tensor_tensor(wkF[:, 384:512], PS[6][:, 0:128], wkF[:, 256:384], OP.mult)
                    V.tensor_tensor(wkF[:, 128:256], PS[1][:, 0:128], wkF[:, 128:256], OP.mult)
                    V.tensor_tensor(wkF[:, 128:256], wkF[:, 128:256], wkF[:, 384:512], OP.add)
                    V.tensor_scalar(oT[:, hq * 128:(hq + 1) * 128], wkF[:, 128:256],
                                    cpool[:, CP_HALF:CP_HALF + 1], None, OP.mult)
            ov = pr.cnt["v_self"]
            pr.wait("s", "v_self", ov)
            pr.dma("s", bass.AP(odr, ci * 128, [[NT, 128], [128 * NT, 32], [1, 128]]),
                   ap3(oT[:, :], 0, [[128, 32], [1, 128]]), "a2o")
        pr.wait("s", "a2o")

    def phaseWo(sw):
        V = pr.chained("v")
        oTf = sw["oTf"]; wcol = sw["wcol"]; aev = sw["aev"]
        pr.dma("g", ap3(oTf[:, :], 0, [[1024, 32], [1, 1024]]),
               bass.AP(odr, 0, [[NT, 128], [128 * NT, 32], [1, 1024]]), "woo")
        pr.wait("t", "woo")
        for m in range(32):
            b = m % 2
            if m >= 2:
                pr.wait("g", "womm")
                pr.wait("t", "v_self")
            pr.dma("g", ap3(wcol[:, :], b * 4096, [[128, 32], [1, 128]]),
                   bass.AP(Wo, m * 128, [[D, 128], [128 * D, 32], [1, 128]]), "wcW")
            pr.wait("t", "wcW")
            def f(t, b=b):
                last = None
                for nn in range(2):
                    for k in range(32):
                        last = t.matmul(PS[2 * b + nn][:, 0:512],
                                        wcol[:, b * 4096 + k * 128:b * 4096 + (k + 1) * 128],
                                        oTf[:, k * 1024 + nn * 512:k * 1024 + (nn + 1) * 512],
                                        start=(k == 0), stop=(k == 31))
                return last
            pr.seq("t", f, "womm")
            pr.wait("v", "womm")
            if m >= 2:
                pr.wait("v", "wost%d" % b)
            V.tensor_copy(aev[:, (b * 2) * 512:(b * 2 + 1) * 512], PS[2 * b][:, 0:512])
            wov = V.tensor_copy(aev[:, (b * 2 + 1) * 512:(b * 2 + 2) * 512], PS[2 * b + 1][:, 0:512])
            pr.wait("s", "v_self", wov)
            pr.dma("s", bass.AP(attnTd, m * 128 * NT, [[NT, 128], [1, 512]]),
                   aev[:, (b * 2) * 512:(b * 2 + 1) * 512], "wost%d" % b)
            pr.dma("s", bass.AP(attnTd, m * 128 * NT + 512, [[NT, 128], [1, 512]]),
                   aev[:, (b * 2 + 1) * 512:(b * 2 + 2) * 512], "wost%d" % b)
        pr.wait("s", "wost0")
        pr.wait("s", "wost1")

    def emit_ln(resA, V, Cc, tag):
        pr.wait("c", "v_self")
        for k in range(32):
            if k >= 2:
                pr.wait("c", tag + "st")
            xc = Cc.activation(wkD[:, (k % 2) * 512:(k % 2 + 1) * 512],
                               resA[:, k * 512:(k + 1) * 512], AF.Copy)
            sq = Cc.activation(wkD[:, 1024 + (k % 2) * 512:1024 + (k % 2 + 1) * 512],
                               resA[:, k * 512:(k + 1) * 512], AF.Square)
            pr.wait("t", "c_self", sq)
            def f(t, k=k):
                t.matmul(PS[5][0:1, 0:512], ct["ones_col"],
                         wkD[:, (k % 2) * 512:(k % 2 + 1) * 512], start=(k == 0), stop=(k == 31))
                return t.matmul(PS[6][0:1, 0:512], ct["ones_col"],
                                wkD[:, 1024 + (k % 2) * 512:1024 + (k % 2 + 1) * 512],
                                start=(k == 0), stop=(k == 31))
            pr.seq("t", f, tag + "st")
        pr.wait("v", tag + "st")
        s0 = global_stat[0][0:1, :]
        cp0 = cpool[0:1, :]
        V.tensor_scalar(s0[0:1, 0:512], PS[5][0:1, 0:512], cp0[0:1, CP_INVD:CP_INVD + 1], None, OP.mult)
        V.tensor_scalar(s0[0:1, 512:1024], PS[6][0:1, 0:512], cp0[0:1, CP_INVD:CP_INVD + 1], None, OP.mult)
        V.tensor_tensor(s0[0:1, 1536:2048], s0[0:1, 0:512], s0[0:1, 0:512], OP.mult)
        V.tensor_tensor(s0[0:1, 1024:1536], s0[0:1, 512:1024], s0[0:1, 1536:2048], OP.subtract)
        V.tensor_scalar(s0[0:1, 1024:1536], s0[0:1, 1024:1536], cp0[0:1, CP_EPS:CP_EPS + 1], None, OP.add)
        rc = V.reciprocal(s0[0:1, 1536:2048], s0[0:1, 1024:1536])
        pr.wait("c", "v_self", rc)
        sqc = Cc.activation(s0[0:1, 2048:2560], s0[0:1, 1536:2048], AF.Sqrt)
        pr.wait("v", "c_self", sqc)
        V.tensor_tensor(s0[0:1, 1536:2048], s0[0:1, 2048:2560], s0[0:1, 2048:2560], OP.mult)
        V.tensor_tensor(s0[0:1, 1536:2048], s0[0:1, 1536:2048], s0[0:1, 1024:1536], OP.mult)
        V.tensor_scalar(s0[0:1, 1536:2048], s0[0:1, 1536:2048], cp0[0:1, CP_NHALF:CP_NHALF + 1],
                        cp0[0:1, CP_1P5:CP_1P5 + 1], OP.mult, OP.add)
        V.tensor_tensor(s0[0:1, 2048:2560], s0[0:1, 1536:2048], s0[0:1, 2048:2560], OP.mult)
        bv = V.tensor_tensor(s0[0:1, 2560:3072], s0[0:1, 2048:2560], s0[0:1, 0:512], OP.mult)
        pr.wait("g", "v_self", bv)
        pr.dma("g", rowsd.ap()[2:3, 0:512], s0[0:1, 2048:2560], tag + "b1")
        pr.dma("g", rowsd.ap()[3:4, 0:512], s0[0:1, 2560:3072], tag + "b1")
        pr.wait("g", tag + "b1")
        statB = global_statB[0]
        pr.dma("g", statB[:, 0:512], bass.AP(rowsd, 2 * 8192, [[0, 128], [1, 512]]), tag + "b2")
        pr.dma("g", statB[:, 512:1024], bass.AP(rowsd, 3 * 8192, [[0, 128], [1, 512]]), tag + "b2")
        pr.wait("v", tag + "b2")
        for k in range(32):
            V.tensor_tensor(resA[:, k * 512:(k + 1) * 512], resA[:, k * 512:(k + 1) * 512],
                            statB[:, 0:512], OP.mult)
            V.tensor_tensor(resA[:, k * 512:(k + 1) * 512], resA[:, k * 512:(k + 1) * 512],
                            statB[:, 512:1024], OP.subtract)

    def phaseA3(sa3):
        V = pr.chained("v")
        Cc = pr.chained("c")
        resA = sa3["resA"]; xroll = sa3["xroll"]
        for H in range(2):
            t0 = H * 512
            if H == 1:
                pr.wait("s", "a3o")
            pr.dma("s", ap3(resA[:, :], 0, [[512, 32], [1, 512]]),
                   bass.AP(attnTd, t0, [[NT, 128], [128 * NT, 32], [1, 512]]), "a3a")
            pr.wait("v", "a3a")
            for k in range(32):
                b = k % 2
                if k >= 2:
                    pr.wait("s", "v_self")
                pr.dma("s", xroll[:, b * 512:(b + 1) * 512],
                       bass.AP(xT, k * 128 * NT + t0, [[NT, 128], [1, 512]]), "a3x%d" % b)
                pr.wait("v", "a3x%d" % b)
                V.tensor_tensor(resA[:, k * 512:(k + 1) * 512], resA[:, k * 512:(k + 1) * 512],
                                xroll[:, b * 512:(b + 1) * 512], OP.add)
            if not debug.get("noln"):
                emit_ln(resA, V, Cc, "l1h%d" % H)
            lv = pr.cnt["v_self"]
            pr.wait("s", "v_self", lv)
            pr.dma("s", bass.AP(x1d, t0, [[NT, 128], [128 * NT, 32], [1, 512]]),
                   ap3(resA[:, :], 0, [[512, 32], [1, 512]]), "a3o")
        pr.wait("s", "a3o")
        if "x1out" in dbg_outs:
            pr.wait("g", "a3o")
            pr.dma("g", dbg_outs["x1out"].ap(), x1d.ap(), "a3d")
            if "adiag2" in dbg_outs:
                pr.dma("g", dbg_outs["adiag2"].ap()[0:1, :], global_stat[0][0:1, :], "a3d")
                pr.dma("g", dbg_outs["adiag2"].ap()[1:3, 0:1024],
                       global_statB[0][0:2, 0:1024], "a3d")
            pr.wait("g", "a3d")
        if "qout" in dbg_outs:
            pr.dma("g", dbg_outs["qout"].ap(), qdr.ap(), "a3d")
            pr.dma("g", dbg_outs["oout"].ap(), odr.ap(), "a3d")
            pr.dma("g", dbg_outs["attnout"].ap(), attnTd.ap(), "a3d")
            pr.wait("g", "a3d")

    def phaseB(sb_):
        V = pr.chained("v")
        Cc = pr.chained("c")
        resB = sb_["resB"]; xbB = sb_["xbB"]; AfB = sb_["AfB"]; BwB = sb_["BwB"]
        rtT = sb_["rtT"]; lg8 = sb_["lg8"]; lgT = sb_["lgT"]; eqT = sb_["eqT"]
        mxc = sb_["mxc"]; rw8 = sb_["rw8"]; rwXs = sb_["rwXs"]; whB = sb_["whB"]
        pr.dma("g", ap3(AfB[:, :], 0, [[128, 32], [1, 128]]),
               bass.AP(Aflat, 0, [[128, 128], [128 * 128, 32], [1, 128]]), "bwl")
        pr.dma("g", ap3(BwB[:, :], 0, [[128, 32], [1, 128]]),
               bass.AP(BwT, 0, [[D, 128], [128, 32], [1, 128]]), "bwl")
        pr.dma("s", ap3(rtT[:, :], 0, [[8, 32], [1, 8]]),
               bass.AP(routeT, 0, [[8, 128], [8 * 128, 32], [1, 8]]), "brt")
        pr.wait("t", "bwl")
        pr.wait("t", "brt")
        for H in range(2):
            t0 = H * 512
            if H == 1:
                pr.wait("s", "bxo")
                pr.wait("g", "bmmh")
                pr.wait("t", "v_self")
            pr.dma("s", ap3(resB[:, :], 0, [[512, 32], [1, 512]]),
                   bass.AP(x1d, t0, [[NT, 128], [128 * NT, 32], [1, 512]]), "bra")
            pr.dma("g", ap3(xbB[:, :], 0, [[512, 32], [1, 512]]),
                   bass.AP(x1d, t0, [[NT, 128], [128 * NT, 32], [1, 512]]), "bxb")
            pr.wait("t", "bra")
            pr.wait("t", "bxb")
            def f(t):
                last = None
                for k in range(32):
                    last = t.matmul(PS[0][0:8, 0:512], rtT[:, k * 8:(k + 1) * 8],
                                    resB[:, k * 512:(k + 1) * 512],
                                    start=(k == 0), stop=(k == 31))
                return last
            pr.seq("t", f, "blg")
            def f(t):
                last = None
                for k in range(32):
                    last = t.matmul(PS[3][:, 0:512], AfB[:, k * 128:(k + 1) * 128],
                                    xbB[:, k * 512:(k + 1) * 512],
                                    start=(k == 0), stop=(k == 31))
                return last
            pr.seq("t", f, "bmmh")
            pr.wait("v", "blg")
            lv8 = V.tensor_copy(lg8[0:8, 0:512], PS[0][0:8, 0:512])
            pr.wait("t", "v_self", lv8)
            def f(t):
                last = None
                for j in range(4):
                    last = t.transpose(PS[1][:, j * 8:(j + 1) * 8],
                                       lg8[0:8, j * 128:(j + 1) * 128], ct["identf"][0:8, 0:8])
                return last
            pr.seq("t", f, "btr1")
            pr.wait("v", "btr1")
            V.tensor_copy(lgT[:, 0:32], PS[1][:, 0:32])
            V.reduce_max(mxc[:, 0:4], ap3(lgT[:, 0:32], 0, [[8, 4], [1, 8]]), axis=AX.X)
            for j in range(4):
                V.tensor_scalar(eqT[:, j * 8:(j + 1) * 8], lgT[:, j * 8:(j + 1) * 8],
                                mxc[:, j:j + 1], None, OP.is_equal)
                V.tensor_scalar(eqT[:, j * 8:(j + 1) * 8], eqT[:, j * 8:(j + 1) * 8],
                                mxc[:, j:j + 1], None, OP.mult)
            eqv = pr.cnt["v_self"]
            pr.wait("t", "v_self", eqv)
            def f(t):
                last = None
                for j in range(4):
                    last = t.transpose(PS[2][0:8, j * 128:(j + 1) * 128],
                                       eqT[:, j * 8:(j + 1) * 8], ct["identf"])
                return last
            pr.seq("t", f, "btr2")
            pr.wait("v", "btr2")
            rwv = V.tensor_copy(rw8[0:8, 0:512], PS[2][0:8, 0:512])
            if "rwout" in dbg_outs:
                pr.wait("g", "v_self", rwv)
                pr.dma("g", bass.AP(dbg_outs["rwout"], t0, [[NT, 8], [1, 512]]),
                       rw8[0:8, 0:512], "bdg")
            pr.wait("t", "v_self", rwv)
            pr.seq("t", lambda t: t.matmul(PS[4][:, 0:512], ct["blocksel"], rw8[0:8, 0:512],
                                           start=True, stop=True), "brwx")
            pr.wait("v", "brwx")
            V.tensor_copy(rwXs[:, 0:512], PS[4][:, 0:512])
            pr.wait("v", "bmmh")
            whv = V.tensor_tensor(whB[:, 0:512], PS[3][:, 0:512], rwXs[:, 0:512], OP.mult)
            pr.wait("t", "v_self", whv)
            addc = {}
            for m in range(32):
                b = m % 2
                if m >= 2:
                    pr.wait("t", "v_self", addc[m - 2])
                def f(t, m=m, b=b):
                    return t.matmul(PS[b][:, 0:512], BwB[:, m * 128:(m + 1) * 128],
                                    whB[:, 0:512], start=True, stop=True)
                pr.seq("t", f, "bout")
                pr.wait("v", "bout")
                addc[m] = V.tensor_tensor(resB[:, m * 512:(m + 1) * 512],
                                          resB[:, m * 512:(m + 1) * 512],
                                          PS[b][:, 0:512], OP.add)
            emit_ln(resB, V, Cc, "l2h%d" % H)
            lv = pr.cnt["v_self"]
            pr.wait("s", "v_self", lv)
            pr.dma("s", bass.AP(x2d, t0, [[NT, 128], [128 * NT, 32], [1, 512]]),
                   ap3(resB[:, :], 0, [[512, 32], [1, 512]]), "bxo")
        pr.wait("s", "bxo")
        if "x2out" in dbg_outs:
            pr.wait("g", "bxo")
            pr.dma("g", dbg_outs["x2out"].ap(), x2d.ap(), "bdg")
            pr.wait("g", "bdg")

    def phaseC(sc_):
        V = pr.chained("v")
        Cc = pr.chained("c")
        resC = sc_["resC"]; xcB = sc_["xcB"]; TbB = sc_["TbB"]; HB = sc_["HB"]
        pr.dma("g", ap3(HB[:, :], 0, [[128, 32], [1, 128]]),
               bass.AP(Kpad, 0, [[1, 128], [128, 32], [1, 128]]), "chb")
        pr.wait("t", "chb")
        for j in range(8):
            b = j % 2
            if j >= 2:
                pr.wait("t", "v_self")
            pr.seq("t", lambda t, j=j, b=b: t.matmul(PS[2 + 2 * b][:, 0:512], ct["rev"],
                                                     HB[:, j * 512:(j + 1) * 512],
                                                     start=True, stop=True), "crev")
            pr.wait("v", "crev")
            V.tensor_copy(TbB[:, j * 512:(j + 1) * 512], PS[2 + 2 * b][:, 0:512])
        for H in range(2):
            t0 = H * 512
            if H == 1:
                pr.wait("s", "cxo")
                pr.wait("g", "cmm")
            pr.dma("s", ap3(resC[:, :], 0, [[512, 32], [1, 512]]),
                   bass.AP(x2d, t0, [[NT, 128], [128 * NT, 32], [1, 512]]), "cra")
            pr.dma("g", ap3(xcB[:, :], 0, [[512, 32], [1, 512]]),
                   bass.AP(x2d, t0, [[NT, 128], [128 * NT, 32], [1, 512]]), "cxb")
            pr.wait("t", "cxb")
            pr.wait("t", "v_self")
            pr.wait("v", "cra")
            addc = {}
            for jd in range(32):
                b = jd % 2
                if jd >= 2:
                    pr.wait("t", "v_self", addc[jd - 2])
                def f(t, jd=jd, b=b):
                    last = None
                    for d in range(jd + 1):
                        id_ = jd - d
                        last = t.matmul(PS[b][:, 0:512], TbB[:, d * 128:(d + 1) * 128],
                                        xcB[:, id_ * 512:(id_ + 1) * 512],
                                        start=(d == 0), stop=(d == jd))
                    return last
                pr.seq("t", f, "cmm")
                pr.wait("v", "cmm")
                addc[jd] = V.scalar_tensor_tensor(resC[:, jd * 512:(jd + 1) * 512],
                                                  resC[:, jd * 512:(jd + 1) * 512],
                                                  cpool[:, CP_D1:CP_D1 + 1],
                                                  PS[b][:, 0:512], OP.mult, OP.add)
            emit_ln(resC, V, Cc, "l3h%d" % H)
            lv = pr.cnt["v_self"]
            pr.wait("s", "v_self", lv)
            pr.dma("s", bass.AP(x3d, t0, [[NT, 128], [128 * NT, 32], [1, 512]]),
                   ap3(resC[:, :], 0, [[512, 32], [1, 512]]), "cxo")
        pr.wait("s", "cxo")
        if "x3out" in dbg_outs:
            pr.wait("g", "cxo")
            pr.dma("g", dbg_outs["x3out"].ap(), x3d.ap(), "cdg")
            pr.wait("g", "cdg")

    def phaseDq(sq_):
        V = pr.chained("v")
        Cc = pr.chained("c")
        qt = sq_["qt"]; msk = sq_["msk"]; acc = sq_["acc"]
        sct = sq_["sct"]; wout = sq_["wout"]
        jobs = [(w1qT, s1T, SH12, 32, agw_in[1]), (w2qT, s2T, SH12, 32, agw_in[2]),
                (w3qT, s3T, SH3, 86, agw_in[3])]
        i = 0
        copyc = {}
        lastact = {}
        lastwv = {}
        for (qsrc, ssrc, ncols, ntiles, dst) in jobs:
            for kt in range(ntiles):
                sl = i % 2
                if i >= 2:
                    pr.wait("g", "c_self", lastact[sl])
                    pr.wait("g", "v_self", lastwv[sl])
                pr.dma("g", qt[:, sl * 1408:sl * 1408 + ncols],
                       bass.AP(qsrc, kt * 128 * ncols, [[ncols, 128], [1, ncols]]),
                       "qst%d" % sl)
                pr.dma("g", sct[:, sl * 1408:sl * 1408 + ncols],
                       bass.AP(ssrc, 2 * kt * ncols, [[ncols, 2], [0, 64], [1, ncols]]),
                       "a3x%d" % sl)
                pr.wait("v", "qst%d" % sl)
                V.memset(acc[:, 0:ncols], float(NF4_BF[0]))
                pr.wait("c", "qst%d" % sl)
                for v in range(1, 16):
                    mb = v % 2
                    gate = copyc.get((i, v - 2)) if v >= 3 else copyc.get((i - 1, 16 - v))
                    if gate:
                        pr.wait("c", "v_self", gate)
                    actc = Cc.activation(msk[:, mb * 1408:mb * 1408 + ncols],
                                         qt[:, sl * 1408:sl * 1408 + ncols],
                                         AF.Relu, bias=cpool[:, v:v + 1])
                    pr.wait("v", "c_self", actc)
                    copyc[(i, v)] = V.copy_predicated(
                        acc[:, 0:ncols], msk[:, mb * 1408:mb * 1408 + ncols],
                        tpool[:, v:v + 1].to_broadcast([128, ncols]))
                lastact[sl] = pr.cnt["c_self"]
                pr.wait("v", "a3x%d" % sl)
                pr.wait("v", "wost%d" % sl)
                lastwv[sl] = V.tensor_tensor(wout[:, sl * 1408:sl * 1408 + ncols],
                                             acc[:, 0:ncols],
                                             sct[:, sl * 1408:sl * 1408 + ncols], OP.mult)
                pr.wait("s", "v_self", lastwv[sl])
                pr.dma("s", bass.AP(dst, kt * 128 * ncols, [[ncols, 128], [1, ncols]]),
                       wout[:, sl * 1408:sl * 1408 + ncols], "wost%d" % sl)
                i += 1
        pr.wait("g", "wost0")
        pr.wait("g", "wost1")
        for wi in (1, 2, 3):
            def f(g, wi=wi):
                return g.collective_compute(
                    "AllGather", OP.bypass, replica_groups=[list(range(NC))],
                    ins=[agw_in[wi].ap().opt()], outs=[agw_out[wi].ap().opt()])
            pr.seq("g", f, "agd")
        pr.wait("g", "agd")

    def phaseD1(sd_):
        V = pr.chained("v")
        Cc = pr.chained("c")
        xdB = sd_["xdB"]; wcolD = sd_["wcolD"]; silB = sd_["silB"]; hv = sd_["hv"]
        pr.dma("g", ap3(xdB[:, :], 0, [[1024, 32], [1, 1024]]),
               bass.AP(x3d, 0, [[NT, 128], [128 * NT, 32], [1, 1024]]), "woo")
        pr.wait("t", "woo")
        prevc = {}
        for tg in range(86):
            j, tt = tg // 11, tg % 11
            sl = tg % 2
            base = sl * 8192
            if tg >= 2:
                pr.wait("g", "d1mm", tg - 1)
            for wi in (1, 2):
                pr.dma("g", ap3(wcolD[:, :], base + (wi - 1) * 4096, [[128, 32], [1, 128]]),
                       bass.AP(agw_out[wi], j * D * SH12 + tt * 128,
                               [[SH12, 128], [128 * SH12, 32], [1, 128]]), "a2q%d" % sl)
            pr.wait("t", "a2q%d" % sl)
            if tg >= 1:
                pc_, pv_ = prevc[tg - 1]
                pr.wait("t", "c_self", pc_)
                pr.wait("t", "v_self", pv_)
            def f(t, base=base):
                last = None
                for wi in range(2):
                    for g2 in range(2):
                        for k in range(32):
                            last = t.matmul(PS[2 * wi + g2][:, 0:512],
                                            wcolD[:, base + wi * 4096 + k * 128:
                                                  base + wi * 4096 + (k + 1) * 128],
                                            xdB[:, k * 1024 + g2 * 512:k * 1024 + (g2 + 1) * 512],
                                            start=(k == 0), stop=(k == 31))
                return last
            pr.seq("t", f, "d1mm")
            pr.wait("c", "d1mm")
            if tg >= 2:
                pr.wait("c", "v_self")
            sc_ = None
            for g2 in range(2):
                sc_ = Cc.activation(silB[:, sl * 1024 + g2 * 512:sl * 1024 + (g2 + 1) * 512],
                                    PS[g2][:, 0:512], AF.Silu)
            pr.wait("v", "c_self", sc_)
            pr.wait("v", "bra" if sl == 0 else "cra")
            hvc = None
            for g2 in range(2):
                hvc = V.tensor_tensor(hv[:, sl * 1024 + g2 * 512:sl * 1024 + (g2 + 1) * 512],
                                      silB[:, sl * 1024 + g2 * 512:sl * 1024 + (g2 + 1) * 512],
                                      PS[2 + g2][:, 0:512], OP.mult)
            prevc[tg] = (sc_, hvc)
            pr.wait("s", "v_self", hvc)
            pr.dma("s", bass.AP(hdr, tg * 128 * NT, [[NT, 128], [1, 1024]]),
                   hv[:, sl * 1024:(sl + 1) * 1024], "bra" if sl == 0 else "cra")
        pr.wait("s", "bra")
        pr.wait("s", "cra")

    def phaseD2(sd_):
        V = pr.chained("v")
        Cc = pr.chained("c")
        hres = sd_["hres"]; resD = sd_["resD"]; w3col = sd_["w3col"]
        pr.wait("g", "bra")
        pr.wait("g", "cra")
        for H in range(2):
            t0 = H * 512
            if H == 1:
                pr.wait("s", "bxo")
                pr.wait("g", "cmm")
                pr.wait("t", "v_self")
            pr.dma("g", ap3(hres[:, :], 0, [[512, 86], [1, 512]]),
                   bass.AP(hdr, t0, [[NT, 128], [128 * NT, 86], [1, 512]]), "cxb")
            pr.dma("s", ap3(resD[:, :], 0, [[512, 32], [1, 512]]),
                   bass.AP(x3d, t0, [[NT, 128], [128 * NT, 32], [1, 512]]), "bxb")
            pr.wait("t", "cxb")
            pr.wait("v", "bxb")
            addc = {}
            for m in range(32):
                b = m % 2
                j, dj = (m * 128) // 512, (m * 128) % 512
                pr.wait("g", "cmm")
                pr.dma("g", ap3(w3col[:, :], 0, [[128, 86], [1, 128]]),
                       bass.AP(agw_out[3], j * HID * SH3 + dj,
                               [[SH3, 128], [128 * SH3, 86], [1, 128]]), "brt")
                pr.wait("t", "brt")
                if m >= 2:
                    pr.wait("t", "v_self", addc[m - 2])
                def f(t, b=b):
                    last = None
                    for k in range(86):
                        last = t.matmul(PS[b][:, 0:512], w3col[:, k * 128:(k + 1) * 128],
                                        hres[:, k * 512:(k + 1) * 512],
                                        start=(k == 0), stop=(k == 85))
                    return last
                pr.seq("t", f, "cmm")
                pr.wait("v", "cmm")
                addc[m] = V.tensor_tensor(resD[:, m * 512:(m + 1) * 512],
                                          resD[:, m * 512:(m + 1) * 512],
                                          PS[b][:, 0:512], OP.add)
            emit_ln(resD, V, Cc, "l2h%d" % H)
            lv = pr.cnt["v_self"]
            pr.wait("s", "v_self", lv)
            pr.dma("s", bass.AP(out, t0, [[NT, 128], [128 * NT, 32], [1, 512]]),
                   ap3(resD[:, :], 0, [[512, 32], [1, 512]]), "bxo")
        pr.wait("s", "bxo")
        pr.wait("g", "bxo")
        pr.wait("t", "bxo")

    from contextlib import ExitStack
    if debug.get("upto", 99) >= 1:
        with ExitStack() as stk:
            mk = lambda n, s, dt: stk.enter_context(nc.sbuf_tensor(n, s, dt))
            sa = dict(
                kst=mk("pa_kst", [128, 8192], BF16),
                kbart=mk("pa_kbart", [128, 8192], BF16),
                ropekt=mk("pa_ropekt", [128, 8192], BF16),
                vnat=mk("pa_vnat", [128, 8192], BF16),
                ebt=mk("pa_ebt", [128, 8192], BF16),
                haloT=mk("pa_haloT", [128, 1024], BF16),
                vhalo=mk("pa_vhalo", [64, 1024], BF16),
                rkhalo=mk("pa_rkhalo", [128, 512], BF16),
                Sst=mk("pa_Sst", [128, 1024], F32),
                Zrun=mk("pa_Zrun", [128, 1024], F32),
                Drun=mk("pa_Drun", [128, 8], F32),
                rkcols=mk("pa_rkcols", [128, 64], F32),
                ebendF=mk("pa_ebendF", [128, 64], F32),
                cosb=mk("pa_cosb", [128, NT + 64], BF16),
                sinb=mk("pa_sinb", [128, NT + 64], BF16),
                wkF=mk("pa_wkF", [128, 1024], F32),
            )
            with ExitStack() as stk2:
                mk2 = lambda n, s, dt: stk2.enter_context(nc.sbuf_tensor(n, s, dt))
                sa.update(
                    xb=mk2("pa_xb", [128, 16384], BF16),
                    xbh=mk2("pa_xbh", [128, 2048], BF16),
                    kvT=mk2("pa_kvT", [128, 8192], BF16),
                    wcol=mk2("pa_wcol", [128, 8192], BF16),
                    qtmp=mk2("pa_qtmp", [128, 1024], BF16),
                    crow=mk2("pa_crow", [1, 2048], F32),
                    rkraw=mk2("pa_rkraw", [128, 16], F32),
                )
                phaseA(sa)
            pr.barrier()
            with ExitStack() as stk2:
                mk2 = lambda n, s, dt: stk2.enter_context(nc.sbuf_tensor(n, s, dt))
                sa.update(
                    qch=mk2("pa_qch", [128, 8192], BF16),
                    oT=mk2("pa_oT", [128, 4096], BF16),
                    Scast=mk2("pa_Scast", [128, 128], BF16),
                )
                phaseA2(sa)
            pr.barrier()
        with ExitStack() as stk:
            mk = lambda n, s, dt: stk.enter_context(nc.sbuf_tensor(n, s, dt))
            phaseWo(dict(oTf=mk("pw_oTf", [128, 32768], BF16),
                         wcol=mk("pw_wcol", [128, 8192], BF16),
                         aev=mk("pw_aev", [128, 2048], F32)))
        pr.barrier()
        with ExitStack() as stk:
            mk = lambda n, s, dt: stk.enter_context(nc.sbuf_tensor(n, s, dt))
            global_stat[0] = mk("pa3_stat", [1, 3072], F32)
            global_statB[0] = mk("pa3_statB", [128, 1024], F32)
            phaseA3(dict(resA=mk("pa3_res", [128, 16384], F32),
                         xroll=mk("pa3_xr", [128, 1024], F32)))
        pr.barrier()

    if debug.get("upto", 99) >= 2:
        with ExitStack() as stk:
            mk = lambda n, s, dt: stk.enter_context(nc.sbuf_tensor(n, s, dt))
            global_stat[0] = mk("pb_stat", [1, 3072], F32)
            global_statB[0] = mk("pb_statB", [128, 1024], F32)
            phaseB(dict(resB=mk("pb_res", [128, 16384], F32),
                        xbB=mk("pb_xb", [128, 16384], BF16),
                        AfB=mk("pb_Af", [128, 4096], BF16),
                        BwB=mk("pb_Bw", [128, 4096], BF16),
                        rtT=mk("pb_rt", [128, 256], F32),
                        lg8=mk("pb_lg8", [8, 512], F32),
                        lgT=mk("pb_lgT", [128, 32], F32),
                        eqT=mk("pb_eqT", [128, 32], F32),
                        mxc=mk("pb_mxc", [128, 4], F32),
                        rw8=mk("pb_rw8", [8, 512], F32),
                        rwXs=mk("pb_rwXs", [128, 512], F32),
                        whB=mk("pb_whB", [128, 512], BF16)))
        pr.barrier()

    if debug.get("upto", 99) >= 3:
        with ExitStack() as stk:
            mk = lambda n, s, dt: stk.enter_context(nc.sbuf_tensor(n, s, dt))
            global_stat[0] = mk("pc_stat", [1, 3072], F32)
            global_statB[0] = mk("pc_statB", [128, 1024], F32)
            phaseC(dict(resC=mk("pc_res", [128, 16384], F32),
                        xcB=mk("pc_xc", [128, 16384], BF16),
                        TbB=mk("pc_Tb", [128, 4096], BF16),
                        HB=mk("pc_HB", [128, 4096], BF16)))
        pr.barrier()

    if debug.get("upto", 99) >= 4:
        with ExitStack() as stk:
            mk = lambda n, s, dt: stk.enter_context(nc.sbuf_tensor(n, s, dt))
            phaseDq(dict(qt=mk("pd_qt", [128, 2816], BF16),
                         msk=mk("pd_msk", [128, 2816], mybir.dt.uint8),
                         acc=mk("pd_acc", [128, 1408], BF16),
                         sct=mk("pd_sct", [128, 2816], BF16),
                         wout=mk("pd_wout", [128, 2816], BF16)))
        pr.barrier()
        with ExitStack() as stk:
            mk = lambda n, s, dt: stk.enter_context(nc.sbuf_tensor(n, s, dt))
            phaseD1(dict(xdB=mk("pd1_x", [128, 32768], BF16),
                         wcolD=mk("pd1_w", [128, 16384], BF16),
                         silB=mk("pd1_s", [128, 2048], BF16),
                         hv=mk("pd1_h", [128, 2048], BF16)))
        pr.barrier()
        with ExitStack() as stk:
            mk = lambda n, s, dt: stk.enter_context(nc.sbuf_tensor(n, s, dt))
            global_stat[0] = mk("pd2_stat", [1, 3072], F32)
            global_statB[0] = mk("pd2_statB", [128, 1024], F32)
            phaseD2(dict(hres=mk("pd2_hr", [128, 44032], BF16),
                         resD=mk("pd2_res", [128, 16384], F32),
                         w3col=mk("pd2_w3", [128, 11008], BF16)))
        pr.barrier()
    with nc.Block() as block:
        pr.emit(block)
    return nc, C, dbg_outs


def host_inputs(inputs, core):
    C = _host_constants()
    beta, sq = core // 4, core % 4
    s0 = sq * NT
    x = inputs["L0_attn_hidden_states"]
    m = {}
    m["xT"] = np.ascontiguousarray(x[beta, s0:s0 + NT, :].T)
    if sq == 0:
        m["xTh"] = np.zeros((D, 64), np.float32)
        m["mask1c0"] = np.full((64, 128), -MSK, np.float32)
    else:
        m["xTh"] = np.ascontiguousarray(x[beta, s0 - 64:s0, :].T)
        jj1 = np.arange(64)[:, None]; ii1 = np.arange(128)[None, :]
        m["mask1c0"] = np.where(jj1 >= ii1, 0.0, -MSK).astype(np.float32)
    m["Wq"] = inputs["L0_attn_Wq"]
    m["Wk"] = inputs["L0_attn_Wk"]
    m["Wv"] = inputs["L0_attn_Wv"]
    m["Wo"] = inputs["L0_attn_Wo"]
    m["routeT"] = np.ascontiguousarray(inputs["L0_ffn_route_w"].T)
    m["Aflat"] = np.ascontiguousarray(inputs["L0_ffn_A"].reshape(128, D).T)
    m["BwT"] = np.ascontiguousarray(inputs["L0_ffn_Bw"].transpose(0, 2, 1).reshape(128, D))
    m["lamr"] = np.ascontiguousarray(inputs["L1_attn_Lambda_ri"].T)
    m["prow"] = np.ascontiguousarray(inputs["L1_attn_p_ri"].T.reshape(1, 128))
    m["qrow"] = np.ascontiguousarray(inputs["L1_attn_q_ri"].T.reshape(1, 128))
    m["VcTre"] = np.ascontiguousarray(inputs["L1_attn_Vc_ri"][:, :, 0].T)
    m["VcTim"] = np.ascontiguousarray(inputs["L1_attn_Vc_ri"][:, :, 1].T)
    m["Ctr"] = np.ascontiguousarray(inputs["L1_attn_Ct_ri"][0].T.reshape(1, 128))
    m["Bp"] = inputs["L1_attn_Bp"]
    m["Dlog"] = np.concatenate([inputs["L1_attn_D"], inputs["L1_attn_log_step"],
                                np.zeros(2, np.float32)]).reshape(1, 4).astype(np.float32)
    w1t = np.zeros((D, HID_PAD), np.int32); w1t[:, :HID] = inputs["L1_ffn_w1_q"].T
    w2t = np.zeros((D, HID_PAD), np.int32); w2t[:, :HID] = inputs["L1_ffn_w2_q"].T
    s1t = np.zeros((64, HID_PAD), np.float32); s1t[:, :HID] = inputs["L1_ffn_w1_scale"].T
    s2t = np.zeros((64, HID_PAD), np.float32); s2t[:, :HID] = inputs["L1_ffn_w2_scale"].T
    m["w1qT"] = np.ascontiguousarray(w1t[:, core * SH12:(core + 1) * SH12])
    m["w2qT"] = np.ascontiguousarray(w2t[:, core * SH12:(core + 1) * SH12])
    m["s1T"] = np.ascontiguousarray(s1t[:, core * SH12:(core + 1) * SH12])
    m["s2T"] = np.ascontiguousarray(s2t[:, core * SH12:(core + 1) * SH12])
    w3t = inputs["L1_ffn_w3_q"].T
    s3t = inputs["L1_ffn_w3_scale"].T
    m["w3qT"] = np.ascontiguousarray(w3t[:, core * SH3:(core + 1) * SH3])
    m["s3T"] = np.ascontiguousarray(s3t[:, core * SH3:(core + 1) * SH3])
    cos, sin = _host_rope(s0)
    sgn = np.where(np.arange(128) < 64, -1.0, 1.0).astype(np.float32)
    m["cosTt"] = np.ascontiguousarray(cos.T).astype(BF)
    m["sinTt"] = np.ascontiguousarray((sin * sgn[None, :]).T).astype(BF)
    for k, v in C.items():
        m["c_" + k] = np.asarray(v)
    mvec = np.zeros(8, np.float32)
    for j in range(8):
        if j // 4 == beta and j < core:
            mvec[j] = 1.0
    sm = np.zeros((128, 16), np.float32)
    sm[:, 0:8] = mvec[None, :]
    sm[:, 8:16] = 1.0 - mvec[None, :]
    m["scanm"] = sm
    return m


def kernel(**inputs):
    nc, C, dbg = build_program()
    in_maps = [host_inputs(inputs, c) for c in range(NC)]
    res = run_bass_kernel_spmd(nc, in_maps, core_ids=list(range(NC)))
    full = np.zeros((2, 4096, D), np.float32)
    for c in range(NC):
        beta, sq = c // 4, c % 4
        full[beta, sq * NT:(sq + 1) * NT, :] = res.results[c]["out"].T
    return full


# version 02
# speedup vs baseline: 1.1463x; 1.1463x over previous
import numpy as np
import ml_dtypes
import concourse.bass as bass
import concourse.mybir as mybir
from concourse.bass_utils import run_bass_kernel_spmd

F32 = mybir.dt.float32
BF16 = mybir.dt.bfloat16
I32 = mybir.dt.int32
U8 = mybir.dt.uint8
OP = mybir.AluOpType
AF = mybir.ActivationFunctionType
AX = mybir.AxisListType
BF = ml_dtypes.bfloat16

D = 4096
NT = 1024
NC = 8
DK = 128
KVH = 8
GRP = 4
CH = 128
NCH = 8
HID = 11008
HID_PAD = 11264
SH12 = HID_PAD // NC
SH3 = D // NC
THETA = 10000.0
MSK = 30000.0
ISQD = 0.08838834764831845
NF4 = np.array([-1.0, -0.6961928009986877, -0.5250730514526367, -0.39491748809814453,
                -0.28444138169288635, -0.18477343022823334, -0.09105003625154495, 0.0,
                0.07958029955625534, 0.16093020141124725, 0.24611230194568634,
                0.33791524171829224, 0.44070982933044434, 0.5626170039176941,
                0.7229568362236023, 1.0], dtype=np.float32)
NF4_BF = NF4.astype(BF)

CP_D1 = 16; CP_2STEP = 17; CP_EPS = 18; CP_INVD = 19; CP_HALF = 20
CP_1P5 = 21; CP_NEG1 = 22; CP_ISQD = 23; CP_NHALF = 24; CP_G16 = 25; CP_P16 = 26


def _host_constants():
    c = {}
    c["ident"] = np.eye(128, dtype=np.float32).astype(BF)
    c["identf"] = np.eye(128, dtype=np.float32)
    rev = np.zeros((128, 128), np.float32)
    rev[np.arange(128), 127 - np.arange(128)] = 1.0
    c["rev"] = rev.astype(BF)
    c["triu_ji"] = np.triu(np.ones((128, 128), np.float32)).astype(BF)
    jj = np.arange(128)[:, None]
    ii = np.arange(128)[None, :]
    c["mask2"] = np.where((jj <= ii) & (jj >= ii - 64), 0.0, -MSK).astype(np.float32)
    c["mask1e"] = np.where(jj >= ii + 64, 0.0, -MSK).astype(np.float32)
    bs = np.zeros((8, 128), np.float32)
    for e in range(8):
        bs[e, e * 16:(e + 1) * 16] = 1.0
    c["blocksel"] = bs
    c["ones_col"] = np.ones((128, 1), np.float32).astype(BF)
    c["ones_row"] = np.ones((1, 128), np.float32).astype(BF)
    pm = np.zeros((128, 128), np.float32)
    for d in range(128):
        pm[(d + 64) % 128, d] = 1.0
    c["perm"] = pm.astype(BF)
    k2 = np.arange(64)
    W64 = np.exp(2j * np.pi * np.outer(k2, k2) / 64.0)
    c["W64re"] = W64.real.astype(np.float32)
    c["W64im"] = W64.imag.astype(np.float32)
    Tw = np.exp(2j * np.pi * np.outer(k2, k2) / 4096.0)
    c["Twre"] = Tw.real.astype(np.float32)
    c["Twim"] = Tw.imag.astype(np.float32)
    W64b = np.exp(2j * np.pi * np.outer(k2, k2) / 64.0) / 4096.0
    c["W64bre"] = W64b.real.astype(np.float32)
    c["W64bim"] = W64b.imag.astype(np.float32)
    t32 = (np.arange(4096).astype(np.float32) / np.float32(4096.0)).astype(np.float32)
    ang = (np.float32(-2.0 * np.pi) * t32).astype(np.float32)
    om = (np.cos(ang.astype(np.float64)) + 1j * np.sin(ang.astype(np.float64)))
    om = om.astype(np.complex64).astype(np.complex128)
    g0 = (1.0 - om) / (1.0 + om)
    cc2 = 2.0 / (1.0 + om)
    c["g0re"] = g0.real.astype(np.float32).reshape(64, 64)
    c["g0im"] = g0.imag.astype(np.float32).reshape(64, 64)
    c["ccre"] = cc2.real.astype(np.float32).reshape(64, 64)
    c["ccim"] = cc2.imag.astype(np.float32).reshape(64, 64)
    return c


def _host_rope(pos0):
    pos = np.arange(pos0 - 64, pos0 + NT).astype(np.float64)
    half = np.arange(64, dtype=np.float64)
    inv = 1.0 / THETA ** (2.0 * half / DK)
    fr = np.maximum(pos, 0.0)[:, None] * inv[None, :]
    emb = np.concatenate([fr, fr], axis=1)
    return np.cos(emb).astype(np.float32), np.sin(emb).astype(np.float32)


class Prog:

    def __init__(self, nc):
        self.nc = nc
        self.ops = {k: [] for k in "gscvt"}
        self.sems = {}
        self.cnt = {}
        self.pending = {k: [] for k in "gscvt"}
        self.last = {k: None for k in "gscvt"}

    def sem(self, name):
        if name not in self.sems:
            self.sems[name] = self.nc.ctx.enter_context(self.nc.semaphore(name))
            self.cnt[name] = 0
        return self.sems[name]

    def on(self, eng, fn):
        self.ops[eng].append(fn)

    def wait(self, eng, name, val=None):
        s = self.sem(name)
        v = self.cnt[name] if val is None else val
        self.ops[eng].append(lambda e, s=s, v=v: e.wait_ge(s, v))
        return v

    def mark(self, eng, name, by=1):
        s = self.sem(name)
        self.cnt[name] += by
        self.pending[eng].append((s, by))
        return self.cnt[name]

    def _take_marks(self, eng):
        m = self.pending[eng]
        self.pending[eng] = []
        return m

    def chained(self, eng):
        return Chained(self, eng)

    def seq(self, eng, fn, name=None, by=1):
        marks = self._take_marks(eng)
        if name is not None:
            s = self.sem(name)
            self.cnt[name] += by
            marks = marks + [(s, by)]
        def run(e, fn=fn, marks=marks):
            i = fn(e)
            for ms, mv in marks:
                i = i.then_inc(ms, mv)
        self.ops[eng].append(run)
        if name is not None:
            self.last[eng] = (name, self.cnt[name])

    def dma(self, eng, out, in_, sem_name, **kw):
        s = self.sem(sem_name)
        self.cnt[sem_name] += 16
        marks = self._take_marks(eng)
        nc = self.nc
        def run(e, out=out, in_=in_, kw=kw, s=s, marks=marks):
            n0 = nc.n_instructions()
            i = e.dma_start(out=out, in_=in_, **kw)
            assert nc.n_instructions() - n0 == 1, "dma split"
            i = i.then_inc(s, 16)
            for ms, mv in marks:
                i = i.then_inc(ms, mv)
        self.ops[eng].append(run)
        self.last[eng] = (sem_name, self.cnt[sem_name])
        return self.cnt[sem_name]

    def barrier(self):
        lasts = {e: self.last[e] for e in "gscvt" if self.last[e] is not None}
        for e in "gscvt":
            if not self.ops[e]:
                continue
            for e2, (nm, vv) in lasts.items():
                if e2 != e:
                    self.wait(e, nm, vv)

    def emit(self, block):
        for tag, attr in (("g", "gpsimd"), ("s", "sync"), ("c", "scalar"),
                          ("v", "vector"), ("t", "tensor")):
            ops = self.ops[tag]
            if ops:
                getattr(block, attr)(lambda e, ops=ops: [f(e) for f in ops])


class Chained:

    def __init__(self, pr, eng):
        object.__setattr__(self, "_pr", pr)
        object.__setattr__(self, "_eng", eng)

    def __getattr__(self, opname):
        pr, eng = self._pr, self._eng
        sname = eng + "_self"
        s = pr.sem(sname)
        def call(*args, **kw):
            prev = pr.cnt[sname]
            pr.cnt[sname] += 1
            marks = pr._take_marks(eng)
            def run(e, prev=prev, opname=opname, args=args, kw=kw, marks=marks):
                if prev > 0:
                    e.wait_ge(s, prev)
                i = getattr(e, opname)(*args, **kw).then_inc(s, 1)
                for ms, mv in marks:
                    i = i.then_inc(ms, mv)
            pr.ops[eng].append(run)
            pr.last[eng] = (sname, pr.cnt[sname])
            return pr.cnt[sname]
        return call


def ap3(t, extra_off, dims):
    a = t if isinstance(t, bass.AP) else t.ap()
    return bass.AP(a.tensor, a.offset + extra_off, [list(a.ap[0])] + [list(d) for d in dims])


def build_program(debug=None):
    debug = debug or {}
    nc = bass.Bass("TRN2", target_bir_lowering=False)
    pr = Prog(nc)
    C = _host_constants()

    ein = lambda n, s, dt=F32: nc.dram_tensor(n, s, dt, kind="ExternalInput")
    dint = lambda n, s, dt=F32: nc.dram_tensor(n, s, dt, kind="Internal")

    xT = ein("xT", [D, NT]); xTh = ein("xTh", [D, 64])
    Wq = ein("Wq", [D, D]); Wk = ein("Wk", [D, 1024]); Wv = ein("Wv", [D, 1024])
    Wo = ein("Wo", [D, D])
    routeT = ein("routeT", [D, 8]); Aflat = ein("Aflat", [D, 128]); BwT = ein("BwT", [128, D])
    prow_i = ein("prow", [1, 128]); qrow_i = ein("qrow", [1, 128]); lamr = ein("lamr", [2, 64])
    VcTre = ein("VcTre", [64, 64]); VcTim = ein("VcTim", [64, 64])
    Ctr = ein("Ctr", [1, 128])
    Bp = ein("Bp", [64, 1]); Dlog = ein("Dlog", [1, 4])
    w1qT = ein("w1qT", [D, SH12], I32); w2qT = ein("w2qT", [D, SH12], I32)
    w3qT = ein("w3qT", [HID, SH3], I32)
    s1T = ein("s1T", [64, SH12]); s2T = ein("s2T", [64, SH12]); s3T = ein("s3T", [172, SH3])
    cosTt = ein("cosTt", [128, NT + 64], BF16); sinTt = ein("sinTt", [128, NT + 64], BF16)
    cin = {k: ein("c_" + k, list(v.shape), BF16 if v.dtype == BF else F32) for k, v in C.items()}
    mask1c0 = ein("mask1c0", [64, 128])
    scanm = ein("scanm", [128, 16])
    out = nc.dram_tensor("out", [D, NT], F32, kind="ExternalOutput")
    dbg_outs = {}
    if "Kout" in debug:
        dbg_outs["Kout"] = nc.dram_tensor("Kout", [1, 4352], F32, kind="ExternalOutput")
        dbg_outs["Kacc"] = nc.dram_tensor("Kacc", [64, 512], F32, kind="ExternalOutput")
        dbg_outs["Kar"] = nc.dram_tensor("Kar", [64, 128], F32, kind="ExternalOutput")
        dbg_outs["Krows"] = nc.dram_tensor("Krows", [1, 642], F32, kind="ExternalOutput")
    if "x1out" in debug:
        dbg_outs["x1out"] = nc.dram_tensor("x1out", [D, NT], F32, kind="ExternalOutput")
    if "x2out" in debug:
        dbg_outs["x2out"] = nc.dram_tensor("x2out", [D, NT], F32, kind="ExternalOutput")
        dbg_outs["rwout"] = nc.dram_tensor("rwout", [8, NT], F32, kind="ExternalOutput")
    if "x3out" in debug:
        dbg_outs["x3out"] = nc.dram_tensor("x3out", [D, NT], F32, kind="ExternalOutput")
    if "aout" in debug:
        dbg_outs["qout"] = nc.dram_tensor("qout", [D, NT], BF16, kind="ExternalOutput")
        dbg_outs["oout"] = nc.dram_tensor("oout", [D, NT], BF16, kind="ExternalOutput")
        dbg_outs["attnout"] = nc.dram_tensor("attnout", [D, NT], F32, kind="ExternalOutput")
        dbg_outs["adiag"] = nc.dram_tensor("adiag", [128, 256], F32, kind="ExternalOutput")
        dbg_outs["adiag2"] = nc.dram_tensor("adiag2", [3, 3072], F32, kind="ExternalOutput")

    x1d = dint("x1d", [D, NT]); x2d = dint("x2d", [D, NT]); x3d = dint("x3d", [D, NT])
    attnTd = dint("attnTd", [D, NT])
    qdr = dint("qdr", [D, NT], BF16)
    odr = dint("odr", [D, NT], BF16)
    hdr = dint("hdr", [HID, NT], BF16)
    Kpad = dint("Kpad", [1, 4352])
    rowsd = dint("rowsd", [4, 8192])
    ag_s_in = dint("ag_s_in", [KVH, 130, 128])
    ag_s_out = dint("ag_s_out", [NC, KVH, 130, 128])
    agw_in = {1: dint("agw1i", [D, SH12], BF16), 2: dint("agw2i", [D, SH12], BF16),
              3: dint("agw3i", [HID, SH3], BF16)}
    agw_out = {1: dint("agw1o", [NC, D, SH12], BF16), 2: dint("agw2o", [NC, D, SH12], BF16),
               3: dint("agw3o", [NC, HID, SH3], BF16)}

    glob = nc.ctx
    sbg = lambda n, s, dt: glob.enter_context(nc.sbuf_tensor(n, s, dt))

    cpool = sbg("cpool", [128, 32], F32)
    tpool = sbg("tpool", [128, 16], BF16)
    ct = {k: sbg("ct_" + k, list(v.shape), BF16 if v.dtype == BF else F32).ap() for k, v in C.items()}
    m1c0_t = sbg("m1c0_t", [64, 128], F32)
    scanm_t = sbg("scanm_t", [128, 16], F32)
    global_stat = [None]
    global_statB = [None]
    wkD = sbg("wkD", [128, 2048], BF16)
    PS = [glob.enter_context(nc.psum_tensor(f"p{i}", [128, 512], F32)) for i in range(7)]
    PSB = glob.enter_context(nc.psum_tensor("pb7", [128, 1024], BF16))

    def setup():
        def f(g):
            for v in range(16):
                g.memset(cpool[:, v:v + 1], float(1 - v))
                g.memset(tpool[:, v:v + 1], float(NF4_BF[v]))
            g.memset(cpool[:, CP_EPS:CP_EPS + 1], 1e-5)
            g.memset(cpool[:, CP_INVD:CP_INVD + 1], 1.0 / D)
            g.memset(cpool[:, CP_HALF:CP_HALF + 1], 0.5)
            g.memset(cpool[:, CP_1P5:CP_1P5 + 1], 1.5)
            g.memset(cpool[:, CP_NEG1:CP_NEG1 + 1], -1.0)
            g.memset(cpool[:, CP_ISQD:CP_ISQD + 1], ISQD)
            g.memset(cpool[:, CP_NHALF:CP_NHALF + 1], -0.5)
            g.memset(cpool[:, CP_G16:CP_G16 + 1], -1.0 / 16.0)
            g.memset(cpool[:, CP_P16:CP_P16 + 1], 1.0 / 16.0)
        pr.on("g", f)
        for k in C:
            pr.dma("g", ct[k][:, :], cin[k].ap(), "cld")
        pr.dma("g", m1c0_t[:, :], mask1c0.ap(), "cld")
        pr.dma("g", scanm_t[:, :], scanm.ap(), "cld")
        for e in "vcts":
            pr.wait(e, "cld")

    def phaseK(sk):
        wkK = sk["wkK"]
        V = pr.chained("v")
        Cc = pr.chained("c")
        pr.dma("g", sk["p"][:, :], prow_i.ap(), "kld")
        pr.dma("g", sk["q"][:, :], qrow_i.ap(), "kld")
        pr.dma("g", sk["vctre"][:, :], VcTre.ap(), "kld")
        pr.dma("g", sk["vctim"][:, :], VcTim.ap(), "kld")
        pr.dma("g", sk["ct"][:, :], Ctr.ap(), "kld")
        pr.dma("g", sk["bp"][:, :], Bp.ap(), "kld")
        pr.dma("g", sk["dl"][:, :], Dlog.ap(), "kld")
        pr.dma("g", sk["lsr"][0:1, 0:64], bass.AP(Dlog, 1, [[1, 1], [0, 64], [1, 1]]), "kld")
        pr.wait("t", "kld")
        def f(t):
            t.matmul(PS[0][0:1, 0:64], sk["bp"][:, :], sk["vctre"][:, :], start=True, stop=True)
            return t.matmul(PS[1][0:1, 0:64], sk["bp"][:, :], sk["vctim"][:, :], start=True, stop=True)
        pr.seq("t", f, "kmmB")
        pr.wait("c", "kld")
        kexp = Cc.activation(sk["lsr"][0:1, 64:128], sk["lsr"][0:1, 0:64], AF.Exp)
        pr.wait("v", "kmmB")
        pr.wait("v", "c_self", kexp)
        row = sk["rows"]
        t0 = wkK[0:1, 5120:5184]
        t1 = wkK[0:1, 5184:5248]
        V.tensor_copy(row[0:1, 0:64], PS[0][0:1, 0:64])
        V.tensor_copy(row[0:1, 64:128], PS[1][0:1, 0:64])
        def cmc(ar, ai, br, bi, orr, oi):
            V.tensor_tensor(t0, ar, br, OP.mult)
            V.tensor_tensor(t1, ai, bi, OP.mult)
            V.tensor_tensor(orr, t0, t1, OP.add)
            V.tensor_tensor(t0, ar, bi, OP.mult)
            V.tensor_tensor(t1, ai, br, OP.mult)
            V.tensor_tensor(oi, t0, t1, OP.subtract)
        ctR, ctI = sk["ct"][0:1, 0:64], sk["ct"][0:1, 64:128]
        pR, pI = sk["p"][0:1, 0:64], sk["p"][0:1, 64:128]
        qR, qI = sk["q"][0:1, 0:64], sk["q"][0:1, 64:128]
        bR, bI = row[0:1, 0:64], row[0:1, 64:128]
        cmc(ctR, ctI, bR, bI, row[0:1, 128:192], row[0:1, 192:256])
        cmc(ctR, ctI, pR, pI, row[0:1, 256:320], row[0:1, 320:384])
        cmc(qR, qI, bR, bI, row[0:1, 384:448], row[0:1, 448:512])
        cmc(qR, qI, pR, pI, row[0:1, 512:576], row[0:1, 576:640])
        V.reciprocal(sk["lsr"][0:1, 128:192], sk["lsr"][0:1, 64:128])
        V.tensor_scalar(row[0:1, 640:641], sk["lsr"][0:1, 128:129], 2.0, None, OP.mult)
        krow = V.tensor_scalar(row[0:1, 641:642], sk["dl"][0:1, 0:1], 1.0, None, OP.add)
        pr.wait("g", "v_self", krow)
        pr.dma("g", rowsd.ap()[0:1, 0:642], row[0:1, 0:642], "krd")
        pr.wait("g", "krd")
        lamb = sk["lamb"]; v4b = sk["v4b"]
        pr.dma("g", lamb[:, :], bass.AP(lamr, 0, [[0, 64], [1, 128]]), "kld2")
        pr.dma("g", v4b[:, :], bass.AP(rowsd, 128, [[0, 64], [1, 512]]), "kld2")
        pr.dma("g", cpool[:, CP_2STEP:CP_2STEP + 1], bass.AP(rowsd, 640, [[0, 128], [1, 1]]), "kld2")
        pr.dma("g", cpool[:, CP_D1:CP_D1 + 1], bass.AP(rowsd, 641, [[0, 128], [1, 1]]), "kld2")
        pr.wait("v", "kld2")
        acc = {nm: (sk["acc"][0:64, i * 128:i * 128 + 64],
                    sk["acc"][0:64, i * 128 + 64:(i + 1) * 128])
               for i, nm in enumerate(["k00", "k01", "k10", "k11"])}
        dre = wkK[0:64, 0:1024]; dim = wkK[0:64, 1024:2048]
        inv = wkK[0:64, 2048:3072]; ta = wkK[0:64, 3072:4096]
        tb = wkK[0:64, 4096:5120]
        gre_q = wkK[0:64, 5120:5136]; gim_q = wkK[0:64, 5136:5152]
        red = wkK[0:64, 5152:5216]
        for nm in ["k00", "k01", "k10", "k11"]:
            V.memset(acc[nm][0], 0.0)
            V.memset(acc[nm][1], 0.0)
        for qtr in range(4):
            k1o = qtr * 16
            V.tensor_scalar(gre_q, ct["g0re"][:, k1o:k1o + 16],
                            cpool[0:64, CP_2STEP:CP_2STEP + 1], None, OP.mult)
            V.tensor_scalar(gim_q, ct["g0im"][:, k1o:k1o + 16],
                            cpool[0:64, CP_2STEP:CP_2STEP + 1], None, OP.mult)
            g3r = ap3(gre_q, 0, [[1, 16], [0, 64]])
            g3i = ap3(gim_q, 0, [[1, 16], [0, 64]])
            l3r = ap3(lamb, 0, [[0, 16], [1, 64]])
            l3i = ap3(lamb, 64, [[0, 16], [1, 64]])
            V.tensor_tensor(dre, g3r, l3r, OP.subtract)
            V.tensor_tensor(dim, g3i, l3i, OP.subtract)
            V.tensor_tensor(inv, dre, dre, OP.mult)
            V.tensor_tensor(ta, dim, dim, OP.mult)
            V.tensor_tensor(ta, inv, ta, OP.add)
            V.reciprocal(inv, ta)
            for i, nm in enumerate(["k00", "k01", "k10", "k11"]):
                vr3 = ap3(v4b, i * 128, [[0, 16], [1, 64]])
                vi3 = ap3(v4b, i * 128 + 64, [[0, 16], [1, 64]])
                ar, ai = acc[nm]
                V.tensor_tensor(ta, vr3, dre, OP.mult)
                V.tensor_tensor(tb, vi3, dim, OP.mult)
                V.tensor_tensor(ta, ta, tb, OP.add)
                V.tensor_tensor(ta, ta, inv, OP.mult)
                V.reduce_sum(red[:, 0:16], ap3(ta, 0, [[64, 16], [1, 64]]), axis=AX.X)
                V.tensor_copy(ar[:, k1o:k1o + 16], red[:, 0:16])
                V.tensor_tensor(ta, vi3, dre, OP.mult)
                V.tensor_tensor(tb, vr3, dim, OP.mult)
                V.tensor_tensor(ta, ta, tb, OP.subtract)
                V.tensor_tensor(ta, ta, inv, OP.mult)
                V.reduce_sum(red[:, 0:16], ap3(ta, 0, [[64, 16], [1, 64]]), axis=AX.X)
                V.tensor_copy(ai[:, k1o:k1o + 16], red[:, 0:16])
        arre = sk["arre"]; arim = sk["arim"]; arimn = sk["arimn"]
        w1 = wkK[0:64, 0:64]; w2 = wkK[0:64, 64:128]
        w3 = wkK[0:64, 128:192]; w4 = wkK[0:64, 192:256]
        w5 = wkK[0:64, 256:320]
        a_re, a_im = acc["k11"]
        V.tensor_scalar(w1, a_re, 1.0, None, OP.add)
        V.tensor_tensor(w2, w1, w1, OP.mult)
        V.tensor_tensor(w3, a_im, a_im, OP.mult)
        V.tensor_tensor(w3, w2, w3, OP.add)
        V.reciprocal(w2, w3)
        br_, bi_ = acc["k01"]; cr_, ci_ = acc["k10"]
        V.tensor_tensor(w3, br_, cr_, OP.mult)
        V.tensor_tensor(w4, bi_, ci_, OP.mult)
        V.tensor_tensor(w3, w3, w4, OP.subtract)
        V.tensor_tensor(w4, br_, ci_, OP.mult)
        V.tensor_tensor(w5, bi_, cr_, OP.mult)
        V.tensor_tensor(w4, w4, w5, OP.add)
        V.tensor_tensor(w5, w3, w1, OP.mult)
        V.tensor_tensor(arre, w4, a_im, OP.mult)
        V.tensor_tensor(w5, w5, arre, OP.add)
        V.tensor_tensor(w5, w5, w2, OP.mult)
        V.tensor_tensor(arre, w4, w1, OP.mult)
        V.tensor_tensor(arim, w3, a_im, OP.mult)
        V.tensor_tensor(arre, arre, arim, OP.subtract)
        V.tensor_tensor(arre, arre, w2, OP.mult)
        k0r, k0i = acc["k00"]
        V.tensor_tensor(w3, k0r, w5, OP.subtract)
        V.tensor_tensor(w4, k0i, arre, OP.subtract)
        V.tensor_tensor(arre, ct["ccre"], w3, OP.mult)
        V.tensor_tensor(w5, ct["ccim"], w4, OP.mult)
        V.tensor_tensor(arre, arre, w5, OP.subtract)
        V.tensor_tensor(arim, ct["ccre"], w4, OP.mult)
        V.tensor_tensor(w5, ct["ccim"], w3, OP.mult)
        V.tensor_tensor(arim, arim, w5, OP.add)
        kar = V.tensor_scalar(arimn, arim, -1.0, None, OP.mult)
        pr.wait("t", "v_self", kar)
        def f(t):
            t.matmul(PS[0][0:64, 0:64], ct["W64re"], arre, start=True, stop=False)
            t.matmul(PS[0][0:64, 0:64], ct["W64im"], arimn, start=False, stop=True)
            t.matmul(PS[1][0:64, 0:64], ct["W64re"], arim, start=True, stop=False)
            return t.matmul(PS[1][0:64, 0:64], ct["W64im"], arre, start=False, stop=True)
        pr.seq("t", f, "kmm1")
        pr.wait("v", "kmm1")
        gre_s = wkK[0:64, 0:64]; gim_s = wkK[0:64, 64:128]
        hre = sk["hre"]; him = sk["him"]
        V.tensor_copy(gre_s, PS[0][0:64, 0:64])
        V.tensor_copy(gim_s, PS[1][0:64, 0:64])
        V.tensor_tensor(hre, gre_s, ct["Twre"], OP.mult)
        V.tensor_tensor(w5, gim_s, ct["Twim"], OP.mult)
        V.tensor_tensor(hre, hre, w5, OP.subtract)
        V.tensor_tensor(him, gre_s, ct["Twim"], OP.mult)
        V.tensor_tensor(w5, gim_s, ct["Twre"], OP.mult)
        kh = V.tensor_tensor(him, him, w5, OP.add)
        pr.wait("t", "v_self", kh)
        def f(t):
            t.transpose(PS[2][0:64, 0:64], sk["hre"], ct["identf"][0:64, 0:64])
            return t.transpose(PS[3][0:64, 0:64], sk["him"], ct["identf"][0:64, 0:64])
        pr.seq("t", f, "kt1")
        pr.wait("v", "kt1")
        V.tensor_copy(sk["hreT"], PS[2][0:64, 0:64])
        V.tensor_copy(sk["himT"], PS[3][0:64, 0:64])
        kt2 = V.tensor_scalar(sk["himT"], sk["himT"], -1.0, None, OP.mult)
        pr.wait("t", "v_self", kt2)
        def f(t):
            t.matmul(PS[4][0:64, 0:64], ct["W64bre"], sk["hreT"], start=True, stop=False)
            return t.matmul(PS[4][0:64, 0:64], ct["W64bim"], sk["himT"], start=False, stop=True)
        pr.seq("t", f, "kmm2")
        pr.wait("v", "kmm2")
        V.tensor_copy(sk["k2d"][:, :], PS[4][0:64, 0:64])
        kk2 = V.memset(wkK[0:1, 0:129], 0.0)
        pr.wait("g", "v_self", kk2)
        pr.dma("g", Kpad.ap()[0:1, 0:127], wkK[0:1, 0:127], "kpd")
        pr.dma("g", Kpad.ap()[0:1, 4223:4352], wkK[0:1, 0:129], "kpd")
        pr.dma("g", bass.AP(Kpad, 127, [[64, 64], [1, 64]]), sk["k2d"][:, :], "kpd")
        pr.wait("g", "kpd")
        if "Kout" in dbg_outs:
            pr.dma("g", dbg_outs["Kout"].ap(), Kpad.ap(), "kdbg")
            pr.dma("g", dbg_outs["Kacc"].ap(), sk["acc"][:, :], "kdbg")
            pr.dma("g", dbg_outs["Kar"].ap()[:, 0:64], arre, "kdbg")
            pr.dma("g", dbg_outs["Kar"].ap()[:, 64:128], arim, "kdbg")
            pr.dma("g", dbg_outs["Krows"].ap(), sk["rows"][0:1, :], "kdbg")
            pr.wait("g", "kdbg")

    setup()
    with nc.sbuf_tensor("skA", [1, 128], F32) as _p, \
         nc.sbuf_tensor("skB", [1, 128], F32) as _q, \
         nc.sbuf_tensor("skC", [64, 64], F32) as _vr, \
         nc.sbuf_tensor("skD", [64, 64], F32) as _vi, \
         nc.sbuf_tensor("skE", [1, 128], F32) as _ct, \
         nc.sbuf_tensor("skF", [64, 1], F32) as _bp, \
         nc.sbuf_tensor("skG", [1, 4], F32) as _dl, \
         nc.sbuf_tensor("skGl", [1, 192], F32) as _lsr, \
         nc.sbuf_tensor("skH", [1, 642], F32) as _rows, \
         nc.sbuf_tensor("skI", [64, 128], F32) as _lamb, \
         nc.sbuf_tensor("skJ", [64, 512], F32) as _v4b, \
         nc.sbuf_tensor("skK", [64, 512], F32) as _acc, \
         nc.sbuf_tensor("skL", [64, 192], F32) as _ar, \
         nc.sbuf_tensor("skM", [64, 128], F32) as _h, \
         nc.sbuf_tensor("skN", [64, 128], F32) as _hT, \
         nc.sbuf_tensor("skO", [64, 64], F32) as _k2d, \
         nc.sbuf_tensor("skW", [128, 5248], F32) as _wkK:
        sk = dict(p=_p, q=_q, vctre=_vr, vctim=_vi, ct=_ct, bp=_bp, dl=_dl, lsr=_lsr, wkK=_wkK,
                  rows=_rows, lamb=_lamb, v4b=_v4b, acc=_acc,
                  arre=_ar[:, 0:64], arim=_ar[:, 64:128], arimn=_ar[:, 128:192],
                  hre=_h[:, 0:64], him=_h[:, 64:128],
                  hreT=_hT[:, 0:64], himT=_hT[:, 64:128], k2d=_k2d)
        phaseK(sk)


    def phaseA(sa):
        V = pr.chained("v")
        Cc = pr.chained("c")
        xb = sa["xb"]; kvT = sa["kvT"]; wcol = sa["wcol"]; qtmp = sa["qtmp"]
        kst = sa["kst"]; kbart = sa["kbart"]
        ropekt = sa["ropekt"]; vnat = sa["vnat"]; ebt = sa["ebt"]
        haloT = sa["haloT"]; vhalo = sa["vhalo"]; rkhalo = sa["rkhalo"]
        Sst = sa["Sst"]; Zrun = sa["Zrun"]; Drun = sa["Drun"]
        crow = sa["crow"]; rkcols = sa["rkcols"]
        ebendF = sa["ebendF"]; wkF = sa["wkF"]
        cosb = sa["cosb"]; sinb = sa["sinb"]

        pr.dma("g", cosb[:, :], cosTt.ap(), "ctb")
        pr.dma("g", sinb[:, :], sinTt.ap(), "ctb")
        V.memset(Zrun[:, :], 0.0)
        V.memset(Drun[:, :], 1.0)
        V.memset(Sst[:, :], 0.0)
        pr.wait("v", "ctb")

        def proj_col(wsrc, coff, b, n_tok, xsrc, nm):
            pr.wait("g", nm)
            pr.dma("g", ap3(wcol[:, :], b * 4096, [[128, 32], [1, 128]]),
                   bass.AP(wsrc, coff, [[wsrc.shape[1], 128], [128 * wsrc.shape[1], 32], [1, 128]]),
                   "wcA")
            pr.wait("t", "wcA")
            def f(t, b=b, n_tok=n_tok):
                last = None
                for k in range(32):
                    last = t.matmul(PS[b][:, 0:n_tok],
                                    wcol[:, b * 4096 + k * 128:b * 4096 + (k + 1) * 128],
                                    xsrc[:, k * n_tok:(k + 1) * n_tok],
                                    start=(k == 0), stop=(k == 31))
                return last
            return pr.seq("t", f, nm)

        for H in range(2):
            t0 = H * 512
            if H == 1:
                pr.wait("g", "qmm")
            pr.dma("g", ap3(xb[:, :], 0, [[512, 32], [1, 512]]),
                   bass.AP(xT, t0, [[NT, 128], [128 * NT, 32], [1, 512]]), "xbA")
            if H == 0:
                pr.dma("g", ap3(sa["xbh"][:, :], 0, [[64, 32], [1, 64]]),
                       bass.AP(xTh, 0, [[64, 128], [128 * 64, 32], [1, 64]]), "xbA")
            pr.wait("t", "xbA")
            for m in range(16):
                wsrc = Wk if m < 8 else Wv
                b = m % 2
                if m >= 2:
                    pr.wait("t", "v_self")
                proj_col(wsrc, (m % 8) * 128, b, 512, xb, "kvmm")
                pr.wait("v", "kvmm")
                V.tensor_copy(kvT[:, m * 512:(m + 1) * 512], PS[b][:, 0:512])
            if H == 0:
                for m in range(16):
                    wsrc = Wk if m < 8 else Wv
                    b = m % 2
                    if m >= 2:
                        pr.wait("t", "v_self")
                    pr.wait("g", "kvmm")
                    pr.dma("g", ap3(wcol[:, :], b * 4096, [[128, 32], [1, 128]]),
                           bass.AP(wsrc, (m % 8) * 128, [[1024, 128], [128 * 1024, 32], [1, 128]]), "wcA")
                    pr.wait("t", "wcA")
                    def f(t, b=b):
                        last = None
                        for k in range(32):
                            last = t.matmul(PS[2 + b][:, 0:64],
                                            wcol[:, b * 4096 + k * 128:b * 4096 + (k + 1) * 128],
                                            sa["xbh"][:, k * 64:(k + 1) * 64],
                                            start=(k == 0), stop=(k == 31))
                        return last
                    pr.seq("t", f, "kvmm")
                    pr.wait("v", "kvmm")
                    V.tensor_copy(haloT[:, m * 64:(m + 1) * 64], PS[2 + b][:, 0:64])
                for h in range(KVH):
                    pr.wait("t", "v_self")
                    pr.seq("t", lambda t, h=h: t.transpose(PSB[0:64, 0:128],
                                                           haloT[:, (8 + h) * 64:(9 + h) * 64],
                                                           ct["ident"]), "hvt")
                    pr.wait("v", "hvt")
                    V.tensor_copy(vhalo[0:64, h * 128:(h + 1) * 128], PSB[0:64, 0:128])
                    kTh = haloT[:, h * 64:(h + 1) * 64]
                    pr.wait("t", "v_self")
                    pr.seq("t", lambda t, kTh=kTh: t.matmul(PS[5][:, 0:64], ct["perm"], kTh,
                                                            start=True, stop=True), "hkp")
                    pr.wait("v", "hkp")
                    V.tensor_copy(wkD[:, 0:64], PS[5][:, 0:64])
                    V.tensor_tensor(wkD[:, 64:128], kTh, cosb[:, 0:64], OP.mult)
                    V.tensor_tensor(wkD[:, 0:64], wkD[:, 0:64], sinb[:, 0:64], OP.mult)
                    V.tensor_tensor(rkhalo[:, h * 64:(h + 1) * 64],
                                    wkD[:, 64:128], wkD[:, 0:64], OP.add)
            for m in range(32):
                b = m % 2
                if m >= 2:
                    pr.wait("t", "v_self")
                proj_col(Wq, m * 128, b, 512, xb, "qmm")
                pr.wait("v", "qmm")
                if H or m >= 2:
                    pr.wait("v", "qst%d" % b)
                qev = V.tensor_copy(qtmp[:, b * 512:(b + 1) * 512], PS[b][:, 0:512])
                pr.wait("s", "v_self", qev)
                pr.dma("s", bass.AP(qdr, m * 128 * NT + t0, [[NT, 128], [1, 512]]),
                       qtmp[:, b * 512:(b + 1) * 512], "qst%d" % b)
            for ci4 in range(4):
                ci = H * 4 + ci4
                for h in range(KVH):
                    kTh = kvT[:, h * 512 + ci4 * 128: h * 512 + (ci4 + 1) * 128]
                    vTh = kvT[:, (8 + h) * 512 + ci4 * 128: (8 + h) * 512 + (ci4 + 1) * 128]
                    pr.wait("t", "v_self")
                    pr.seq("t", lambda t, vTh=vTh: t.transpose(PSB[:, 0:128], vTh, ct["ident"]), "vtr")
                    pr.wait("v", "vtr")
                    V.tensor_copy(vnat[:, ci * 1024 + h * 128: ci * 1024 + (h + 1) * 128], PSB[:, 0:128])
                    lg = wkF[:, 0:128]; Lc = wkF[:, 128:256]; emb = wkF[:, 256:384]
                    pr.wait("c", "v_self")
                    e1c = Cc.activation(lg, kTh, AF.Exp, scale=cpool[:, CP_NEG1:CP_NEG1 + 1])
                    pr.wait("v", "c_self", e1c)
                    lnc = V.tensor_scalar(lg, lg, 1.0, None, OP.add)
                    pr.wait("c", "v_self", lnc)
                    lnd = Cc.activation(lg, lg, AF.Ln)
                    pr.wait("v", "c_self", lnd)
                    scn = V.tensor_tensor_scan(Lc, lg, lg, 0.0, OP.add, OP.bypass)
                    pr.wait("c", "v_self", scn)
                    ebc = Cc.activation(ebt[:, (ci * 8 + h) * 128:(ci * 8 + h + 1) * 128], Lc,
                                        AF.Exp, scale=cpool[:, CP_G16:CP_G16 + 1])
                    emc = Cc.activation(emb, Lc, AF.Exp, scale=cpool[:, CP_P16:CP_P16 + 1])
                    exk = wkD[:, 256:384]
                    pr.wait("c", "kcs")
                    if ci4 % 2 == 0 and h == 0 and (H or ci4):
                        pr.wait("v", "rkb")
                    xkc = Cc.activation(exk, kTh, AF.Exp)
                    pr.wait("t", "c_self", xkc)
                    pr.seq("t", lambda t, exk=exk: t.matmul(PS[5][0:1, 0:128], ct["ones_col"], exk,
                                                            start=True, stop=True), "kcs")
                    pr.wait("v", "kcs")
                    pr.wait("v", "c_self", emc)
                    V.tensor_copy(crow[0:1, ((ci4 % 2) * 8 + h) * 128:((ci4 % 2) * 8 + h + 1) * 128],
                                  PS[5][0:1, 0:128])
                    V.tensor_tensor(kst[:, (ci * 8 + h) * 128:(ci * 8 + h + 1) * 128], exk, emb, OP.mult)
                    pr.wait("t", "v_self")
                    pr.seq("t", lambda t, kTh=kTh: t.matmul(PS[5][:, 0:128], ct["perm"], kTh,
                                                            start=True, stop=True), "rkp")
                    pr.wait("v", "rkp")
                    V.tensor_copy(wkD[:, 0:128], PS[5][:, 0:128])
                    V.tensor_tensor(wkD[:, 128:256], kTh, cosb[:, 64 + ci * 128:64 + (ci + 1) * 128], OP.mult)
                    V.tensor_tensor(wkD[:, 0:128], wkD[:, 0:128], sinb[:, 64 + ci * 128:64 + (ci + 1) * 128], OP.mult)
                    V.tensor_tensor(ropekt[:, (ci * 8 + h) * 128:(ci * 8 + h + 1) * 128],
                                    wkD[:, 128:256], wkD[:, 0:128], OP.add)
                if ci4 % 2 == 1:
                    qtr = H * 2 + ci4 // 2
                    rr = pr.cnt["v_self"]
                    pr.wait("g", "v_self", rr)
                    pr.dma("g", rowsd.ap()[0:1, 0:2048], crow[0:1, 0:2048], "rkb")
                    pr.wait("g", "rkb")
                    pr.dma("g", ap3(sa["rkraw"][:, 0:16], 0, [[1, 16], [1, 1]]),
                           bass.AP(rowsd, 0, [[1, 128], [128, 16], [1, 1]]), "rkc",
                           allow_slow_non_contiguous=True)
                    pr.wait("v", "rkc")
                    V.reciprocal(rkcols[:, qtr * 16:(qtr + 1) * 16], sa["rkraw"][:, 0:16])
            V.tensor_copy(ebendF[:, H * 32:H * 32 + 32],
                          ap3(ebt[:, :], H * 4096 + 127, [[128, 32], [1, 1]]))
            for ci4 in range(4):
                ci = H * 4 + ci4
                for h in range(KVH):
                    g = ci * 8 + h
                    kb = wkD[:, 256:384]
                    pr.wait("v", "kbt")
                    V.tensor_scalar(kb, kst[:, g * 128:(g + 1) * 128],
                                    ebendF[:, g:g + 1], None, OP.mult)
                    kbv = pr.cnt["v_self"]
                    pr.wait("t", "v_self", kbv)
                    pr.seq("t", lambda t, kb=kb: t.transpose(PSB[:, 0:128], kb, ct["ident"]), "kbt")
                    pr.wait("v", "kbt")
                    V.tensor_scalar(kbart[:, ci * 1024 + h * 128: ci * 1024 + (h + 1) * 128],
                                    PSB[:, 0:128], rkcols[:, g:g + 1], None, OP.mult)
                    zn = pr.cnt["v_self"]
                    pr.wait("t", "v_self", zn)
                    def f(t, ci=ci, h=h):
                        return t.matmul(PS[6][:, 0:128],
                                        kbart[:, ci * 1024 + h * 128: ci * 1024 + (h + 1) * 128],
                                        vnat[:, ci * 1024 + h * 128: ci * 1024 + (h + 1) * 128],
                                        start=True, stop=True)
                    pr.seq("t", f, "zmm")
                    pr.wait("v", "zmm")
                    V.scalar_tensor_tensor(Zrun[:, h * 128:(h + 1) * 128],
                                           Zrun[:, h * 128:(h + 1) * 128],
                                           ebendF[:, g:g + 1],
                                           PS[6][:, 0:128], OP.mult, OP.add)
                    V.tensor_scalar(Drun[:, h:h + 1], Drun[:, h:h + 1],
                                    ebendF[:, g:g + 1], None, OP.mult)
        wr = pr.cnt["v_self"]
        pr.wait("g", "v_self", wr)
        for h in range(KVH):
            pr.dma("g", bass.AP(ag_s_in, h * 130 * 128, [[128, 128], [1, 128]]),
                   Zrun[:, h * 128:(h + 1) * 128], "agw")
            pr.dma("g", bass.AP(ag_s_in, h * 130 * 128 + 128 * 128, [[1, 128], [1, 1]]),
                   Drun[:, h:h + 1], "agw", allow_slow_non_contiguous=True)
        pr.wait("g", "agw")
        def f(g):
            return g.collective_compute(
                "AllGather", OP.bypass, replica_groups=[list(range(NC))],
                ins=[ag_s_in.ap().opt()], outs=[ag_s_out.ap().opt()])
        pr.seq("g", f, "agd")
        pr.wait("s", "agd")
        for j in range(NC):
            for h in range(KVH):
                zf = pr.dma("s", wkF[:, 0:128],
                            bass.AP(ag_s_out, (j * KVH + h) * 130 * 128, [[128, 128], [1, 128]]), "sfz")
                df = pr.dma("s", wkF[:, 128:129],
                            bass.AP(ag_s_out, (j * KVH + h) * 130 * 128 + 128 * 128, [[1, 128], [1, 1]]),
                            "sfd", allow_slow_non_contiguous=True)
                pr.wait("v", "sfz", zf)
                pr.wait("v", "sfd", df)
                V.tensor_scalar(wkF[:, 129:130], wkF[:, 128:129], scanm_t[:, j:j + 1], None, OP.mult)
                V.tensor_scalar(wkF[:, 129:130], wkF[:, 129:130], scanm_t[:, 8 + j:9 + j], None, OP.add)
                V.tensor_scalar(wkF[:, 0:128], wkF[:, 0:128], scanm_t[:, j:j + 1], None, OP.mult)
                fs = V.scalar_tensor_tensor(Sst[:, h * 128:(h + 1) * 128],
                                            Sst[:, h * 128:(h + 1) * 128],
                                            wkF[:, 129:130], wkF[:, 0:128], OP.mult, OP.add)
                pr.wait("s", "v_self", fs)


    def phaseA2(sa):
        V = pr.chained("v")
        Cc = pr.chained("c")
        kst = sa["kst"]; kbart = sa["kbart"]; ropekt = sa["ropekt"]
        vnat = sa["vnat"]; ebt = sa["ebt"]; vhalo = sa["vhalo"]; rkhalo = sa["rkhalo"]
        Sst = sa["Sst"]; rkcols = sa["rkcols"]; ebendF = sa["ebendF"]
        wkF = sa["wkF"]; cosb = sa["cosb"]; sinb = sa["sinb"]
        qch = sa["qch"]; oT = sa["oT"]; Scast = sa["Scast"]
        pr.wait("g", "qst0")
        pr.wait("g", "qst1")
        for ci in range(NCH):
            qs_ = (ci % 2) * 4096
            if ci >= 2:
                pr.wait("g", "a2rp")
                pr.wait("g", "c_self")
            pr.dma("g", ap3(qch[:, :], qs_, [[128, 32], [1, 128]]),
                   bass.AP(qdr, ci * 128, [[NT, 128], [128 * NT, 32], [1, 128]]), "a2q%d" % (ci % 2))
            pr.wait("v", "a2o")
            for h in range(KVH):
                g = ci * 8 + h
                pr.wait("v", "a2ol")
                sc_ = V.tensor_copy(Scast[:, :], Sst[:, h * 128:(h + 1) * 128])
                pr.wait("t", "v_self", sc_)
                def f(t, ci=ci, h=h):
                    return t.matmul(PS[0][:, 0:128],
                                    kbart[:, ci * 1024 + h * 128:ci * 1024 + (h + 1) * 128],
                                    vnat[:, ci * 1024 + h * 128:ci * 1024 + (h + 1) * 128],
                                    start=True, stop=True)
                pr.seq("t", f, "a2z")
                pr.wait("v", "a2z")
                V.scalar_tensor_tensor(Sst[:, h * 128:(h + 1) * 128],
                                       Sst[:, h * 128:(h + 1) * 128],
                                       ebendF[:, g:g + 1], PS[0][:, 0:128], OP.mult, OP.add)
                for qi in range(GRP):
                    hq = h * GRP + qi
                    qTh = qch[:, qs_ + hq * 128:qs_ + (hq + 1) * 128]
                    exq = wkD[:, 0:128]
                    rqt = wkD[:, 128:256]
                    qst_ = wkD[:, 256:384]
                    Am = wkD[:, 384:512]
                    e1 = wkD[:, 512:640]
                    e2 = wkD[:, 640:768]
                    rqb = wkD[:, 768:896]
                    pr.wait("c", "a2q%d" % (ci % 2))
                    pr.wait("c", "v_self")
                    xqc = Cc.activation(exq, qTh, AF.Exp)
                    pr.wait("t", "c_self", xqc)
                    pr.wait("t", "v_self")
                    pr.seq("t", lambda t, exq=exq: t.matmul(PS[4][0:1, 0:128], ct["ones_col"], exq,
                                                            start=True, stop=True), "a2cq")
                    pr.wait("v", "a2cq")
                    V.reciprocal(wkF[0:1, 512:640], PS[4][0:1, 0:128])
                    rqc = V.tensor_copy(rqb[0:1, 0:128], wkF[0:1, 512:640])
                    pr.wait("t", "v_self", rqc)
                    pr.seq("t", lambda t, rqb=rqb: t.matmul(PS[5][:, 0:128], ct["ones_row"], rqb[0:1, 0:128],
                                                            start=True, stop=True), "a2xq")
                    pr.seq("t", lambda t, qTh=qTh: t.matmul(PS[2][:, 0:128], ct["perm"], qTh,
                                                            start=True, stop=True), "a2rp")
                    pr.wait("v", "a2rp")
                    V.tensor_copy(rqt, PS[2][:, 0:128])
                    V.tensor_tensor(qst_, qTh, cosb[:, 64 + ci * 128:64 + (ci + 1) * 128], OP.mult)
                    V.tensor_tensor(rqt, rqt, sinb[:, 64 + ci * 128:64 + (ci + 1) * 128], OP.mult)
                    V.tensor_tensor(rqt, qst_, rqt, OP.add)
                    qsv = V.tensor_tensor(qst_, exq, ebt[:, g * 128:(g + 1) * 128], OP.mult)
                    pr.wait("t", "v_self", qsv)
                    pr.seq("t", lambda t, g=g, qst_=qst_: t.matmul(PS[0][:, 0:128],
                                                                   kst[:, g * 128:(g + 1) * 128], qst_,
                                                                   start=True, stop=True), "a2at")
                    pr.wait("v", "a2at")
                    amv = V.scalar_tensor_tensor(Am, PS[0][:, 0:128], rkcols[:, g:g + 1],
                                                 ct["triu_ji"], OP.mult, OP.mult)
                    pr.wait("t", "v_self", amv)
                    def f(t, ci=ci, h=h, Am=Am, qst_=qst_):
                        t.matmul(PS[1][:, 0:128],
                                 vnat[:, ci * 1024 + h * 128:ci * 1024 + (h + 1) * 128], Am,
                                 start=True, stop=False)
                        return t.matmul(PS[1][:, 0:128], Scast[:, :], qst_, start=False, stop=True)
                    pr.seq("t", f, "a2ol")
                    if ci == 0:
                        pk = rkhalo[:, h * 64:(h + 1) * 64]
                        pv_ = vhalo[0:64, h * 128:(h + 1) * 128]
                        msk1 = m1c0_t
                        W1 = 64
                    else:
                        pk = ropekt[:, ((ci - 1) * 8 + h) * 128:((ci - 1) * 8 + h + 1) * 128]
                        pv_ = vnat[:, (ci - 1) * 1024 + h * 128:(ci - 1) * 1024 + (h + 1) * 128]
                        msk1 = ct["mask1e"]
                        W1 = 128
                    def f(t, pk=pk, rqt=rqt, W1=W1):
                        return t.matmul(PS[2][0:W1, 0:128], pk, rqt, start=True, stop=True)
                    pr.seq("t", f, "a2s1")
                    def f(t, g=g, rqt=rqt):
                        return t.matmul(PS[3][:, 0:128],
                                        ropekt[:, g * 128:(g + 1) * 128], rqt, start=True, stop=True)
                    pr.seq("t", f, "a2s2")
                    pr.wait("v", "a2s1")
                    pr.wait("v", "a2s2")
                    m1v = V.tensor_tensor(wkF[0:W1, 0:128], PS[2][0:W1, 0:128],
                                          msk1[0:W1, :] if W1 == 64 else msk1, OP.add)
                    m2v = V.tensor_tensor(wkF[:, 128:256], PS[3][:, 0:128], ct["mask2"], OP.add)
                    pr.wait("c", "v_self", m2v)
                    ex1 = Cc.activation(e1[0:W1, 0:128], wkF[0:W1, 0:128], AF.Exp,
                                        scale=cpool[0:W1, CP_ISQD:CP_ISQD + 1])
                    ex2 = Cc.activation(e2, wkF[:, 128:256], AF.Exp,
                                        scale=cpool[:, CP_ISQD:CP_ISQD + 1])
                    pr.wait("t", "c_self", ex2)
                    pr.wait("t", "v_self")
                    def f(t, e1=e1, e2=e2, W1=W1):
                        t.matmul(PS[4][0:1, 0:128], ct["ones_col"][0:W1, :], e1[0:W1, 0:128],
                                 start=True, stop=False)
                        return t.matmul(PS[4][0:1, 0:128], ct["ones_col"], e2, start=False, stop=True)
                    pr.seq("t", f, "a2cs")
                    def f(t, e1=e1, e2=e2, ci=ci, h=h, pv_=pv_, W1=W1):
                        t.matmul(PS[6][:, 0:128], pv_, e1[0:W1, 0:128], start=True, stop=False)
                        return t.matmul(PS[6][:, 0:128],
                                        vnat[:, ci * 1024 + h * 128:ci * 1024 + (h + 1) * 128], e2,
                                        start=False, stop=True)
                    pr.seq("t", f, "a2pv")
                    pr.wait("v", "a2cs")
                    V.reciprocal(wkF[0:1, 512:640], PS[4][0:1, 0:128])
                    rsc = V.tensor_copy(rqb[0:1, 0:128], wkF[0:1, 512:640])
                    pr.wait("t", "v_self", rsc)
                    pr.seq("t", lambda t, rqb=rqb: t.matmul(PS[4][:, 0:128], ct["ones_row"],
                                                            rqb[0:1, 0:128], start=True, stop=True), "a2xs")
                    pr.wait("v", "a2xs")
                    pr.wait("v", "a2xq")
                    pr.wait("v", "a2pv")
                    pr.wait("v", "a2ol")
                    V.tensor_copy(wkF[:, 128:256], PS[5][:, 0:128])
                    V.tensor_copy(wkF[:, 256:384], PS[4][:, 0:128])
                    V.tensor_tensor(wkF[:, 384:512], PS[6][:, 0:128], wkF[:, 256:384], OP.mult)
                    V.tensor_tensor(wkF[:, 128:256], PS[1][:, 0:128], wkF[:, 128:256], OP.mult)
                    V.tensor_tensor(wkF[:, 128:256], wkF[:, 128:256], wkF[:, 384:512], OP.add)
                    V.tensor_scalar(oT[:, hq * 128:(hq + 1) * 128], wkF[:, 128:256],
                                    cpool[:, CP_HALF:CP_HALF + 1], None, OP.mult)
            ov = pr.cnt["v_self"]
            pr.wait("s", "v_self", ov)
            pr.dma("s", bass.AP(odr, ci * 128, [[NT, 128], [128 * NT, 32], [1, 128]]),
                   ap3(oT[:, :], 0, [[128, 32], [1, 128]]), "a2o")
        pr.wait("s", "a2o")

    def phaseWo(sw):
        V = pr.chained("v")
        oTf = sw["oTf"]; wcol = sw["wcol"]; aev = sw["aev"]
        pr.dma("g", ap3(oTf[:, :], 0, [[1024, 32], [1, 1024]]),
               bass.AP(odr, 0, [[NT, 128], [128 * NT, 32], [1, 1024]]), "woo")
        pr.wait("t", "woo")
        for m in range(32):
            b = m % 2
            if m >= 2:
                pr.wait("g", "womm")
                pr.wait("t", "v_self")
            pr.dma("g", ap3(wcol[:, :], b * 4096, [[128, 32], [1, 128]]),
                   bass.AP(Wo, m * 128, [[D, 128], [128 * D, 32], [1, 128]]), "wcW")
            pr.wait("t", "wcW")
            def f(t, b=b):
                last = None
                for nn in range(2):
                    for k in range(32):
                        last = t.matmul(PS[2 * b + nn][:, 0:512],
                                        wcol[:, b * 4096 + k * 128:b * 4096 + (k + 1) * 128],
                                        oTf[:, k * 1024 + nn * 512:k * 1024 + (nn + 1) * 512],
                                        start=(k == 0), stop=(k == 31))
                return last
            pr.seq("t", f, "womm")
            pr.wait("v", "womm")
            if m >= 2:
                pr.wait("v", "wost%d" % b)
            V.tensor_copy(aev[:, (b * 2) * 512:(b * 2 + 1) * 512], PS[2 * b][:, 0:512])
            wov = V.tensor_copy(aev[:, (b * 2 + 1) * 512:(b * 2 + 2) * 512], PS[2 * b + 1][:, 0:512])
            pr.wait("s", "v_self", wov)
            pr.dma("s", bass.AP(attnTd, m * 128 * NT, [[NT, 128], [1, 512]]),
                   aev[:, (b * 2) * 512:(b * 2 + 1) * 512], "wost%d" % b)
            pr.dma("s", bass.AP(attnTd, m * 128 * NT + 512, [[NT, 128], [1, 512]]),
                   aev[:, (b * 2 + 1) * 512:(b * 2 + 2) * 512], "wost%d" % b)
        pr.wait("s", "wost0")
        pr.wait("s", "wost1")

    def emit_ln(resA, V, Cc, tag):
        pr.wait("c", "v_self")
        for k in range(32):
            if k >= 2:
                pr.wait("c", tag + "st")
            xc = Cc.activation(wkD[:, (k % 2) * 512:(k % 2 + 1) * 512],
                               resA[:, k * 512:(k + 1) * 512], AF.Copy)
            sq = Cc.activation(wkD[:, 1024 + (k % 2) * 512:1024 + (k % 2 + 1) * 512],
                               resA[:, k * 512:(k + 1) * 512], AF.Square)
            pr.wait("t", "c_self", sq)
            def f(t, k=k):
                t.matmul(PS[5][0:1, 0:512], ct["ones_col"],
                         wkD[:, (k % 2) * 512:(k % 2 + 1) * 512], start=(k == 0), stop=(k == 31))
                return t.matmul(PS[6][0:1, 0:512], ct["ones_col"],
                                wkD[:, 1024 + (k % 2) * 512:1024 + (k % 2 + 1) * 512],
                                start=(k == 0), stop=(k == 31))
            pr.seq("t", f, tag + "st")
        pr.wait("v", tag + "st")
        s0 = global_stat[0][0:1, :]
        cp0 = cpool[0:1, :]
        V.tensor_scalar(s0[0:1, 0:512], PS[5][0:1, 0:512], cp0[0:1, CP_INVD:CP_INVD + 1], None, OP.mult)
        V.tensor_scalar(s0[0:1, 512:1024], PS[6][0:1, 0:512], cp0[0:1, CP_INVD:CP_INVD + 1], None, OP.mult)
        V.tensor_tensor(s0[0:1, 1536:2048], s0[0:1, 0:512], s0[0:1, 0:512], OP.mult)
        V.tensor_tensor(s0[0:1, 1024:1536], s0[0:1, 512:1024], s0[0:1, 1536:2048], OP.subtract)
        V.tensor_scalar(s0[0:1, 1024:1536], s0[0:1, 1024:1536], cp0[0:1, CP_EPS:CP_EPS + 1], None, OP.add)
        rc = V.reciprocal(s0[0:1, 1536:2048], s0[0:1, 1024:1536])
        pr.wait("c", "v_self", rc)
        sqc = Cc.activation(s0[0:1, 2048:2560], s0[0:1, 1536:2048], AF.Sqrt)
        pr.wait("v", "c_self", sqc)
        V.tensor_tensor(s0[0:1, 1536:2048], s0[0:1, 2048:2560], s0[0:1, 2048:2560], OP.mult)
        V.tensor_tensor(s0[0:1, 1536:2048], s0[0:1, 1536:2048], s0[0:1, 1024:1536], OP.mult)
        V.tensor_scalar(s0[0:1, 1536:2048], s0[0:1, 1536:2048], cp0[0:1, CP_NHALF:CP_NHALF + 1],
                        cp0[0:1, CP_1P5:CP_1P5 + 1], OP.mult, OP.add)
        V.tensor_tensor(s0[0:1, 2048:2560], s0[0:1, 1536:2048], s0[0:1, 2048:2560], OP.mult)
        bv = V.tensor_tensor(s0[0:1, 2560:3072], s0[0:1, 2048:2560], s0[0:1, 0:512], OP.mult)
        pr.wait("g", "v_self", bv)
        pr.dma("g", rowsd.ap()[2:3, 0:512], s0[0:1, 2048:2560], tag + "b1")
        pr.dma("g", rowsd.ap()[3:4, 0:512], s0[0:1, 2560:3072], tag + "b1")
        pr.wait("g", tag + "b1")
        statB = global_statB[0]
        pr.dma("g", statB[:, 0:512], bass.AP(rowsd, 2 * 8192, [[0, 128], [1, 512]]), tag + "b2")
        pr.dma("g", statB[:, 512:1024], bass.AP(rowsd, 3 * 8192, [[0, 128], [1, 512]]), tag + "b2")
        pr.wait("v", tag + "b2")
        for k in range(32):
            V.tensor_tensor(resA[:, k * 512:(k + 1) * 512], resA[:, k * 512:(k + 1) * 512],
                            statB[:, 0:512], OP.mult)
            V.tensor_tensor(resA[:, k * 512:(k + 1) * 512], resA[:, k * 512:(k + 1) * 512],
                            statB[:, 512:1024], OP.subtract)

    def phaseA3(sa3):
        V = pr.chained("v")
        Cc = pr.chained("c")
        resA = sa3["resA"]; xroll = sa3["xroll"]
        for H in range(2):
            t0 = H * 512
            if H == 1:
                pr.wait("s", "a3o")
            pr.dma("s", ap3(resA[:, :], 0, [[512, 32], [1, 512]]),
                   bass.AP(attnTd, t0, [[NT, 128], [128 * NT, 32], [1, 512]]), "a3a")
            pr.wait("v", "a3a")
            for k in range(32):
                b = k % 2
                if k >= 2:
                    pr.wait("s", "v_self")
                pr.dma("s", xroll[:, b * 512:(b + 1) * 512],
                       bass.AP(xT, k * 128 * NT + t0, [[NT, 128], [1, 512]]), "a3x%d" % b)
                pr.wait("v", "a3x%d" % b)
                V.tensor_tensor(resA[:, k * 512:(k + 1) * 512], resA[:, k * 512:(k + 1) * 512],
                                xroll[:, b * 512:(b + 1) * 512], OP.add)
            if not debug.get("noln"):
                emit_ln(resA, V, Cc, "l1h%d" % H)
            lv = pr.cnt["v_self"]
            pr.wait("s", "v_self", lv)
            pr.dma("s", bass.AP(x1d, t0, [[NT, 128], [128 * NT, 32], [1, 512]]),
                   ap3(resA[:, :], 0, [[512, 32], [1, 512]]), "a3o")
        pr.wait("s", "a3o")
        if "x1out" in dbg_outs:
            pr.wait("g", "a3o")
            pr.dma("g", dbg_outs["x1out"].ap(), x1d.ap(), "a3d")
            if "adiag2" in dbg_outs:
                pr.dma("g", dbg_outs["adiag2"].ap()[0:1, :], global_stat[0][0:1, :], "a3d")
                pr.dma("g", dbg_outs["adiag2"].ap()[1:3, 0:1024],
                       global_statB[0][0:2, 0:1024], "a3d")
            pr.wait("g", "a3d")
        if "qout" in dbg_outs:
            pr.dma("g", dbg_outs["qout"].ap(), qdr.ap(), "a3d")
            pr.dma("g", dbg_outs["oout"].ap(), odr.ap(), "a3d")
            pr.dma("g", dbg_outs["attnout"].ap(), attnTd.ap(), "a3d")
            pr.wait("g", "a3d")

    def phaseB(sb_):
        V = pr.chained("v")
        Cc = pr.chained("c")
        resB = sb_["resB"]; xbB = sb_["xbB"]; AfB = sb_["AfB"]; BwB = sb_["BwB"]
        rtT = sb_["rtT"]; lg8 = sb_["lg8"]; lgT = sb_["lgT"]; eqT = sb_["eqT"]
        mxc = sb_["mxc"]; rw8 = sb_["rw8"]; rwXs = sb_["rwXs"]; whB = sb_["whB"]
        pr.dma("g", ap3(AfB[:, :], 0, [[128, 32], [1, 128]]),
               bass.AP(Aflat, 0, [[128, 128], [128 * 128, 32], [1, 128]]), "bwl")
        pr.dma("g", ap3(BwB[:, :], 0, [[128, 32], [1, 128]]),
               bass.AP(BwT, 0, [[D, 128], [128, 32], [1, 128]]), "bwl")
        pr.dma("s", ap3(rtT[:, :], 0, [[8, 32], [1, 8]]),
               bass.AP(routeT, 0, [[8, 128], [8 * 128, 32], [1, 8]]), "brt")
        pr.wait("t", "bwl")
        pr.wait("t", "brt")
        for H in range(2):
            t0 = H * 512
            if H == 1:
                pr.wait("s", "bxo")
                pr.wait("g", "bmmh")
                pr.wait("t", "v_self")
            pr.dma("s", ap3(resB[:, :], 0, [[512, 32], [1, 512]]),
                   bass.AP(x1d, t0, [[NT, 128], [128 * NT, 32], [1, 512]]), "bra")
            pr.dma("g", ap3(xbB[:, :], 0, [[512, 32], [1, 512]]),
                   bass.AP(x1d, t0, [[NT, 128], [128 * NT, 32], [1, 512]]), "bxb")
            pr.wait("t", "bra")
            pr.wait("t", "bxb")
            def f(t):
                last = None
                for k in range(32):
                    last = t.matmul(PS[0][0:8, 0:512], rtT[:, k * 8:(k + 1) * 8],
                                    resB[:, k * 512:(k + 1) * 512],
                                    start=(k == 0), stop=(k == 31))
                return last
            pr.seq("t", f, "blg")
            def f(t):
                last = None
                for k in range(32):
                    last = t.matmul(PS[3][:, 0:512], AfB[:, k * 128:(k + 1) * 128],
                                    xbB[:, k * 512:(k + 1) * 512],
                                    start=(k == 0), stop=(k == 31))
                return last
            pr.seq("t", f, "bmmh")
            pr.wait("v", "blg")
            lv8 = V.tensor_copy(lg8[0:8, 0:512], PS[0][0:8, 0:512])
            pr.wait("t", "v_self", lv8)
            def f(t):
                last = None
                for j in range(4):
                    last = t.transpose(PS[1][:, j * 8:(j + 1) * 8],
                                       lg8[0:8, j * 128:(j + 1) * 128], ct["identf"][0:8, 0:8])
                return last
            pr.seq("t", f, "btr1")
            pr.wait("v", "btr1")
            V.tensor_copy(lgT[:, 0:32], PS[1][:, 0:32])
            V.reduce_max(mxc[:, 0:4], ap3(lgT[:, 0:32], 0, [[8, 4], [1, 8]]), axis=AX.X)
            for j in range(4):
                V.tensor_scalar(eqT[:, j * 8:(j + 1) * 8], lgT[:, j * 8:(j + 1) * 8],
                                mxc[:, j:j + 1], None, OP.is_equal)
                V.tensor_scalar(eqT[:, j * 8:(j + 1) * 8], eqT[:, j * 8:(j + 1) * 8],
                                mxc[:, j:j + 1], None, OP.mult)
            eqv = pr.cnt["v_self"]
            pr.wait("t", "v_self", eqv)
            def f(t):
                last = None
                for j in range(4):
                    last = t.transpose(PS[2][0:8, j * 128:(j + 1) * 128],
                                       eqT[:, j * 8:(j + 1) * 8], ct["identf"])
                return last
            pr.seq("t", f, "btr2")
            pr.wait("v", "btr2")
            rwv = V.tensor_copy(rw8[0:8, 0:512], PS[2][0:8, 0:512])
            if "rwout" in dbg_outs:
                pr.wait("g", "v_self", rwv)
                pr.dma("g", bass.AP(dbg_outs["rwout"], t0, [[NT, 8], [1, 512]]),
                       rw8[0:8, 0:512], "bdg")
            pr.wait("t", "v_self", rwv)
            pr.seq("t", lambda t: t.matmul(PS[4][:, 0:512], ct["blocksel"], rw8[0:8, 0:512],
                                           start=True, stop=True), "brwx")
            pr.wait("v", "brwx")
            V.tensor_copy(rwXs[:, 0:512], PS[4][:, 0:512])
            pr.wait("v", "bmmh")
            whv = V.tensor_tensor(whB[:, 0:512], PS[3][:, 0:512], rwXs[:, 0:512], OP.mult)
            pr.wait("t", "v_self", whv)
            addc = {}
            for m in range(32):
                b = m % 2
                if m >= 2:
                    pr.wait("t", "v_self", addc[m - 2])
                def f(t, m=m, b=b):
                    return t.matmul(PS[b][:, 0:512], BwB[:, m * 128:(m + 1) * 128],
                                    whB[:, 0:512], start=True, stop=True)
                pr.seq("t", f, "bout")
                pr.wait("v", "bout")
                addc[m] = V.tensor_tensor(resB[:, m * 512:(m + 1) * 512],
                                          resB[:, m * 512:(m + 1) * 512],
                                          PS[b][:, 0:512], OP.add)
            emit_ln(resB, V, Cc, "l2h%d" % H)
            lv = pr.cnt["v_self"]
            pr.wait("s", "v_self", lv)
            pr.dma("s", bass.AP(x2d, t0, [[NT, 128], [128 * NT, 32], [1, 512]]),
                   ap3(resB[:, :], 0, [[512, 32], [1, 512]]), "bxo")
        pr.wait("s", "bxo")
        if "x2out" in dbg_outs:
            pr.wait("g", "bxo")
            pr.dma("g", dbg_outs["x2out"].ap(), x2d.ap(), "bdg")
            pr.wait("g", "bdg")

    def phaseC(sc_):
        V = pr.chained("v")
        Cc = pr.chained("c")
        resC = sc_["resC"]; xcB = sc_["xcB"]; TbB = sc_["TbB"]; HB = sc_["HB"]
        pr.dma("g", ap3(HB[:, :], 0, [[128, 32], [1, 128]]),
               bass.AP(Kpad, 0, [[1, 128], [128, 32], [1, 128]]), "chb")
        pr.wait("t", "chb")
        for j in range(8):
            b = j % 2
            if j >= 2:
                pr.wait("t", "v_self")
            pr.seq("t", lambda t, j=j, b=b: t.matmul(PS[2 + 2 * b][:, 0:512], ct["rev"],
                                                     HB[:, j * 512:(j + 1) * 512],
                                                     start=True, stop=True), "crev")
            pr.wait("v", "crev")
            V.tensor_copy(TbB[:, j * 512:(j + 1) * 512], PS[2 + 2 * b][:, 0:512])
        for H in range(2):
            t0 = H * 512
            if H == 1:
                pr.wait("s", "cxo")
                pr.wait("g", "cmm")
            pr.dma("s", ap3(resC[:, :], 0, [[512, 32], [1, 512]]),
                   bass.AP(x2d, t0, [[NT, 128], [128 * NT, 32], [1, 512]]), "cra")
            pr.dma("g", ap3(xcB[:, :], 0, [[512, 32], [1, 512]]),
                   bass.AP(x2d, t0, [[NT, 128], [128 * NT, 32], [1, 512]]), "cxb")
            pr.wait("t", "cxb")
            pr.wait("t", "v_self")
            pr.wait("v", "cra")
            addc = {}
            for jd in range(32):
                b = jd % 2
                if jd >= 2:
                    pr.wait("t", "v_self", addc[jd - 2])
                def f(t, jd=jd, b=b):
                    last = None
                    for d in range(jd + 1):
                        id_ = jd - d
                        last = t.matmul(PS[b][:, 0:512], TbB[:, d * 128:(d + 1) * 128],
                                        xcB[:, id_ * 512:(id_ + 1) * 512],
                                        start=(d == 0), stop=(d == jd))
                    return last
                pr.seq("t", f, "cmm")
                pr.wait("v", "cmm")
                addc[jd] = V.scalar_tensor_tensor(resC[:, jd * 512:(jd + 1) * 512],
                                                  resC[:, jd * 512:(jd + 1) * 512],
                                                  cpool[:, CP_D1:CP_D1 + 1],
                                                  PS[b][:, 0:512], OP.mult, OP.add)
            emit_ln(resC, V, Cc, "l3h%d" % H)
            lv = pr.cnt["v_self"]
            pr.wait("s", "v_self", lv)
            pr.dma("s", bass.AP(x3d, t0, [[NT, 128], [128 * NT, 32], [1, 512]]),
                   ap3(resC[:, :], 0, [[512, 32], [1, 512]]), "cxo")
        pr.wait("s", "cxo")
        if "x3out" in dbg_outs:
            pr.wait("g", "cxo")
            pr.dma("g", dbg_outs["x3out"].ap(), x3d.ap(), "cdg")
            pr.wait("g", "cdg")

    def mk_dq(sq_):
        V = pr.chained("v")
        Cc = pr.chained("c")
        qt = sq_["qt"]; msk = sq_["msk"]; acc = sq_["acc"]
        sct = sq_["sct"]; wout = sq_["wout"]
        jobs = [(w1qT, s1T, SH12, 32, agw_in[1]), (w2qT, s2T, SH12, 32, agw_in[2]),
                (w3qT, s3T, SH3, 86, agw_in[3])]
        tiles = [(qsrc, ssrc, ncols, kt, dst)
                 for (qsrc, ssrc, ncols, ntiles, dst) in jobs for kt in range(ntiles)]
        st = {"i": 0, "copyc": {}, "lastact": {}, "lastwv": {}}

        def emit_tile():
            i = st["i"]
            if i >= len(tiles):
                return False
            qsrc, ssrc, ncols, kt, dst = tiles[i]
            copyc, lastact, lastwv = st["copyc"], st["lastact"], st["lastwv"]
            sl = i % 2
            if i >= 2:
                pr.wait("g", "c_self", lastact[sl])
                pr.wait("g", "v_self", lastwv[sl])
            pr.dma("g", qt[:, sl * 1408:sl * 1408 + ncols],
                   bass.AP(qsrc, kt * 128 * ncols, [[ncols, 128], [1, ncols]]),
                   "qst%d" % sl)
            pr.dma("g", sct[:, sl * 1408:sl * 1408 + ncols],
                   bass.AP(ssrc, 2 * kt * ncols, [[ncols, 2], [0, 64], [1, ncols]]),
                   "a3x%d" % sl)
            pr.wait("v", "qst%d" % sl)
            V.memset(acc[:, 0:ncols], float(NF4_BF[0]))
            pr.wait("c", "qst%d" % sl)
            for v in range(1, 16):
                mb = v % 2
                gate = copyc.get((i, v - 2)) if v >= 3 else copyc.get((i - 1, 16 - v))
                if gate:
                    pr.wait("c", "v_self", gate)
                actc = Cc.activation(msk[:, mb * 1408:mb * 1408 + ncols],
                                     qt[:, sl * 1408:sl * 1408 + ncols],
                                     AF.Relu, bias=cpool[:, v:v + 1])
                pr.wait("v", "c_self", actc)
                copyc[(i, v)] = V.copy_predicated(
                    acc[:, 0:ncols], msk[:, mb * 1408:mb * 1408 + ncols],
                    tpool[:, v:v + 1].to_broadcast([128, ncols]))
            lastact[sl] = pr.cnt["c_self"]
            pr.wait("v", "a3x%d" % sl)
            pr.wait("v", "wost%d" % sl)
            lastwv[sl] = V.tensor_tensor(wout[:, sl * 1408:sl * 1408 + ncols],
                                         acc[:, 0:ncols],
                                         sct[:, sl * 1408:sl * 1408 + ncols], OP.mult)
            pr.wait("s", "v_self", lastwv[sl])
            pr.dma("s", bass.AP(dst, kt * 128 * ncols, [[ncols, 128], [1, ncols]]),
                   wout[:, sl * 1408:sl * 1408 + ncols], "wost%d" % sl)
            st["i"] = i + 1
            return True

        def emit_ag(wi):
            pr.wait("g", "wost0")
            pr.wait("g", "wost1")
            def f(g, wi=wi):
                return g.collective_compute(
                    "AllGather", OP.bypass, replica_groups=[list(range(NC))],
                    ins=[agw_in[wi].ap().opt()], outs=[agw_out[wi].ap().opt()])
            pr.seq("g", f, "agd")

        return emit_tile, emit_ag

    def phaseD1(sd_, dq=None):
        V = pr.chained("v")
        Cc = pr.chained("c")
        xdB = sd_["xdB"]; wcolD = sd_["wcolD"]; silB = sd_["silB"]; hv = sd_["hv"]
        agbase = pr.cnt["agd"]
        pr.dma("g", ap3(xdB[:, :], 0, [[1024, 32], [1, 1024]]),
               bass.AP(x3d, 0, [[NT, 128], [128 * NT, 32], [1, 1024]]), "woo")
        pr.wait("t", "woo")
        pr.wait("g", "agd", agbase)
        prevc = {}
        pr.sem("d1mm")
        d1mmb = pr.cnt["d1mm"]
        for tg in range(86):
            j, tt = tg // 11, tg % 11
            sl = tg % 2
            base = sl * 8192
            if dq:
                dq(tg)
            if tg >= 2:
                pr.wait("g", "d1mm", d1mmb + tg - 1)
            for wi in (1, 2):
                pr.dma("g", ap3(wcolD[:, :], base + (wi - 1) * 4096, [[128, 32], [1, 128]]),
                       bass.AP(agw_out[wi], j * D * SH12 + tt * 128,
                               [[SH12, 128], [128 * SH12, 32], [1, 128]]), "a2q%d" % sl)
            pr.wait("t", "a2q%d" % sl)
            if tg >= 1:
                pc_, pv_ = prevc[tg - 1]
                pr.wait("t", "c_self", pc_)
                pr.wait("t", "v_self", pv_)
            def f(t, base=base):
                last = None
                for wi in range(2):
                    for k in range(32):
                        for g2 in range(2):
                            last = t.matmul(PS[2 * wi + g2][:, 0:512],
                                            wcolD[:, base + wi * 4096 + k * 128:
                                                  base + wi * 4096 + (k + 1) * 128],
                                            xdB[:, k * 1024 + g2 * 512:k * 1024 + (g2 + 1) * 512],
                                            start=(k == 0), stop=(k == 31))
                return last
            pr.seq("t", f, "d1mm")
            pr.wait("c", "d1mm")
            if tg >= 2:
                pr.wait("c", "v_self", prevc[tg - 2][1])
            sc_ = None
            for g2 in range(2):
                sc_ = Cc.activation(silB[:, sl * 1024 + g2 * 512:sl * 1024 + (g2 + 1) * 512],
                                    PS[g2][:, 0:512], AF.Silu)
            pr.wait("v", "c_self", sc_)
            pr.wait("v", "bra" if sl == 0 else "cra")
            hvc = None
            for g2 in range(2):
                hvc = V.tensor_tensor(hv[:, sl * 1024 + g2 * 512:sl * 1024 + (g2 + 1) * 512],
                                      silB[:, sl * 1024 + g2 * 512:sl * 1024 + (g2 + 1) * 512],
                                      PS[2 + g2][:, 0:512], OP.mult)
            prevc[tg] = (sc_, hvc)
            pr.wait("s", "v_self", hvc)
            pr.dma("s", bass.AP(hdr, tg * 128 * NT, [[NT, 128], [1, 1024]]),
                   hv[:, sl * 1024:(sl + 1) * 1024], "bra" if sl == 0 else "cra")
        pr.wait("s", "bra")
        pr.wait("s", "cra")

    def phaseD2(sd_):
        V = pr.chained("v")
        Cc = pr.chained("c")
        hres = sd_["hres"]; resD = sd_["resD"]; w3col = sd_["w3col"]
        pr.wait("g", "bra")
        pr.wait("g", "cra")
        pr.wait("g", "agd")
        cmmb = {"v": 0}

        def em_load(s):
            m, hk = s // 2, s % 2
            j, dj = (m * 128) // 512, (m * 128) % 512
            nm = "kld" if hk == 0 else "kld2"
            if s >= 2:
                pr.wait("g", "cmm", cmmb["v"] + s - 1)
            pr.dma("g", ap3(w3col[:, :], hk * 5504, [[128, 43], [1, 128]]),
                   bass.AP(agw_out[3], j * HID * SH3 + dj + hk * 43 * 128 * SH3,
                           [[SH3, 128], [128 * SH3, 43], [1, 128]]), nm)

        for H in range(2):
            t0 = H * 512
            if H == 1:
                pr.wait("s", "bxo")
                pr.wait("g", "cmm")
                pr.wait("t", "v_self")
            pr.dma("g", ap3(hres[:, :], 0, [[512, 86], [1, 512]]),
                   bass.AP(hdr, t0, [[NT, 128], [128 * NT, 86], [1, 512]]), "cxb")
            pr.dma("s", ap3(resD[:, :], 0, [[512, 32], [1, 512]]),
                   bass.AP(x3d, t0, [[NT, 128], [128 * NT, 32], [1, 512]]), "bxb")
            pr.wait("t", "cxb")
            pr.wait("v", "bxb")
            cmmb["v"] = pr.cnt["cmm"]
            em_load(0)
            em_load(1)
            addc = {}
            for m in range(32):
                b = m % 2
                if m >= 2:
                    pr.wait("t", "v_self", addc[m - 2])
                for hk in range(2):
                    s = 2 * m + hk
                    pr.wait("t", "kld" if hk == 0 else "kld2")
                    def f(t, b=b, hk=hk):
                        last = None
                        for k in range(43):
                            last = t.matmul(PS[b][:, 0:512],
                                            w3col[:, hk * 5504 + k * 128:hk * 5504 + (k + 1) * 128],
                                            hres[:, (43 * hk + k) * 512:(43 * hk + k + 1) * 512],
                                            start=(hk == 0 and k == 0), stop=(hk == 1 and k == 42))
                        return last
                    pr.seq("t", f, "cmm")
                    if s + 2 < 64:
                        em_load(s + 2)
                pr.wait("v", "cmm")
                addc[m] = V.tensor_tensor(resD[:, m * 512:(m + 1) * 512],
                                          resD[:, m * 512:(m + 1) * 512],
                                          PS[b][:, 0:512], OP.add)
            emit_ln(resD, V, Cc, "l2h%d" % H)
            lv = pr.cnt["v_self"]
            pr.wait("s", "v_self", lv)
            pr.dma("s", bass.AP(out, t0, [[NT, 128], [128 * NT, 32], [1, 512]]),
                   ap3(resD[:, :], 0, [[512, 32], [1, 512]]), "bxo")
        pr.wait("s", "bxo")
        pr.wait("g", "bxo")
        pr.wait("t", "bxo")

    from contextlib import ExitStack
    if debug.get("upto", 99) >= 1:
        with ExitStack() as stk:
            mk = lambda n, s, dt: stk.enter_context(nc.sbuf_tensor(n, s, dt))
            sa = dict(
                kst=mk("pa_kst", [128, 8192], BF16),
                kbart=mk("pa_kbart", [128, 8192], BF16),
                ropekt=mk("pa_ropekt", [128, 8192], BF16),
                vnat=mk("pa_vnat", [128, 8192], BF16),
                ebt=mk("pa_ebt", [128, 8192], BF16),
                haloT=mk("pa_haloT", [128, 1024], BF16),
                vhalo=mk("pa_vhalo", [64, 1024], BF16),
                rkhalo=mk("pa_rkhalo", [128, 512], BF16),
                Sst=mk("pa_Sst", [128, 1024], F32),
                Zrun=mk("pa_Zrun", [128, 1024], F32),
                Drun=mk("pa_Drun", [128, 8], F32),
                rkcols=mk("pa_rkcols", [128, 64], F32),
                ebendF=mk("pa_ebendF", [128, 64], F32),
                cosb=mk("pa_cosb", [128, NT + 64], BF16),
                sinb=mk("pa_sinb", [128, NT + 64], BF16),
                wkF=mk("pa_wkF", [128, 1024], F32),
            )
            with ExitStack() as stk2:
                mk2 = lambda n, s, dt: stk2.enter_context(nc.sbuf_tensor(n, s, dt))
                sa.update(
                    xb=mk2("pa_xb", [128, 16384], BF16),
                    xbh=mk2("pa_xbh", [128, 2048], BF16),
                    kvT=mk2("pa_kvT", [128, 8192], BF16),
                    wcol=mk2("pa_wcol", [128, 8192], BF16),
                    qtmp=mk2("pa_qtmp", [128, 1024], BF16),
                    crow=mk2("pa_crow", [1, 2048], F32),
                    rkraw=mk2("pa_rkraw", [128, 16], F32),
                )
                phaseA(sa)
            pr.barrier()
            with ExitStack() as stk2:
                mk2 = lambda n, s, dt: stk2.enter_context(nc.sbuf_tensor(n, s, dt))
                sa.update(
                    qch=mk2("pa_qch", [128, 8192], BF16),
                    oT=mk2("pa_oT", [128, 4096], BF16),
                    Scast=mk2("pa_Scast", [128, 128], BF16),
                )
                phaseA2(sa)
            pr.barrier()
        with ExitStack() as stk:
            mk = lambda n, s, dt: stk.enter_context(nc.sbuf_tensor(n, s, dt))
            phaseWo(dict(oTf=mk("pw_oTf", [128, 32768], BF16),
                         wcol=mk("pw_wcol", [128, 8192], BF16),
                         aev=mk("pw_aev", [128, 2048], F32)))
        pr.barrier()
        with ExitStack() as stk:
            mk = lambda n, s, dt: stk.enter_context(nc.sbuf_tensor(n, s, dt))
            global_stat[0] = mk("pa3_stat", [1, 3072], F32)
            global_statB[0] = mk("pa3_statB", [128, 1024], F32)
            phaseA3(dict(resA=mk("pa3_res", [128, 16384], F32),
                         xroll=mk("pa3_xr", [128, 1024], F32)))
        pr.barrier()

    if debug.get("upto", 99) >= 2:
        with ExitStack() as stk:
            mk = lambda n, s, dt: stk.enter_context(nc.sbuf_tensor(n, s, dt))
            global_stat[0] = mk("pb_stat", [1, 3072], F32)
            global_statB[0] = mk("pb_statB", [128, 1024], F32)
            phaseB(dict(resB=mk("pb_res", [128, 16384], F32),
                        xbB=mk("pb_xb", [128, 16384], BF16),
                        AfB=mk("pb_Af", [128, 4096], BF16),
                        BwB=mk("pb_Bw", [128, 4096], BF16),
                        rtT=mk("pb_rt", [128, 256], F32),
                        lg8=mk("pb_lg8", [8, 512], F32),
                        lgT=mk("pb_lgT", [128, 32], F32),
                        eqT=mk("pb_eqT", [128, 32], F32),
                        mxc=mk("pb_mxc", [128, 4], F32),
                        rw8=mk("pb_rw8", [8, 512], F32),
                        rwXs=mk("pb_rwXs", [128, 512], F32),
                        whB=mk("pb_whB", [128, 512], BF16)))
        pr.barrier()

    if debug.get("upto", 99) >= 3:
        with ExitStack() as stk:
            mk = lambda n, s, dt: stk.enter_context(nc.sbuf_tensor(n, s, dt))
            global_stat[0] = mk("pc_stat", [1, 3072], F32)
            global_statB[0] = mk("pc_statB", [128, 1024], F32)
            phaseC(dict(resC=mk("pc_res", [128, 16384], F32),
                        xcB=mk("pc_xc", [128, 16384], BF16),
                        TbB=mk("pc_Tb", [128, 4096], BF16),
                        HB=mk("pc_HB", [128, 4096], BF16)))
        pr.barrier()

    if debug.get("upto", 99) >= 4:
        with ExitStack() as stk:
            mk = lambda n, s, dt: stk.enter_context(nc.sbuf_tensor(n, s, dt))
            dq_emit, dq_ag = mk_dq(dict(qt=mk("pd_qt", [128, 2816], BF16),
                                        msk=mk("pd_msk", [128, 2816], mybir.dt.uint8),
                                        acc=mk("pd_acc", [128, 1408], BF16),
                                        sct=mk("pd_sct", [128, 2816], BF16),
                                        wout=mk("pd_wout", [128, 2816], BF16)))
            for _ in range(32):
                dq_emit()
            dq_ag(1)
            for _ in range(32):
                dq_emit()
            dq_ag(2)

            def d1_hook(tg):
                if tg < 43:
                    dq_emit()
                    dq_emit()
                elif tg == 45:
                    dq_ag(3)
            phaseD1(dict(xdB=mk("pd1_x", [128, 32768], BF16),
                         wcolD=mk("pd1_w", [128, 16384], BF16),
                         silB=mk("pd1_s", [128, 2048], BF16),
                         hv=mk("pd1_h", [128, 2048], BF16)), dq=d1_hook)
        pr.barrier()
        with ExitStack() as stk:
            mk = lambda n, s, dt: stk.enter_context(nc.sbuf_tensor(n, s, dt))
            global_stat[0] = mk("pd2_stat", [1, 3072], F32)
            global_statB[0] = mk("pd2_statB", [128, 1024], F32)
            phaseD2(dict(hres=mk("pd2_hr", [128, 44032], BF16),
                         resD=mk("pd2_res", [128, 16384], F32),
                         w3col=mk("pd2_w3", [128, 11008], BF16)))
        pr.barrier()
    with nc.Block() as block:
        pr.emit(block)
    return nc, C, dbg_outs


def host_inputs(inputs, core):
    C = _host_constants()
    beta, sq = core // 4, core % 4
    s0 = sq * NT
    x = inputs["L0_attn_hidden_states"]
    m = {}
    m["xT"] = np.ascontiguousarray(x[beta, s0:s0 + NT, :].T)
    if sq == 0:
        m["xTh"] = np.zeros((D, 64), np.float32)
        m["mask1c0"] = np.full((64, 128), -MSK, np.float32)
    else:
        m["xTh"] = np.ascontiguousarray(x[beta, s0 - 64:s0, :].T)
        jj1 = np.arange(64)[:, None]; ii1 = np.arange(128)[None, :]
        m["mask1c0"] = np.where(jj1 >= ii1, 0.0, -MSK).astype(np.float32)
    m["Wq"] = inputs["L0_attn_Wq"]
    m["Wk"] = inputs["L0_attn_Wk"]
    m["Wv"] = inputs["L0_attn_Wv"]
    m["Wo"] = inputs["L0_attn_Wo"]
    m["routeT"] = np.ascontiguousarray(inputs["L0_ffn_route_w"].T)
    m["Aflat"] = np.ascontiguousarray(inputs["L0_ffn_A"].reshape(128, D).T)
    m["BwT"] = np.ascontiguousarray(inputs["L0_ffn_Bw"].transpose(0, 2, 1).reshape(128, D))
    m["lamr"] = np.ascontiguousarray(inputs["L1_attn_Lambda_ri"].T)
    m["prow"] = np.ascontiguousarray(inputs["L1_attn_p_ri"].T.reshape(1, 128))
    m["qrow"] = np.ascontiguousarray(inputs["L1_attn_q_ri"].T.reshape(1, 128))
    m["VcTre"] = np.ascontiguousarray(inputs["L1_attn_Vc_ri"][:, :, 0].T)
    m["VcTim"] = np.ascontiguousarray(inputs["L1_attn_Vc_ri"][:, :, 1].T)
    m["Ctr"] = np.ascontiguousarray(inputs["L1_attn_Ct_ri"][0].T.reshape(1, 128))
    m["Bp"] = inputs["L1_attn_Bp"]
    m["Dlog"] = np.concatenate([inputs["L1_attn_D"], inputs["L1_attn_log_step"],
                                np.zeros(2, np.float32)]).reshape(1, 4).astype(np.float32)
    w1t = np.zeros((D, HID_PAD), np.int32); w1t[:, :HID] = inputs["L1_ffn_w1_q"].T
    w2t = np.zeros((D, HID_PAD), np.int32); w2t[:, :HID] = inputs["L1_ffn_w2_q"].T
    s1t = np.zeros((64, HID_PAD), np.float32); s1t[:, :HID] = inputs["L1_ffn_w1_scale"].T
    s2t = np.zeros((64, HID_PAD), np.float32); s2t[:, :HID] = inputs["L1_ffn_w2_scale"].T
    m["w1qT"] = np.ascontiguousarray(w1t[:, core * SH12:(core + 1) * SH12])
    m["w2qT"] = np.ascontiguousarray(w2t[:, core * SH12:(core + 1) * SH12])
    m["s1T"] = np.ascontiguousarray(s1t[:, core * SH12:(core + 1) * SH12])
    m["s2T"] = np.ascontiguousarray(s2t[:, core * SH12:(core + 1) * SH12])
    w3t = inputs["L1_ffn_w3_q"].T
    s3t = inputs["L1_ffn_w3_scale"].T
    m["w3qT"] = np.ascontiguousarray(w3t[:, core * SH3:(core + 1) * SH3])
    m["s3T"] = np.ascontiguousarray(s3t[:, core * SH3:(core + 1) * SH3])
    cos, sin = _host_rope(s0)
    sgn = np.where(np.arange(128) < 64, -1.0, 1.0).astype(np.float32)
    m["cosTt"] = np.ascontiguousarray(cos.T).astype(BF)
    m["sinTt"] = np.ascontiguousarray((sin * sgn[None, :]).T).astype(BF)
    for k, v in C.items():
        m["c_" + k] = np.asarray(v)
    mvec = np.zeros(8, np.float32)
    for j in range(8):
        if j // 4 == beta and j < core:
            mvec[j] = 1.0
    sm = np.zeros((128, 16), np.float32)
    sm[:, 0:8] = mvec[None, :]
    sm[:, 8:16] = 1.0 - mvec[None, :]
    m["scanm"] = sm
    return m


def kernel(**inputs):
    nc, C, dbg = build_program()
    in_maps = [host_inputs(inputs, c) for c in range(NC)]
    res = run_bass_kernel_spmd(nc, in_maps, core_ids=list(range(NC)))
    full = np.zeros((2, 4096, D), np.float32)
    for c in range(NC):
        beta, sq = c // 4, c % 4
        full[beta, sq * NT:(sq + 1) * NT, :] = res.results[c]["out"].T
    return full


# version 03
# speedup vs baseline: 1.2618x; 1.1007x over previous
import numpy as np
import ml_dtypes
import concourse.bass as bass
import concourse.mybir as mybir
from concourse.bass_utils import run_bass_kernel_spmd

F32 = mybir.dt.float32
BF16 = mybir.dt.bfloat16
I32 = mybir.dt.int32
U8 = mybir.dt.uint8
OP = mybir.AluOpType
AF = mybir.ActivationFunctionType
AX = mybir.AxisListType
BF = ml_dtypes.bfloat16

D = 4096
NT = 1024
NC = 8
DK = 128
KVH = 8
GRP = 4
CH = 128
NCH = 8
HID = 11008
HID_PAD = 11264
SH12 = HID_PAD // NC
SH3 = D // NC
THETA = 10000.0
MSK = 30000.0
ISQD = 0.08838834764831845
NF4 = np.array([-1.0, -0.6961928009986877, -0.5250730514526367, -0.39491748809814453,
                -0.28444138169288635, -0.18477343022823334, -0.09105003625154495, 0.0,
                0.07958029955625534, 0.16093020141124725, 0.24611230194568634,
                0.33791524171829224, 0.44070982933044434, 0.5626170039176941,
                0.7229568362236023, 1.0], dtype=np.float32)
NF4_BF = NF4.astype(BF)

CP_D1 = 16; CP_2STEP = 17; CP_EPS = 18; CP_INVD = 19; CP_HALF = 20
CP_1P5 = 21; CP_NEG1 = 22; CP_ISQD = 23; CP_NHALF = 24; CP_G16 = 25; CP_P16 = 26


def _host_constants():
    c = {}
    c["ident"] = np.eye(128, dtype=np.float32).astype(BF)
    c["identf"] = np.eye(128, dtype=np.float32)
    rev = np.zeros((128, 128), np.float32)
    rev[np.arange(128), 127 - np.arange(128)] = 1.0
    c["rev"] = rev.astype(BF)
    c["triu_ji"] = np.triu(np.ones((128, 128), np.float32)).astype(BF)
    c["triu4"] = np.tile(np.triu(np.ones((128, 128), np.float32)), (1, 4)).astype(BF)
    jj = np.arange(128)[:, None]
    ii = np.arange(128)[None, :]
    c["mask2"] = np.where((jj <= ii) & (jj >= ii - 64), 0.0, -MSK).astype(np.float32)
    c["mask1e"] = np.where(jj >= ii + 64, 0.0, -MSK).astype(np.float32)
    c["m2q"] = np.tile(np.where((jj <= ii) & (jj >= ii - 64), 1.0, 0.0), (1, 4)).astype(BF)
    c["m1q"] = np.tile(np.where(jj >= ii + 64, 1.0, 0.0), (1, 4)).astype(BF)
    bs = np.zeros((8, 128), np.float32)
    for e in range(8):
        bs[e, e * 16:(e + 1) * 16] = 1.0
    c["blocksel"] = bs
    c["ones_col"] = np.ones((128, 1), np.float32).astype(BF)
    c["ones_row"] = np.ones((1, 128), np.float32).astype(BF)
    pm = np.zeros((128, 128), np.float32)
    for d in range(128):
        pm[(d + 64) % 128, d] = 1.0
    c["perm"] = pm.astype(BF)
    k2 = np.arange(64)
    W64 = np.exp(2j * np.pi * np.outer(k2, k2) / 64.0)
    c["W64re"] = W64.real.astype(np.float32)
    c["W64im"] = W64.imag.astype(np.float32)
    Tw = np.exp(2j * np.pi * np.outer(k2, k2) / 4096.0)
    c["Twre"] = Tw.real.astype(np.float32)
    c["Twim"] = Tw.imag.astype(np.float32)
    W64b = np.exp(2j * np.pi * np.outer(k2, k2) / 64.0) / 4096.0
    c["W64bre"] = W64b.real.astype(np.float32)
    c["W64bim"] = W64b.imag.astype(np.float32)
    t32 = (np.arange(4096).astype(np.float32) / np.float32(4096.0)).astype(np.float32)
    ang = (np.float32(-2.0 * np.pi) * t32).astype(np.float32)
    om = (np.cos(ang.astype(np.float64)) + 1j * np.sin(ang.astype(np.float64)))
    om = om.astype(np.complex64).astype(np.complex128)
    g0 = (1.0 - om) / (1.0 + om)
    cc2 = 2.0 / (1.0 + om)
    c["g0re"] = g0.real.astype(np.float32).reshape(64, 64)
    c["g0im"] = g0.imag.astype(np.float32).reshape(64, 64)
    c["ccre"] = cc2.real.astype(np.float32).reshape(64, 64)
    c["ccim"] = cc2.imag.astype(np.float32).reshape(64, 64)
    return c


def _host_rope(pos0):
    pos = np.arange(pos0 - 64, pos0 + NT).astype(np.float64)
    half = np.arange(64, dtype=np.float64)
    inv = 1.0 / THETA ** (2.0 * half / DK)
    fr = np.maximum(pos, 0.0)[:, None] * inv[None, :]
    emb = np.concatenate([fr, fr], axis=1)
    return np.cos(emb).astype(np.float32), np.sin(emb).astype(np.float32)


class Prog:

    def __init__(self, nc):
        self.nc = nc
        self.ops = {k: [] for k in "gscvt"}
        self.sems = {}
        self.cnt = {}
        self.pending = {k: [] for k in "gscvt"}
        self.last = {k: None for k in "gscvt"}

    def sem(self, name):
        if name not in self.sems:
            self.sems[name] = self.nc.ctx.enter_context(self.nc.semaphore(name))
            self.cnt[name] = 0
        return self.sems[name]

    def on(self, eng, fn):
        self.ops[eng].append(fn)

    def wait(self, eng, name, val=None):
        s = self.sem(name)
        v = self.cnt[name] if val is None else val
        self.ops[eng].append(lambda e, s=s, v=v: e.wait_ge(s, v))
        return v

    def mark(self, eng, name, by=1):
        s = self.sem(name)
        self.cnt[name] += by
        self.pending[eng].append((s, by))
        return self.cnt[name]

    def _take_marks(self, eng):
        m = self.pending[eng]
        self.pending[eng] = []
        return m

    def chained(self, eng):
        return Chained(self, eng)

    def seq(self, eng, fn, name=None, by=1):
        marks = self._take_marks(eng)
        if name is not None:
            s = self.sem(name)
            self.cnt[name] += by
            marks = marks + [(s, by)]
        def run(e, fn=fn, marks=marks):
            i = fn(e)
            for ms, mv in marks:
                i = i.then_inc(ms, mv)
        self.ops[eng].append(run)
        if name is not None:
            self.last[eng] = (name, self.cnt[name])

    def dma(self, eng, out, in_, sem_name, **kw):
        s = self.sem(sem_name)
        self.cnt[sem_name] += 16
        marks = self._take_marks(eng)
        nc = self.nc
        def run(e, out=out, in_=in_, kw=kw, s=s, marks=marks):
            n0 = nc.n_instructions()
            i = e.dma_start(out=out, in_=in_, **kw)
            assert nc.n_instructions() - n0 == 1, "dma split"
            i = i.then_inc(s, 16)
            for ms, mv in marks:
                i = i.then_inc(ms, mv)
        self.ops[eng].append(run)
        self.last[eng] = (sem_name, self.cnt[sem_name])
        return self.cnt[sem_name]

    def barrier(self):
        lasts = {e: self.last[e] for e in "gscvt" if self.last[e] is not None}
        for e in "gscvt":
            if not self.ops[e]:
                continue
            for e2, (nm, vv) in lasts.items():
                if e2 != e:
                    self.wait(e, nm, vv)

    def emit(self, block):
        for tag, attr in (("g", "gpsimd"), ("s", "sync"), ("c", "scalar"),
                          ("v", "vector"), ("t", "tensor")):
            ops = self.ops[tag]
            if ops:
                getattr(block, attr)(lambda e, ops=ops: [f(e) for f in ops])


class Chained:

    def __init__(self, pr, eng):
        object.__setattr__(self, "_pr", pr)
        object.__setattr__(self, "_eng", eng)

    def __getattr__(self, opname):
        pr, eng = self._pr, self._eng
        sname = eng + "_self"
        s = pr.sem(sname)
        def call(*args, **kw):
            prev = pr.cnt[sname]
            pr.cnt[sname] += 1
            marks = pr._take_marks(eng)
            def run(e, prev=prev, opname=opname, args=args, kw=kw, marks=marks):
                if prev > 0:
                    e.wait_ge(s, prev)
                i = getattr(e, opname)(*args, **kw).then_inc(s, 1)
                for ms, mv in marks:
                    i = i.then_inc(ms, mv)
            pr.ops[eng].append(run)
            pr.last[eng] = (sname, pr.cnt[sname])
            return pr.cnt[sname]
        return call


def ap3(t, extra_off, dims):
    a = t if isinstance(t, bass.AP) else t.ap()
    return bass.AP(a.tensor, a.offset + extra_off, [list(a.ap[0])] + [list(d) for d in dims])


def build_program(debug=None):
    debug = debug or {}
    nc = bass.Bass("TRN2", target_bir_lowering=False)
    pr = Prog(nc)
    C = _host_constants()

    ein = lambda n, s, dt=F32: nc.dram_tensor(n, s, dt, kind="ExternalInput")
    dint = lambda n, s, dt=F32: nc.dram_tensor(n, s, dt, kind="Internal")

    xT = ein("xT", [D, NT]); xTh = ein("xTh", [D, 64])
    Wq = ein("Wq", [D, D]); Wk = ein("Wk", [D, 1024]); Wv = ein("Wv", [D, 1024])
    Wo = ein("Wo", [D, D])
    routeT = ein("routeT", [D, 8]); Aflat = ein("Aflat", [D, 128]); BwT = ein("BwT", [128, D])
    prow_i = ein("prow", [1, 128]); qrow_i = ein("qrow", [1, 128]); lamr = ein("lamr", [2, 64])
    VcTre = ein("VcTre", [64, 64]); VcTim = ein("VcTim", [64, 64])
    Ctr = ein("Ctr", [1, 128])
    Bp = ein("Bp", [64, 1]); Dlog = ein("Dlog", [1, 4])
    w1qT = ein("w1qT", [D, SH12], I32); w2qT = ein("w2qT", [D, SH12], I32)
    w3qT = ein("w3qT", [HID, SH3], I32)
    s1T = ein("s1T", [64, SH12]); s2T = ein("s2T", [64, SH12]); s3T = ein("s3T", [172, SH3])
    cosTt = ein("cosTt", [128, NT + 64], BF16); sinTt = ein("sinTt", [128, NT + 64], BF16)
    cin = {k: ein("c_" + k, list(v.shape), BF16 if v.dtype == BF else F32) for k, v in C.items()}
    mask1c0 = ein("mask1c0", [64, 512], BF16)
    cosT4 = ein("cosT4", [128, 4096], BF16); sinT4 = ein("sinT4", [128, 4096], BF16)
    scanm = ein("scanm", [128, 16])
    out = nc.dram_tensor("out", [D, NT], F32, kind="ExternalOutput")
    dbg_outs = {}
    if "Kout" in debug:
        dbg_outs["Kout"] = nc.dram_tensor("Kout", [1, 4352], F32, kind="ExternalOutput")
        dbg_outs["Kacc"] = nc.dram_tensor("Kacc", [64, 512], F32, kind="ExternalOutput")
        dbg_outs["Kar"] = nc.dram_tensor("Kar", [64, 128], F32, kind="ExternalOutput")
        dbg_outs["Krows"] = nc.dram_tensor("Krows", [1, 642], F32, kind="ExternalOutput")
    if "x1out" in debug:
        dbg_outs["x1out"] = nc.dram_tensor("x1out", [D, NT], F32, kind="ExternalOutput")
    if "x2out" in debug:
        dbg_outs["x2out"] = nc.dram_tensor("x2out", [D, NT], F32, kind="ExternalOutput")
        dbg_outs["rwout"] = nc.dram_tensor("rwout", [8, NT], F32, kind="ExternalOutput")
    if "x3out" in debug:
        dbg_outs["x3out"] = nc.dram_tensor("x3out", [D, NT], F32, kind="ExternalOutput")
    if "aout" in debug:
        dbg_outs["qout"] = nc.dram_tensor("qout", [D, NT], BF16, kind="ExternalOutput")
        dbg_outs["oout"] = nc.dram_tensor("oout", [D, NT], BF16, kind="ExternalOutput")
        dbg_outs["attnout"] = nc.dram_tensor("attnout", [D, NT], F32, kind="ExternalOutput")
        dbg_outs["adiag"] = nc.dram_tensor("adiag", [128, 256], F32, kind="ExternalOutput")
        dbg_outs["adiag2"] = nc.dram_tensor("adiag2", [3, 3072], F32, kind="ExternalOutput")

    x1d = dint("x1d", [D, NT]); x2d = dint("x2d", [D, NT]); x3d = dint("x3d", [D, NT])
    attnTd = dint("attnTd", [D, NT])
    qdr = dint("qdr", [D, NT], BF16)
    odr = dint("odr", [D, NT], BF16)
    hdr = dint("hdr", [HID, NT], BF16)
    Kpad = dint("Kpad", [1, 4352])
    rowsd = dint("rowsd", [4, 8192])
    ag_s_in = dint("ag_s_in", [KVH, 130, 128])
    ag_s_out = dint("ag_s_out", [NC, KVH, 130, 128])
    agw_in = {1: dint("agw1i", [D, SH12], BF16), 2: dint("agw2i", [D, SH12], BF16),
              3: dint("agw3i", [HID, SH3], BF16)}
    agw_out = {1: dint("agw1o", [NC, D, SH12], BF16), 2: dint("agw2o", [NC, D, SH12], BF16),
               3: dint("agw3o", [NC, HID, SH3], BF16)}

    glob = nc.ctx
    sbg = lambda n, s, dt: glob.enter_context(nc.sbuf_tensor(n, s, dt))

    cpool = sbg("cpool", [128, 32], F32)
    tpool = sbg("tpool", [128, 16], BF16)
    ct = {k: sbg("ct_" + k, list(v.shape), BF16 if v.dtype == BF else F32).ap() for k, v in C.items()}
    m1c0_t = sbg("m1c0_t", [64, 512], BF16)
    scanm_t = sbg("scanm_t", [128, 16], F32)
    global_stat = [None]
    global_statB = [None]
    wkD = sbg("wkD", [128, 2048], BF16)
    PS = [glob.enter_context(nc.psum_tensor(f"p{i}", [128, 512], F32)) for i in range(7)]
    PSB = glob.enter_context(nc.psum_tensor("pb7", [128, 1024], BF16))

    def setup():
        def f(g):
            for v in range(16):
                g.memset(cpool[:, v:v + 1], float(1 - v))
                g.memset(tpool[:, v:v + 1], float(NF4_BF[v]))
            g.memset(cpool[:, CP_EPS:CP_EPS + 1], 1e-5)
            g.memset(cpool[:, CP_INVD:CP_INVD + 1], 1.0 / D)
            g.memset(cpool[:, CP_HALF:CP_HALF + 1], 0.5)
            g.memset(cpool[:, CP_1P5:CP_1P5 + 1], 1.5)
            g.memset(cpool[:, CP_NEG1:CP_NEG1 + 1], -1.0)
            g.memset(cpool[:, CP_ISQD:CP_ISQD + 1], ISQD)
            g.memset(cpool[:, CP_NHALF:CP_NHALF + 1], -0.5)
            g.memset(cpool[:, CP_G16:CP_G16 + 1], -1.0 / 16.0)
            g.memset(cpool[:, CP_P16:CP_P16 + 1], 1.0 / 16.0)
        pr.on("g", f)
        for k in C:
            pr.dma("g", ct[k][:, :], cin[k].ap(), "cld")
        pr.dma("g", m1c0_t[:, :], mask1c0.ap(), "cld")
        pr.dma("g", scanm_t[:, :], scanm.ap(), "cld")
        for e in "vcts":
            pr.wait(e, "cld")

    def phaseK(sk):
        wkK = sk["wkK"]
        V = pr.chained("v")
        Cc = pr.chained("c")
        pr.dma("g", sk["p"][:, :], prow_i.ap(), "kld")
        pr.dma("g", sk["q"][:, :], qrow_i.ap(), "kld")
        pr.dma("g", sk["vctre"][:, :], VcTre.ap(), "kld")
        pr.dma("g", sk["vctim"][:, :], VcTim.ap(), "kld")
        pr.dma("g", sk["ct"][:, :], Ctr.ap(), "kld")
        pr.dma("g", sk["bp"][:, :], Bp.ap(), "kld")
        pr.dma("g", sk["dl"][:, :], Dlog.ap(), "kld")
        pr.dma("g", sk["lsr"][0:1, 0:64], bass.AP(Dlog, 1, [[1, 1], [0, 64], [1, 1]]), "kld")
        pr.wait("t", "kld")
        def f(t):
            t.matmul(PS[0][0:1, 0:64], sk["bp"][:, :], sk["vctre"][:, :], start=True, stop=True)
            return t.matmul(PS[1][0:1, 0:64], sk["bp"][:, :], sk["vctim"][:, :], start=True, stop=True)
        pr.seq("t", f, "kmmB")
        pr.wait("c", "kld")
        kexp = Cc.activation(sk["lsr"][0:1, 64:128], sk["lsr"][0:1, 0:64], AF.Exp)
        pr.wait("v", "kmmB")
        pr.wait("v", "c_self", kexp)
        row = sk["rows"]
        t0 = wkK[0:1, 5120:5184]
        t1 = wkK[0:1, 5184:5248]
        V.tensor_copy(row[0:1, 0:64], PS[0][0:1, 0:64])
        V.tensor_copy(row[0:1, 64:128], PS[1][0:1, 0:64])
        def cmc(ar, ai, br, bi, orr, oi):
            V.tensor_tensor(t0, ar, br, OP.mult)
            V.tensor_tensor(t1, ai, bi, OP.mult)
            V.tensor_tensor(orr, t0, t1, OP.add)
            V.tensor_tensor(t0, ar, bi, OP.mult)
            V.tensor_tensor(t1, ai, br, OP.mult)
            V.tensor_tensor(oi, t0, t1, OP.subtract)
        ctR, ctI = sk["ct"][0:1, 0:64], sk["ct"][0:1, 64:128]
        pR, pI = sk["p"][0:1, 0:64], sk["p"][0:1, 64:128]
        qR, qI = sk["q"][0:1, 0:64], sk["q"][0:1, 64:128]
        bR, bI = row[0:1, 0:64], row[0:1, 64:128]
        cmc(ctR, ctI, bR, bI, row[0:1, 128:192], row[0:1, 192:256])
        cmc(ctR, ctI, pR, pI, row[0:1, 256:320], row[0:1, 320:384])
        cmc(qR, qI, bR, bI, row[0:1, 384:448], row[0:1, 448:512])
        cmc(qR, qI, pR, pI, row[0:1, 512:576], row[0:1, 576:640])
        V.reciprocal(sk["lsr"][0:1, 128:192], sk["lsr"][0:1, 64:128])
        V.tensor_scalar(row[0:1, 640:641], sk["lsr"][0:1, 128:129], 2.0, None, OP.mult)
        krow = V.tensor_scalar(row[0:1, 641:642], sk["dl"][0:1, 0:1], 1.0, None, OP.add)
        pr.wait("g", "v_self", krow)
        pr.dma("g", rowsd.ap()[0:1, 0:642], row[0:1, 0:642], "krd")
        pr.wait("g", "krd")
        lamb = sk["lamb"]; v4b = sk["v4b"]
        pr.dma("g", lamb[:, :], bass.AP(lamr, 0, [[0, 64], [1, 128]]), "kld2")
        pr.dma("g", v4b[:, :], bass.AP(rowsd, 128, [[0, 64], [1, 512]]), "kld2")
        pr.dma("g", cpool[:, CP_2STEP:CP_2STEP + 1], bass.AP(rowsd, 640, [[0, 128], [1, 1]]), "kld2")
        pr.dma("g", cpool[:, CP_D1:CP_D1 + 1], bass.AP(rowsd, 641, [[0, 128], [1, 1]]), "kld2")
        pr.wait("v", "kld2")
        acc = {nm: (sk["acc"][0:64, i * 128:i * 128 + 64],
                    sk["acc"][0:64, i * 128 + 64:(i + 1) * 128])
               for i, nm in enumerate(["k00", "k01", "k10", "k11"])}
        dre = wkK[0:64, 0:1024]; dim = wkK[0:64, 1024:2048]
        inv = wkK[0:64, 2048:3072]; ta = wkK[0:64, 3072:4096]
        tb = wkK[0:64, 4096:5120]
        gre_q = wkK[0:64, 5120:5136]; gim_q = wkK[0:64, 5136:5152]
        red = wkK[0:64, 5152:5216]
        for nm in ["k00", "k01", "k10", "k11"]:
            V.memset(acc[nm][0], 0.0)
            V.memset(acc[nm][1], 0.0)
        for qtr in range(4):
            k1o = qtr * 16
            V.tensor_scalar(gre_q, ct["g0re"][:, k1o:k1o + 16],
                            cpool[0:64, CP_2STEP:CP_2STEP + 1], None, OP.mult)
            V.tensor_scalar(gim_q, ct["g0im"][:, k1o:k1o + 16],
                            cpool[0:64, CP_2STEP:CP_2STEP + 1], None, OP.mult)
            g3r = ap3(gre_q, 0, [[1, 16], [0, 64]])
            g3i = ap3(gim_q, 0, [[1, 16], [0, 64]])
            l3r = ap3(lamb, 0, [[0, 16], [1, 64]])
            l3i = ap3(lamb, 64, [[0, 16], [1, 64]])
            V.tensor_tensor(dre, g3r, l3r, OP.subtract)
            V.tensor_tensor(dim, g3i, l3i, OP.subtract)
            V.tensor_tensor(inv, dre, dre, OP.mult)
            V.tensor_tensor(ta, dim, dim, OP.mult)
            V.tensor_tensor(ta, inv, ta, OP.add)
            V.reciprocal(inv, ta)
            for i, nm in enumerate(["k00", "k01", "k10", "k11"]):
                vr3 = ap3(v4b, i * 128, [[0, 16], [1, 64]])
                vi3 = ap3(v4b, i * 128 + 64, [[0, 16], [1, 64]])
                ar, ai = acc[nm]
                V.tensor_tensor(ta, vr3, dre, OP.mult)
                V.tensor_tensor(tb, vi3, dim, OP.mult)
                V.tensor_tensor(ta, ta, tb, OP.add)
                V.tensor_tensor(ta, ta, inv, OP.mult)
                V.reduce_sum(red[:, 0:16], ap3(ta, 0, [[64, 16], [1, 64]]), axis=AX.X)
                V.tensor_copy(ar[:, k1o:k1o + 16], red[:, 0:16])
                V.tensor_tensor(ta, vi3, dre, OP.mult)
                V.tensor_tensor(tb, vr3, dim, OP.mult)
                V.tensor_tensor(ta, ta, tb, OP.subtract)
                V.tensor_tensor(ta, ta, inv, OP.mult)
                V.reduce_sum(red[:, 0:16], ap3(ta, 0, [[64, 16], [1, 64]]), axis=AX.X)
                V.tensor_copy(ai[:, k1o:k1o + 16], red[:, 0:16])
        arre = sk["arre"]; arim = sk["arim"]; arimn = sk["arimn"]
        w1 = wkK[0:64, 0:64]; w2 = wkK[0:64, 64:128]
        w3 = wkK[0:64, 128:192]; w4 = wkK[0:64, 192:256]
        w5 = wkK[0:64, 256:320]
        a_re, a_im = acc["k11"]
        V.tensor_scalar(w1, a_re, 1.0, None, OP.add)
        V.tensor_tensor(w2, w1, w1, OP.mult)
        V.tensor_tensor(w3, a_im, a_im, OP.mult)
        V.tensor_tensor(w3, w2, w3, OP.add)
        V.reciprocal(w2, w3)
        br_, bi_ = acc["k01"]; cr_, ci_ = acc["k10"]
        V.tensor_tensor(w3, br_, cr_, OP.mult)
        V.tensor_tensor(w4, bi_, ci_, OP.mult)
        V.tensor_tensor(w3, w3, w4, OP.subtract)
        V.tensor_tensor(w4, br_, ci_, OP.mult)
        V.tensor_tensor(w5, bi_, cr_, OP.mult)
        V.tensor_tensor(w4, w4, w5, OP.add)
        V.tensor_tensor(w5, w3, w1, OP.mult)
        V.tensor_tensor(arre, w4, a_im, OP.mult)
        V.tensor_tensor(w5, w5, arre, OP.add)
        V.tensor_tensor(w5, w5, w2, OP.mult)
        V.tensor_tensor(arre, w4, w1, OP.mult)
        V.tensor_tensor(arim, w3, a_im, OP.mult)
        V.tensor_tensor(arre, arre, arim, OP.subtract)
        V.tensor_tensor(arre, arre, w2, OP.mult)
        k0r, k0i = acc["k00"]
        V.tensor_tensor(w3, k0r, w5, OP.subtract)
        V.tensor_tensor(w4, k0i, arre, OP.subtract)
        V.tensor_tensor(arre, ct["ccre"], w3, OP.mult)
        V.tensor_tensor(w5, ct["ccim"], w4, OP.mult)
        V.tensor_tensor(arre, arre, w5, OP.subtract)
        V.tensor_tensor(arim, ct["ccre"], w4, OP.mult)
        V.tensor_tensor(w5, ct["ccim"], w3, OP.mult)
        V.tensor_tensor(arim, arim, w5, OP.add)
        kar = V.tensor_scalar(arimn, arim, -1.0, None, OP.mult)
        pr.wait("t", "v_self", kar)
        def f(t):
            t.matmul(PS[0][0:64, 0:64], ct["W64re"], arre, start=True, stop=False)
            t.matmul(PS[0][0:64, 0:64], ct["W64im"], arimn, start=False, stop=True)
            t.matmul(PS[1][0:64, 0:64], ct["W64re"], arim, start=True, stop=False)
            return t.matmul(PS[1][0:64, 0:64], ct["W64im"], arre, start=False, stop=True)
        pr.seq("t", f, "kmm1")
        pr.wait("v", "kmm1")
        gre_s = wkK[0:64, 0:64]; gim_s = wkK[0:64, 64:128]
        hre = sk["hre"]; him = sk["him"]
        V.tensor_copy(gre_s, PS[0][0:64, 0:64])
        V.tensor_copy(gim_s, PS[1][0:64, 0:64])
        V.tensor_tensor(hre, gre_s, ct["Twre"], OP.mult)
        V.tensor_tensor(w5, gim_s, ct["Twim"], OP.mult)
        V.tensor_tensor(hre, hre, w5, OP.subtract)
        V.tensor_tensor(him, gre_s, ct["Twim"], OP.mult)
        V.tensor_tensor(w5, gim_s, ct["Twre"], OP.mult)
        kh = V.tensor_tensor(him, him, w5, OP.add)
        pr.wait("t", "v_self", kh)
        def f(t):
            t.transpose(PS[2][0:64, 0:64], sk["hre"], ct["identf"][0:64, 0:64])
            return t.transpose(PS[3][0:64, 0:64], sk["him"], ct["identf"][0:64, 0:64])
        pr.seq("t", f, "kt1")
        pr.wait("v", "kt1")
        V.tensor_copy(sk["hreT"], PS[2][0:64, 0:64])
        V.tensor_copy(sk["himT"], PS[3][0:64, 0:64])
        kt2 = V.tensor_scalar(sk["himT"], sk["himT"], -1.0, None, OP.mult)
        pr.wait("t", "v_self", kt2)
        def f(t):
            t.matmul(PS[4][0:64, 0:64], ct["W64bre"], sk["hreT"], start=True, stop=False)
            return t.matmul(PS[4][0:64, 0:64], ct["W64bim"], sk["himT"], start=False, stop=True)
        pr.seq("t", f, "kmm2")
        pr.wait("v", "kmm2")
        V.tensor_copy(sk["k2d"][:, :], PS[4][0:64, 0:64])
        kk2 = V.memset(wkK[0:1, 0:129], 0.0)
        pr.wait("g", "v_self", kk2)
        pr.dma("g", Kpad.ap()[0:1, 0:127], wkK[0:1, 0:127], "kpd")
        pr.dma("g", Kpad.ap()[0:1, 4223:4352], wkK[0:1, 0:129], "kpd")
        pr.dma("g", bass.AP(Kpad, 127, [[64, 64], [1, 64]]), sk["k2d"][:, :], "kpd")
        pr.wait("g", "kpd")
        if "Kout" in dbg_outs:
            pr.dma("g", dbg_outs["Kout"].ap(), Kpad.ap(), "kdbg")
            pr.dma("g", dbg_outs["Kacc"].ap(), sk["acc"][:, :], "kdbg")
            pr.dma("g", dbg_outs["Kar"].ap()[:, 0:64], arre, "kdbg")
            pr.dma("g", dbg_outs["Kar"].ap()[:, 64:128], arim, "kdbg")
            pr.dma("g", dbg_outs["Krows"].ap(), sk["rows"][0:1, :], "kdbg")
            pr.wait("g", "kdbg")

    setup()
    with nc.sbuf_tensor("skA", [1, 128], F32) as _p, \
         nc.sbuf_tensor("skB", [1, 128], F32) as _q, \
         nc.sbuf_tensor("skC", [64, 64], F32) as _vr, \
         nc.sbuf_tensor("skD", [64, 64], F32) as _vi, \
         nc.sbuf_tensor("skE", [1, 128], F32) as _ct, \
         nc.sbuf_tensor("skF", [64, 1], F32) as _bp, \
         nc.sbuf_tensor("skG", [1, 4], F32) as _dl, \
         nc.sbuf_tensor("skGl", [1, 192], F32) as _lsr, \
         nc.sbuf_tensor("skH", [1, 642], F32) as _rows, \
         nc.sbuf_tensor("skI", [64, 128], F32) as _lamb, \
         nc.sbuf_tensor("skJ", [64, 512], F32) as _v4b, \
         nc.sbuf_tensor("skK", [64, 512], F32) as _acc, \
         nc.sbuf_tensor("skL", [64, 192], F32) as _ar, \
         nc.sbuf_tensor("skM", [64, 128], F32) as _h, \
         nc.sbuf_tensor("skN", [64, 128], F32) as _hT, \
         nc.sbuf_tensor("skO", [64, 64], F32) as _k2d, \
         nc.sbuf_tensor("skW", [128, 5248], F32) as _wkK:
        sk = dict(p=_p, q=_q, vctre=_vr, vctim=_vi, ct=_ct, bp=_bp, dl=_dl, lsr=_lsr, wkK=_wkK,
                  rows=_rows, lamb=_lamb, v4b=_v4b, acc=_acc,
                  arre=_ar[:, 0:64], arim=_ar[:, 64:128], arimn=_ar[:, 128:192],
                  hre=_h[:, 0:64], him=_h[:, 64:128],
                  hreT=_hT[:, 0:64], himT=_hT[:, 64:128], k2d=_k2d)
        phaseK(sk)


    def phaseA(sa):
        V = pr.chained("v")
        Cc = pr.chained("c")
        xb = sa["xb"]; kvT = sa["kvT"]; wcol = sa["wcol"]; qtmp = sa["qtmp"]
        kst = sa["kst"]; kbart = sa["kbart"]
        ropekt = sa["ropekt"]; vnat = sa["vnat"]; ebt = sa["ebt"]
        haloT = sa["haloT"]; vhalo = sa["vhalo"]; rkhalo = sa["rkhalo"]
        Sst = sa["Sst"]; Zrun = sa["Zrun"]; Drun = sa["Drun"]
        crow = sa["crow"]; rkcols = sa["rkcols"]
        ebendF = sa["ebendF"]; wkF = sa["wkF"]
        cosb = sa["cosb"]; sinb = sa["sinb"]

        pr.dma("g", cosb[:, :], cosTt.ap(), "ctb")
        pr.dma("g", sinb[:, :], sinTt.ap(), "ctb")
        V.memset(Zrun[:, :], 0.0)
        V.memset(Drun[:, :], 1.0)
        V.memset(Sst[:, :], 0.0)
        pr.wait("v", "ctb")

        def proj_col(wsrc, coff, b, n_tok, xsrc, nm):
            pr.wait("g", nm)
            pr.dma("g", ap3(wcol[:, :], b * 4096, [[128, 32], [1, 128]]),
                   bass.AP(wsrc, coff, [[wsrc.shape[1], 128], [128 * wsrc.shape[1], 32], [1, 128]]),
                   "wcA")
            pr.wait("t", "wcA")
            def f(t, b=b, n_tok=n_tok):
                last = None
                for k in range(32):
                    last = t.matmul(PS[b][:, 0:n_tok],
                                    wcol[:, b * 4096 + k * 128:b * 4096 + (k + 1) * 128],
                                    xsrc[:, k * n_tok:(k + 1) * n_tok],
                                    start=(k == 0), stop=(k == 31))
                return last
            return pr.seq("t", f, nm)

        for H in range(2):
            t0 = H * 512
            if H == 1:
                pr.wait("g", "qmm")
            pr.dma("g", ap3(xb[:, :], 0, [[512, 32], [1, 512]]),
                   bass.AP(xT, t0, [[NT, 128], [128 * NT, 32], [1, 512]]), "xbA")
            if H == 0:
                pr.dma("g", ap3(sa["xbh"][:, :], 0, [[64, 32], [1, 64]]),
                       bass.AP(xTh, 0, [[64, 128], [128 * 64, 32], [1, 64]]), "xbA")
            pr.wait("t", "xbA")
            for m in range(16):
                wsrc = Wk if m < 8 else Wv
                b = m % 2
                if m >= 2:
                    pr.wait("t", "v_self")
                proj_col(wsrc, (m % 8) * 128, b, 512, xb, "kvmm")
                pr.wait("v", "kvmm")
                V.tensor_copy(kvT[:, m * 512:(m + 1) * 512], PS[b][:, 0:512])
            if H == 0:
                for m in range(16):
                    wsrc = Wk if m < 8 else Wv
                    b = m % 2
                    if m >= 2:
                        pr.wait("t", "v_self")
                    pr.wait("g", "kvmm")
                    pr.dma("g", ap3(wcol[:, :], b * 4096, [[128, 32], [1, 128]]),
                           bass.AP(wsrc, (m % 8) * 128, [[1024, 128], [128 * 1024, 32], [1, 128]]), "wcA")
                    pr.wait("t", "wcA")
                    def f(t, b=b):
                        last = None
                        for k in range(32):
                            last = t.matmul(PS[2 + b][:, 0:64],
                                            wcol[:, b * 4096 + k * 128:b * 4096 + (k + 1) * 128],
                                            sa["xbh"][:, k * 64:(k + 1) * 64],
                                            start=(k == 0), stop=(k == 31))
                        return last
                    pr.seq("t", f, "kvmm")
                    pr.wait("v", "kvmm")
                    V.tensor_copy(haloT[:, m * 64:(m + 1) * 64], PS[2 + b][:, 0:64])
                for h in range(KVH):
                    pr.wait("t", "v_self")
                    pr.seq("t", lambda t, h=h: t.transpose(PSB[0:64, 0:128],
                                                           haloT[:, (8 + h) * 64:(9 + h) * 64],
                                                           ct["ident"]), "hvt")
                    pr.wait("v", "hvt")
                    V.tensor_copy(vhalo[0:64, h * 128:(h + 1) * 128], PSB[0:64, 0:128])
                    kTh = haloT[:, h * 64:(h + 1) * 64]
                    pr.wait("t", "v_self")
                    pr.seq("t", lambda t, kTh=kTh: t.matmul(PS[5][:, 0:64], ct["perm"], kTh,
                                                            start=True, stop=True), "hkp")
                    pr.wait("v", "hkp")
                    V.tensor_copy(wkD[:, 0:64], PS[5][:, 0:64])
                    V.tensor_tensor(wkD[:, 64:128], kTh, cosb[:, 0:64], OP.mult)
                    V.tensor_tensor(wkD[:, 0:64], wkD[:, 0:64], sinb[:, 0:64], OP.mult)
                    V.tensor_tensor(rkhalo[:, h * 64:(h + 1) * 64],
                                    wkD[:, 64:128], wkD[:, 0:64], OP.add)
            for m in range(32):
                b = m % 2
                if m >= 2:
                    pr.wait("t", "v_self")
                proj_col(Wq, m * 128, b, 512, xb, "qmm")
                pr.wait("v", "qmm")
                if H or m >= 2:
                    pr.wait("v", "qst%d" % b)
                qev = V.tensor_copy(qtmp[:, b * 512:(b + 1) * 512], PS[b][:, 0:512])
                pr.wait("s", "v_self", qev)
                pr.dma("s", bass.AP(qdr, m * 128 * NT + t0, [[NT, 128], [1, 512]]),
                       qtmp[:, b * 512:(b + 1) * 512], "qst%d" % b)
            for ci4 in range(4):
                ci = H * 4 + ci4
                for h in range(KVH):
                    kTh = kvT[:, h * 512 + ci4 * 128: h * 512 + (ci4 + 1) * 128]
                    vTh = kvT[:, (8 + h) * 512 + ci4 * 128: (8 + h) * 512 + (ci4 + 1) * 128]
                    pr.wait("t", "v_self")
                    pr.seq("t", lambda t, vTh=vTh: t.transpose(PSB[:, 0:128], vTh, ct["ident"]), "vtr")
                    pr.wait("v", "vtr")
                    V.tensor_copy(vnat[:, ci * 1024 + h * 128: ci * 1024 + (h + 1) * 128], PSB[:, 0:128])
                    lg = wkF[:, 0:128]; Lc = wkF[:, 128:256]; emb = wkF[:, 256:384]
                    pr.wait("c", "v_self")
                    e1c = Cc.activation(lg, kTh, AF.Exp, scale=cpool[:, CP_NEG1:CP_NEG1 + 1])
                    pr.wait("v", "c_self", e1c)
                    lnc = V.tensor_scalar(lg, lg, 1.0, None, OP.add)
                    pr.wait("c", "v_self", lnc)
                    lnd = Cc.activation(lg, lg, AF.Ln)
                    pr.wait("v", "c_self", lnd)
                    scn = V.tensor_tensor_scan(Lc, lg, lg, 0.0, OP.add, OP.bypass)
                    pr.wait("c", "v_self", scn)
                    ebc = Cc.activation(ebt[:, (ci * 8 + h) * 128:(ci * 8 + h + 1) * 128], Lc,
                                        AF.Exp, scale=cpool[:, CP_G16:CP_G16 + 1])
                    emc = Cc.activation(emb, Lc, AF.Exp, scale=cpool[:, CP_P16:CP_P16 + 1])
                    exk = wkD[:, 256:384]
                    pr.wait("c", "kcs")
                    if ci4 % 2 == 0 and h == 0 and (H or ci4):
                        pr.wait("v", "rkb")
                    xkc = Cc.activation(exk, kTh, AF.Exp)
                    pr.wait("t", "c_self", xkc)
                    pr.seq("t", lambda t, exk=exk: t.matmul(PS[5][0:1, 0:128], ct["ones_col"], exk,
                                                            start=True, stop=True), "kcs")
                    pr.wait("v", "kcs")
                    pr.wait("v", "c_self", emc)
                    V.tensor_copy(crow[0:1, ((ci4 % 2) * 8 + h) * 128:((ci4 % 2) * 8 + h + 1) * 128],
                                  PS[5][0:1, 0:128])
                    V.tensor_tensor(kst[:, (ci * 8 + h) * 128:(ci * 8 + h + 1) * 128], exk, emb, OP.mult)
                    pr.wait("t", "v_self")
                    pr.seq("t", lambda t, kTh=kTh: t.matmul(PS[5][:, 0:128], ct["perm"], kTh,
                                                            start=True, stop=True), "rkp")
                    pr.wait("v", "rkp")
                    V.tensor_copy(wkD[:, 0:128], PS[5][:, 0:128])
                    V.tensor_tensor(wkD[:, 128:256], kTh, cosb[:, 64 + ci * 128:64 + (ci + 1) * 128], OP.mult)
                    V.tensor_tensor(wkD[:, 0:128], wkD[:, 0:128], sinb[:, 64 + ci * 128:64 + (ci + 1) * 128], OP.mult)
                    V.tensor_tensor(ropekt[:, (ci * 8 + h) * 128:(ci * 8 + h + 1) * 128],
                                    wkD[:, 128:256], wkD[:, 0:128], OP.add)
                if ci4 % 2 == 1:
                    qtr = H * 2 + ci4 // 2
                    rr = pr.cnt["v_self"]
                    pr.wait("g", "v_self", rr)
                    pr.dma("g", rowsd.ap()[0:1, 0:2048], crow[0:1, 0:2048], "rkb")
                    pr.wait("g", "rkb")
                    pr.dma("g", ap3(sa["rkraw"][:, 0:16], 0, [[1, 16], [1, 1]]),
                           bass.AP(rowsd, 0, [[1, 128], [128, 16], [1, 1]]), "rkc",
                           allow_slow_non_contiguous=True)
                    pr.wait("v", "rkc")
                    V.reciprocal(rkcols[:, qtr * 16:(qtr + 1) * 16], sa["rkraw"][:, 0:16])
            V.tensor_copy(ebendF[:, H * 32:H * 32 + 32],
                          ap3(ebt[:, :], H * 4096 + 127, [[128, 32], [1, 1]]))
            for ci4 in range(4):
                ci = H * 4 + ci4
                for h in range(KVH):
                    g = ci * 8 + h
                    kb = wkD[:, 256:384]
                    pr.wait("v", "kbt")
                    V.tensor_scalar(kb, kst[:, g * 128:(g + 1) * 128],
                                    ebendF[:, g:g + 1], None, OP.mult)
                    kbv = pr.cnt["v_self"]
                    pr.wait("t", "v_self", kbv)
                    pr.seq("t", lambda t, kb=kb: t.transpose(PSB[:, 0:128], kb, ct["ident"]), "kbt")
                    pr.wait("v", "kbt")
                    V.tensor_scalar(kbart[:, ci * 1024 + h * 128: ci * 1024 + (h + 1) * 128],
                                    PSB[:, 0:128], rkcols[:, g:g + 1], None, OP.mult)
                    zn = pr.cnt["v_self"]
                    pr.wait("t", "v_self", zn)
                    def f(t, ci=ci, h=h):
                        return t.matmul(PS[6][:, 0:128],
                                        kbart[:, ci * 1024 + h * 128: ci * 1024 + (h + 1) * 128],
                                        vnat[:, ci * 1024 + h * 128: ci * 1024 + (h + 1) * 128],
                                        start=True, stop=True)
                    pr.seq("t", f, "zmm")
                    pr.wait("v", "zmm")
                    V.scalar_tensor_tensor(Zrun[:, h * 128:(h + 1) * 128],
                                           Zrun[:, h * 128:(h + 1) * 128],
                                           ebendF[:, g:g + 1],
                                           PS[6][:, 0:128], OP.mult, OP.add)
                    V.tensor_scalar(Drun[:, h:h + 1], Drun[:, h:h + 1],
                                    ebendF[:, g:g + 1], None, OP.mult)
        wr = pr.cnt["v_self"]
        pr.wait("g", "v_self", wr)
        for h in range(KVH):
            pr.dma("g", bass.AP(ag_s_in, h * 130 * 128, [[128, 128], [1, 128]]),
                   Zrun[:, h * 128:(h + 1) * 128], "agw")
            pr.dma("g", bass.AP(ag_s_in, h * 130 * 128 + 128 * 128, [[1, 128], [1, 1]]),
                   Drun[:, h:h + 1], "agw", allow_slow_non_contiguous=True)
        pr.wait("g", "agw")
        def f(g):
            return g.collective_compute(
                "AllGather", OP.bypass, replica_groups=[list(range(NC))],
                ins=[ag_s_in.ap().opt()], outs=[ag_s_out.ap().opt()])
        pr.seq("g", f, "agd")
        pr.wait("s", "agd")
        for j in range(NC):
            for h in range(KVH):
                zf = pr.dma("s", wkF[:, 0:128],
                            bass.AP(ag_s_out, (j * KVH + h) * 130 * 128, [[128, 128], [1, 128]]), "sfz")
                df = pr.dma("s", wkF[:, 128:129],
                            bass.AP(ag_s_out, (j * KVH + h) * 130 * 128 + 128 * 128, [[1, 128], [1, 1]]),
                            "sfd", allow_slow_non_contiguous=True)
                pr.wait("v", "sfz", zf)
                pr.wait("v", "sfd", df)
                V.tensor_scalar(wkF[:, 129:130], wkF[:, 128:129], scanm_t[:, j:j + 1], None, OP.mult)
                V.tensor_scalar(wkF[:, 129:130], wkF[:, 129:130], scanm_t[:, 8 + j:9 + j], None, OP.add)
                V.tensor_scalar(wkF[:, 0:128], wkF[:, 0:128], scanm_t[:, j:j + 1], None, OP.mult)
                fs = V.scalar_tensor_tensor(Sst[:, h * 128:(h + 1) * 128],
                                            Sst[:, h * 128:(h + 1) * 128],
                                            wkF[:, 129:130], wkF[:, 0:128], OP.mult, OP.add)
                pr.wait("s", "v_self", fs)


    def phaseA2(sa):
        V = pr.chained("v")
        Cc = pr.chained("c")
        kst = sa["kst"]; kbart = sa["kbart"]; ropekt = sa["ropekt"]
        vnat = sa["vnat"]; ebt = sa["ebt"]; vhalo = sa["vhalo"]; rkhalo = sa["rkhalo"]
        Sst = sa["Sst"]; rkcols = sa["rkcols"]; ebendF = sa["ebendF"]
        wkF = sa["wkF"]; cosb = sa["cosb"]; sinb = sa["sinb"]
        qch = sa["qch"]; oT = sa["oT"]; Scast = sa["Scast"]
        pr.dma("g", sa["cosb4"][:, :], cosT4.ap(), "ctb")
        pr.dma("g", sa["sinb4"][:, :], sinT4.ap(), "ctb")
        pr.wait("v", "ctb")
        pr.wait("g", "qst0")
        pr.wait("g", "qst1")
        for ci in range(NCH):
            qs_ = (ci % 2) * 4096
            if ci >= 2:
                pr.wait("g", "a2rp")
                pr.wait("g", "c_self")
            pr.dma("g", ap3(qch[:, :], qs_, [[128, 32], [1, 128]]),
                   bass.AP(qdr, ci * 128, [[NT, 128], [128 * NT, 32], [1, 128]]), "a2q%d" % (ci % 2))
            pr.wait("v", "a2o")
            for h in range(KVH):
                g = ci * 8 + h
                c4 = qs_ + h * 512
                qT4 = qch[:, c4:c4 + 512]
                exq4 = wkD[:, 0:512]
                rqt4 = wkD[:, 512:1024]
                qs4 = wkD[:, 1024:1536]
                Am4 = wkD[:, 1536:2048]
                e12 = sa["e12"]; ebt4 = sa["ebt4"]; a2row = sa["a2row"]
                rqb4 = sa["rqb4"]; cosb4 = sa["cosb4"]; sinb4 = sa["sinb4"]
                e1 = e12[:, 0:512]; e2 = e12[:, 512:1024]
                pr.wait("v", "a2ol")
                sc_ = V.tensor_copy(Scast[:, :], Sst[:, h * 128:(h + 1) * 128])
                pr.wait("t", "v_self", sc_)
                def f(t, ci=ci, h=h):
                    return t.matmul(PS[0][:, 0:128],
                                    kbart[:, ci * 1024 + h * 128:ci * 1024 + (h + 1) * 128],
                                    vnat[:, ci * 1024 + h * 128:ci * 1024 + (h + 1) * 128],
                                    start=True, stop=True)
                pr.seq("t", f, "a2z")
                pr.wait("v", "a2z")
                V.scalar_tensor_tensor(Sst[:, h * 128:(h + 1) * 128],
                                       Sst[:, h * 128:(h + 1) * 128],
                                       ebendF[:, g:g + 1], PS[0][:, 0:128], OP.mult, OP.add)
                for r in range(4):
                    V.tensor_copy(ebt4[:, r * 128:(r + 1) * 128], ebt[:, g * 128:(g + 1) * 128])
                pr.wait("c", "a2q%d" % (ci % 2))
                pr.wait("c", "v_self")
                xqc = Cc.activation(exq4, qT4, AF.Exp)
                pr.wait("t", "c_self", xqc)
                pr.wait("t", "v_self")
                pr.seq("t", lambda t, exq4=exq4: t.matmul(PS[4][0:1, 0:512], ct["ones_col"], exq4,
                                                          start=True, stop=True), "a2cq")
                pr.wait("v", "a2cq")
                V.reciprocal(a2row[0:1, 0:512], PS[4][0:1, 0:512])
                rqc = V.tensor_copy(rqb4[0:1, 0:512], a2row[0:1, 0:512])
                pr.wait("t", "v_self", rqc)
                pr.seq("t", lambda t, rqb4=rqb4: t.matmul(PS[5][:, 0:512], ct["ones_row"],
                                                          rqb4[0:1, 0:512], start=True, stop=True), "a2xq")
                pr.seq("t", lambda t, qT4=qT4: t.matmul(PS[2][:, 0:512], ct["perm"], qT4,
                                                        start=True, stop=True), "a2rp")
                pr.wait("v", "a2rp")
                V.tensor_copy(rqt4, PS[2][:, 0:512])
                V.tensor_tensor(qs4, qT4, cosb4[:, ci * 512:(ci + 1) * 512], OP.mult)
                V.tensor_tensor(rqt4, rqt4, sinb4[:, ci * 512:(ci + 1) * 512], OP.mult)
                V.tensor_tensor(rqt4, qs4, rqt4, OP.add)
                qsv = V.tensor_tensor(qs4, exq4, ebt4[:, :], OP.mult)
                pr.wait("t", "v_self", qsv)
                pr.seq("t", lambda t, g=g, qs4=qs4: t.matmul(PS[0][:, 0:512],
                                                             kst[:, g * 128:(g + 1) * 128], qs4,
                                                             start=True, stop=True), "a2at")
                pr.wait("v", "a2at")
                amv = V.scalar_tensor_tensor(Am4, PS[0][:, 0:512], rkcols[:, g:g + 1],
                                             ct["triu4"], OP.mult, OP.mult)
                pr.wait("t", "v_self", amv)
                def f(t, ci=ci, h=h, Am4=Am4, qs4=qs4):
                    t.matmul(PS[1][:, 0:512],
                             vnat[:, ci * 1024 + h * 128:ci * 1024 + (h + 1) * 128], Am4,
                             start=True, stop=False)
                    return t.matmul(PS[1][:, 0:512], Scast[:, :], qs4, start=False, stop=True)
                pr.seq("t", f, "a2ol")
                if ci == 0:
                    pk = rkhalo[:, h * 64:(h + 1) * 64]
                    pv_ = vhalo[0:64, h * 128:(h + 1) * 128]
                    msk1 = m1c0_t
                    W1 = 64
                else:
                    pk = ropekt[:, ((ci - 1) * 8 + h) * 128:((ci - 1) * 8 + h + 1) * 128]
                    pv_ = vnat[:, (ci - 1) * 1024 + h * 128:(ci - 1) * 1024 + (h + 1) * 128]
                    msk1 = ct["m1q"]
                    W1 = 128
                def f(t, pk=pk, rqt4=rqt4, W1=W1):
                    return t.matmul(PS[2][0:W1, 0:512], pk, rqt4, start=True, stop=True)
                pr.seq("t", f, "a2s1")
                def f(t, g=g, rqt4=rqt4):
                    return t.matmul(PS[3][:, 0:512],
                                    ropekt[:, g * 128:(g + 1) * 128], rqt4, start=True, stop=True)
                pr.seq("t", f, "a2s2")
                pr.wait("c", "a2s1")
                pr.wait("c", "a2s2")
                ex1 = Cc.activation(e1[0:W1, 0:512], PS[2][0:W1, 0:512], AF.Exp,
                                    scale=cpool[0:W1, CP_ISQD:CP_ISQD + 1])
                ex2 = Cc.activation(e2, PS[3][:, 0:512], AF.Exp,
                                    scale=cpool[:, CP_ISQD:CP_ISQD + 1])
                pr.wait("v", "c_self", ex2)
                m1v = V.tensor_tensor(e1[0:W1, 0:512], e1[0:W1, 0:512],
                                      msk1[0:W1, :], OP.mult)
                m2v = V.tensor_tensor(e2, e2, ct["m2q"], OP.mult)
                pr.wait("t", "v_self", m2v)
                def f(t, e1=e1, e2=e2, W1=W1):
                    t.matmul(PS[4][0:1, 0:512], ct["ones_col"][0:W1, :], e1[0:W1, 0:512],
                             start=True, stop=False)
                    return t.matmul(PS[4][0:1, 0:512], ct["ones_col"], e2, start=False, stop=True)
                pr.seq("t", f, "a2cs")
                def f(t, e1=e1, e2=e2, ci=ci, h=h, pv_=pv_, W1=W1):
                    t.matmul(PS[6][:, 0:512], pv_, e1[0:W1, 0:512], start=True, stop=False)
                    return t.matmul(PS[6][:, 0:512],
                                    vnat[:, ci * 1024 + h * 128:ci * 1024 + (h + 1) * 128], e2,
                                    start=False, stop=True)
                pr.seq("t", f, "a2pv")
                pr.wait("v", "a2cs")
                V.reciprocal(a2row[0:1, 512:1024], PS[4][0:1, 0:512])
                rsc = V.tensor_copy(rqb4[0:1, 512:1024], a2row[0:1, 512:1024])
                pr.wait("t", "v_self", rsc)
                pr.seq("t", lambda t, rqb4=rqb4: t.matmul(PS[4][:, 0:512], ct["ones_row"],
                                                          rqb4[0:1, 512:1024], start=True, stop=True), "a2xs")
                pr.wait("v", "a2xs")
                pr.wait("v", "a2xq")
                pr.wait("v", "a2pv")
                pr.wait("v", "a2ol")
                V.tensor_copy(wkF[:, 0:512], PS[5][:, 0:512])
                V.tensor_copy(wkF[:, 512:1024], PS[4][:, 0:512])
                V.tensor_tensor(wkF[:, 512:1024], PS[6][:, 0:512], wkF[:, 512:1024], OP.mult)
                V.tensor_tensor(wkF[:, 0:512], PS[1][:, 0:512], wkF[:, 0:512], OP.mult)
                V.tensor_tensor(wkF[:, 0:512], wkF[:, 0:512], wkF[:, 512:1024], OP.add)
                V.tensor_scalar(oT[:, h * 512:(h + 1) * 512], wkF[:, 0:512],
                                cpool[:, CP_HALF:CP_HALF + 1], None, OP.mult)
            ov = pr.cnt["v_self"]
            pr.wait("s", "v_self", ov)
            pr.dma("s", bass.AP(odr, ci * 128, [[NT, 128], [128 * NT, 32], [1, 128]]),
                   ap3(oT[:, :], 0, [[128, 32], [1, 128]]), "a2o")
        pr.wait("s", "a2o")

    def phaseWo(sw):
        V = pr.chained("v")
        oTf = sw["oTf"]; wcol = sw["wcol"]; aev = sw["aev"]
        pr.dma("g", ap3(oTf[:, :], 0, [[1024, 32], [1, 1024]]),
               bass.AP(odr, 0, [[NT, 128], [128 * NT, 32], [1, 1024]]), "woo")
        pr.wait("t", "woo")
        for m in range(32):
            b = m % 2
            if m >= 2:
                pr.wait("g", "womm")
                pr.wait("t", "v_self")
            pr.dma("g", ap3(wcol[:, :], b * 4096, [[128, 32], [1, 128]]),
                   bass.AP(Wo, m * 128, [[D, 128], [128 * D, 32], [1, 128]]), "wcW")
            pr.wait("t", "wcW")
            def f(t, b=b):
                last = None
                for nn in range(2):
                    for k in range(32):
                        last = t.matmul(PS[2 * b + nn][:, 0:512],
                                        wcol[:, b * 4096 + k * 128:b * 4096 + (k + 1) * 128],
                                        oTf[:, k * 1024 + nn * 512:k * 1024 + (nn + 1) * 512],
                                        start=(k == 0), stop=(k == 31))
                return last
            pr.seq("t", f, "womm")
            pr.wait("v", "womm")
            if m >= 2:
                pr.wait("v", "wost%d" % b)
            V.tensor_copy(aev[:, (b * 2) * 512:(b * 2 + 1) * 512], PS[2 * b][:, 0:512])
            wov = V.tensor_copy(aev[:, (b * 2 + 1) * 512:(b * 2 + 2) * 512], PS[2 * b + 1][:, 0:512])
            pr.wait("s", "v_self", wov)
            pr.dma("s", bass.AP(attnTd, m * 128 * NT, [[NT, 128], [1, 512]]),
                   aev[:, (b * 2) * 512:(b * 2 + 1) * 512], "wost%d" % b)
            pr.dma("s", bass.AP(attnTd, m * 128 * NT + 512, [[NT, 128], [1, 512]]),
                   aev[:, (b * 2 + 1) * 512:(b * 2 + 2) * 512], "wost%d" % b)
        pr.wait("s", "wost0")
        pr.wait("s", "wost1")

    def emit_ln(resA, V, Cc, tag):
        pr.wait("c", "v_self")
        for k in range(32):
            if k >= 2:
                pr.wait("c", tag + "st")
            xc = Cc.activation(wkD[:, (k % 2) * 512:(k % 2 + 1) * 512],
                               resA[:, k * 512:(k + 1) * 512], AF.Copy)
            sq = Cc.activation(wkD[:, 1024 + (k % 2) * 512:1024 + (k % 2 + 1) * 512],
                               resA[:, k * 512:(k + 1) * 512], AF.Square)
            pr.wait("t", "c_self", sq)
            def f(t, k=k):
                t.matmul(PS[5][0:1, 0:512], ct["ones_col"],
                         wkD[:, (k % 2) * 512:(k % 2 + 1) * 512], start=(k == 0), stop=(k == 31))
                return t.matmul(PS[6][0:1, 0:512], ct["ones_col"],
                                wkD[:, 1024 + (k % 2) * 512:1024 + (k % 2 + 1) * 512],
                                start=(k == 0), stop=(k == 31))
            pr.seq("t", f, tag + "st")
        pr.wait("v", tag + "st")
        s0 = global_stat[0][0:1, :]
        cp0 = cpool[0:1, :]
        V.tensor_scalar(s0[0:1, 0:512], PS[5][0:1, 0:512], cp0[0:1, CP_INVD:CP_INVD + 1], None, OP.mult)
        V.tensor_scalar(s0[0:1, 512:1024], PS[6][0:1, 0:512], cp0[0:1, CP_INVD:CP_INVD + 1], None, OP.mult)
        V.tensor_tensor(s0[0:1, 1536:2048], s0[0:1, 0:512], s0[0:1, 0:512], OP.mult)
        V.tensor_tensor(s0[0:1, 1024:1536], s0[0:1, 512:1024], s0[0:1, 1536:2048], OP.subtract)
        V.tensor_scalar(s0[0:1, 1024:1536], s0[0:1, 1024:1536], cp0[0:1, CP_EPS:CP_EPS + 1], None, OP.add)
        rc = V.reciprocal(s0[0:1, 1536:2048], s0[0:1, 1024:1536])
        pr.wait("c", "v_self", rc)
        sqc = Cc.activation(s0[0:1, 2048:2560], s0[0:1, 1536:2048], AF.Sqrt)
        pr.wait("v", "c_self", sqc)
        V.tensor_tensor(s0[0:1, 1536:2048], s0[0:1, 2048:2560], s0[0:1, 2048:2560], OP.mult)
        V.tensor_tensor(s0[0:1, 1536:2048], s0[0:1, 1536:2048], s0[0:1, 1024:1536], OP.mult)
        V.tensor_scalar(s0[0:1, 1536:2048], s0[0:1, 1536:2048], cp0[0:1, CP_NHALF:CP_NHALF + 1],
                        cp0[0:1, CP_1P5:CP_1P5 + 1], OP.mult, OP.add)
        V.tensor_tensor(s0[0:1, 2048:2560], s0[0:1, 1536:2048], s0[0:1, 2048:2560], OP.mult)
        bv = V.tensor_tensor(s0[0:1, 2560:3072], s0[0:1, 2048:2560], s0[0:1, 0:512], OP.mult)
        pr.wait("g", "v_self", bv)
        pr.dma("g", rowsd.ap()[2:3, 0:512], s0[0:1, 2048:2560], tag + "b1")
        pr.dma("g", rowsd.ap()[3:4, 0:512], s0[0:1, 2560:3072], tag + "b1")
        pr.wait("g", tag + "b1")
        statB = global_statB[0]
        pr.dma("g", statB[:, 0:512], bass.AP(rowsd, 2 * 8192, [[0, 128], [1, 512]]), tag + "b2")
        pr.dma("g", statB[:, 512:1024], bass.AP(rowsd, 3 * 8192, [[0, 128], [1, 512]]), tag + "b2")
        pr.wait("v", tag + "b2")
        for k in range(32):
            V.tensor_tensor(resA[:, k * 512:(k + 1) * 512], resA[:, k * 512:(k + 1) * 512],
                            statB[:, 0:512], OP.mult)
            V.tensor_tensor(resA[:, k * 512:(k + 1) * 512], resA[:, k * 512:(k + 1) * 512],
                            statB[:, 512:1024], OP.subtract)

    def phaseA3(sa3):
        V = pr.chained("v")
        Cc = pr.chained("c")
        resA = sa3["resA"]; xroll = sa3["xroll"]
        for H in range(2):
            t0 = H * 512
            if H == 1:
                pr.wait("s", "a3o")
            pr.dma("s", ap3(resA[:, :], 0, [[512, 32], [1, 512]]),
                   bass.AP(attnTd, t0, [[NT, 128], [128 * NT, 32], [1, 512]]), "a3a")
            pr.wait("v", "a3a")
            for k in range(32):
                b = k % 2
                if k >= 2:
                    pr.wait("s", "v_self")
                pr.dma("s", xroll[:, b * 512:(b + 1) * 512],
                       bass.AP(xT, k * 128 * NT + t0, [[NT, 128], [1, 512]]), "a3x%d" % b)
                pr.wait("v", "a3x%d" % b)
                V.tensor_tensor(resA[:, k * 512:(k + 1) * 512], resA[:, k * 512:(k + 1) * 512],
                                xroll[:, b * 512:(b + 1) * 512], OP.add)
            if not debug.get("noln"):
                emit_ln(resA, V, Cc, "l1h%d" % H)
            lv = pr.cnt["v_self"]
            pr.wait("s", "v_self", lv)
            pr.dma("s", bass.AP(x1d, t0, [[NT, 128], [128 * NT, 32], [1, 512]]),
                   ap3(resA[:, :], 0, [[512, 32], [1, 512]]), "a3o")
        pr.wait("s", "a3o")
        if "x1out" in dbg_outs:
            pr.wait("g", "a3o")
            pr.dma("g", dbg_outs["x1out"].ap(), x1d.ap(), "a3d")
            if "adiag2" in dbg_outs:
                pr.dma("g", dbg_outs["adiag2"].ap()[0:1, :], global_stat[0][0:1, :], "a3d")
                pr.dma("g", dbg_outs["adiag2"].ap()[1:3, 0:1024],
                       global_statB[0][0:2, 0:1024], "a3d")
            pr.wait("g", "a3d")
        if "qout" in dbg_outs:
            pr.dma("g", dbg_outs["qout"].ap(), qdr.ap(), "a3d")
            pr.dma("g", dbg_outs["oout"].ap(), odr.ap(), "a3d")
            pr.dma("g", dbg_outs["attnout"].ap(), attnTd.ap(), "a3d")
            pr.wait("g", "a3d")

    def phaseB(sb_):
        V = pr.chained("v")
        Cc = pr.chained("c")
        resB = sb_["resB"]; xbB = sb_["xbB"]; AfB = sb_["AfB"]; BwB = sb_["BwB"]
        rtT = sb_["rtT"]; lg8 = sb_["lg8"]; lgT = sb_["lgT"]; eqT = sb_["eqT"]
        mxc = sb_["mxc"]; rw8 = sb_["rw8"]; rwXs = sb_["rwXs"]; whB = sb_["whB"]
        pr.dma("g", ap3(AfB[:, :], 0, [[128, 32], [1, 128]]),
               bass.AP(Aflat, 0, [[128, 128], [128 * 128, 32], [1, 128]]), "bwl")
        pr.dma("g", ap3(BwB[:, :], 0, [[128, 32], [1, 128]]),
               bass.AP(BwT, 0, [[D, 128], [128, 32], [1, 128]]), "bwl")
        pr.dma("s", ap3(rtT[:, :], 0, [[8, 32], [1, 8]]),
               bass.AP(routeT, 0, [[8, 128], [8 * 128, 32], [1, 8]]), "brt")
        pr.wait("t", "bwl")
        pr.wait("t", "brt")
        for H in range(2):
            t0 = H * 512
            if H == 1:
                pr.wait("s", "bxo")
                pr.wait("g", "bmmh")
                pr.wait("t", "v_self")
            pr.dma("s", ap3(resB[:, :], 0, [[512, 32], [1, 512]]),
                   bass.AP(x1d, t0, [[NT, 128], [128 * NT, 32], [1, 512]]), "bra")
            pr.dma("g", ap3(xbB[:, :], 0, [[512, 32], [1, 512]]),
                   bass.AP(x1d, t0, [[NT, 128], [128 * NT, 32], [1, 512]]), "bxb")
            pr.wait("t", "bra")
            pr.wait("t", "bxb")
            def f(t):
                last = None
                for k in range(32):
                    last = t.matmul(PS[0][0:8, 0:512], rtT[:, k * 8:(k + 1) * 8],
                                    resB[:, k * 512:(k + 1) * 512],
                                    start=(k == 0), stop=(k == 31))
                return last
            pr.seq("t", f, "blg")
            def f(t):
                last = None
                for k in range(32):
                    last = t.matmul(PS[3][:, 0:512], AfB[:, k * 128:(k + 1) * 128],
                                    xbB[:, k * 512:(k + 1) * 512],
                                    start=(k == 0), stop=(k == 31))
                return last
            pr.seq("t", f, "bmmh")
            pr.wait("v", "blg")
            lv8 = V.tensor_copy(lg8[0:8, 0:512], PS[0][0:8, 0:512])
            pr.wait("t", "v_self", lv8)
            def f(t):
                last = None
                for j in range(4):
                    last = t.transpose(PS[1][:, j * 8:(j + 1) * 8],
                                       lg8[0:8, j * 128:(j + 1) * 128], ct["identf"][0:8, 0:8])
                return last
            pr.seq("t", f, "btr1")
            pr.wait("v", "btr1")
            V.tensor_copy(lgT[:, 0:32], PS[1][:, 0:32])
            V.reduce_max(mxc[:, 0:4], ap3(lgT[:, 0:32], 0, [[8, 4], [1, 8]]), axis=AX.X)
            for j in range(4):
                V.tensor_scalar(eqT[:, j * 8:(j + 1) * 8], lgT[:, j * 8:(j + 1) * 8],
                                mxc[:, j:j + 1], None, OP.is_equal)
                V.tensor_scalar(eqT[:, j * 8:(j + 1) * 8], eqT[:, j * 8:(j + 1) * 8],
                                mxc[:, j:j + 1], None, OP.mult)
            eqv = pr.cnt["v_self"]
            pr.wait("t", "v_self", eqv)
            def f(t):
                last = None
                for j in range(4):
                    last = t.transpose(PS[2][0:8, j * 128:(j + 1) * 128],
                                       eqT[:, j * 8:(j + 1) * 8], ct["identf"])
                return last
            pr.seq("t", f, "btr2")
            pr.wait("v", "btr2")
            rwv = V.tensor_copy(rw8[0:8, 0:512], PS[2][0:8, 0:512])
            if "rwout" in dbg_outs:
                pr.wait("g", "v_self", rwv)
                pr.dma("g", bass.AP(dbg_outs["rwout"], t0, [[NT, 8], [1, 512]]),
                       rw8[0:8, 0:512], "bdg")
            pr.wait("t", "v_self", rwv)
            pr.seq("t", lambda t: t.matmul(PS[4][:, 0:512], ct["blocksel"], rw8[0:8, 0:512],
                                           start=True, stop=True), "brwx")
            pr.wait("v", "brwx")
            V.tensor_copy(rwXs[:, 0:512], PS[4][:, 0:512])
            pr.wait("v", "bmmh")
            whv = V.tensor_tensor(whB[:, 0:512], PS[3][:, 0:512], rwXs[:, 0:512], OP.mult)
            pr.wait("t", "v_self", whv)
            addc = {}
            for m in range(32):
                b = m % 2
                if m >= 2:
                    pr.wait("t", "v_self", addc[m - 2])
                def f(t, m=m, b=b):
                    return t.matmul(PS[b][:, 0:512], BwB[:, m * 128:(m + 1) * 128],
                                    whB[:, 0:512], start=True, stop=True)
                pr.seq("t", f, "bout")
                pr.wait("v", "bout")
                addc[m] = V.tensor_tensor(resB[:, m * 512:(m + 1) * 512],
                                          resB[:, m * 512:(m + 1) * 512],
                                          PS[b][:, 0:512], OP.add)
            emit_ln(resB, V, Cc, "l2h%d" % H)
            lv = pr.cnt["v_self"]
            pr.wait("s", "v_self", lv)
            pr.dma("s", bass.AP(x2d, t0, [[NT, 128], [128 * NT, 32], [1, 512]]),
                   ap3(resB[:, :], 0, [[512, 32], [1, 512]]), "bxo")
        pr.wait("s", "bxo")
        if "x2out" in dbg_outs:
            pr.wait("g", "bxo")
            pr.dma("g", dbg_outs["x2out"].ap(), x2d.ap(), "bdg")
            pr.wait("g", "bdg")

    def phaseC(sc_):
        V = pr.chained("v")
        Cc = pr.chained("c")
        resC = sc_["resC"]; xcB = sc_["xcB"]; TbB = sc_["TbB"]; HB = sc_["HB"]
        pr.dma("g", ap3(HB[:, :], 0, [[128, 32], [1, 128]]),
               bass.AP(Kpad, 0, [[1, 128], [128, 32], [1, 128]]), "chb")
        pr.wait("t", "chb")
        for j in range(8):
            b = j % 2
            if j >= 2:
                pr.wait("t", "v_self")
            pr.seq("t", lambda t, j=j, b=b: t.matmul(PS[2 + 2 * b][:, 0:512], ct["rev"],
                                                     HB[:, j * 512:(j + 1) * 512],
                                                     start=True, stop=True), "crev")
            pr.wait("v", "crev")
            V.tensor_copy(TbB[:, j * 512:(j + 1) * 512], PS[2 + 2 * b][:, 0:512])
        for H in range(2):
            t0 = H * 512
            if H == 1:
                pr.wait("s", "cxo")
                pr.wait("g", "cmm")
            pr.dma("s", ap3(resC[:, :], 0, [[512, 32], [1, 512]]),
                   bass.AP(x2d, t0, [[NT, 128], [128 * NT, 32], [1, 512]]), "cra")
            pr.dma("g", ap3(xcB[:, :], 0, [[512, 32], [1, 512]]),
                   bass.AP(x2d, t0, [[NT, 128], [128 * NT, 32], [1, 512]]), "cxb")
            pr.wait("t", "cxb")
            pr.wait("t", "v_self")
            pr.wait("v", "cra")
            addc = {}
            for jd in range(32):
                b = jd % 2
                if jd >= 2:
                    pr.wait("t", "v_self", addc[jd - 2])
                def f(t, jd=jd, b=b):
                    last = None
                    for d in range(jd + 1):
                        id_ = jd - d
                        last = t.matmul(PS[b][:, 0:512], TbB[:, d * 128:(d + 1) * 128],
                                        xcB[:, id_ * 512:(id_ + 1) * 512],
                                        start=(d == 0), stop=(d == jd))
                    return last
                pr.seq("t", f, "cmm")
                pr.wait("v", "cmm")
                addc[jd] = V.scalar_tensor_tensor(resC[:, jd * 512:(jd + 1) * 512],
                                                  resC[:, jd * 512:(jd + 1) * 512],
                                                  cpool[:, CP_D1:CP_D1 + 1],
                                                  PS[b][:, 0:512], OP.mult, OP.add)
            emit_ln(resC, V, Cc, "l3h%d" % H)
            lv = pr.cnt["v_self"]
            pr.wait("s", "v_self", lv)
            pr.dma("s", bass.AP(x3d, t0, [[NT, 128], [128 * NT, 32], [1, 512]]),
                   ap3(resC[:, :], 0, [[512, 32], [1, 512]]), "cxo")
        pr.wait("s", "cxo")
        if "x3out" in dbg_outs:
            pr.wait("g", "cxo")
            pr.dma("g", dbg_outs["x3out"].ap(), x3d.ap(), "cdg")
            pr.wait("g", "cdg")

    def mk_dq(sq_):
        V = pr.chained("v")
        Cc = pr.chained("c")
        qt = sq_["qt"]; msk = sq_["msk"]; acc = sq_["acc"]
        sct = sq_["sct"]; wout = sq_["wout"]
        jobs = [(w1qT, s1T, SH12, 32, agw_in[1]), (w2qT, s2T, SH12, 32, agw_in[2]),
                (w3qT, s3T, SH3, 86, agw_in[3])]
        tiles = [(qsrc, ssrc, ncols, kt, dst)
                 for (qsrc, ssrc, ncols, ntiles, dst) in jobs for kt in range(ntiles)]
        st = {"i": 0, "copyc": {}, "lastact": {}, "lastwv": {}}

        def emit_tile():
            i = st["i"]
            if i >= len(tiles):
                return False
            qsrc, ssrc, ncols, kt, dst = tiles[i]
            copyc, lastact, lastwv = st["copyc"], st["lastact"], st["lastwv"]
            sl = i % 2
            if i >= 2:
                pr.wait("g", "c_self", lastact[sl])
                pr.wait("g", "v_self", lastwv[sl])
            pr.dma("g", qt[:, sl * 1408:sl * 1408 + ncols],
                   bass.AP(qsrc, kt * 128 * ncols, [[ncols, 128], [1, ncols]]),
                   "qst%d" % sl)
            pr.dma("g", sct[:, sl * 1408:sl * 1408 + ncols],
                   bass.AP(ssrc, 2 * kt * ncols, [[ncols, 2], [0, 64], [1, ncols]]),
                   "a3x%d" % sl)
            pr.wait("v", "qst%d" % sl)
            V.memset(acc[:, 0:ncols], float(NF4_BF[0]))
            pr.wait("c", "qst%d" % sl)
            for v in range(1, 16):
                mb = v % 2
                gate = copyc.get((i, v - 2)) if v >= 3 else copyc.get((i - 1, 16 - v))
                if gate:
                    pr.wait("c", "v_self", gate)
                actc = Cc.activation(msk[:, mb * 1408:mb * 1408 + ncols],
                                     qt[:, sl * 1408:sl * 1408 + ncols],
                                     AF.Relu, bias=cpool[:, v:v + 1])
                pr.wait("v", "c_self", actc)
                copyc[(i, v)] = V.copy_predicated(
                    acc[:, 0:ncols], msk[:, mb * 1408:mb * 1408 + ncols],
                    tpool[:, v:v + 1].to_broadcast([128, ncols]))
            lastact[sl] = pr.cnt["c_self"]
            pr.wait("v", "a3x%d" % sl)
            pr.wait("v", "wost%d" % sl)
            lastwv[sl] = V.tensor_tensor(wout[:, sl * 1408:sl * 1408 + ncols],
                                         acc[:, 0:ncols],
                                         sct[:, sl * 1408:sl * 1408 + ncols], OP.mult)
            pr.wait("s", "v_self", lastwv[sl])
            pr.dma("s", bass.AP(dst, kt * 128 * ncols, [[ncols, 128], [1, ncols]]),
                   wout[:, sl * 1408:sl * 1408 + ncols], "wost%d" % sl)
            st["i"] = i + 1
            return True

        def emit_ag(wi):
            pr.wait("g", "wost0")
            pr.wait("g", "wost1")
            def f(g, wi=wi):
                return g.collective_compute(
                    "AllGather", OP.bypass, replica_groups=[list(range(NC))],
                    ins=[agw_in[wi].ap().opt()], outs=[agw_out[wi].ap().opt()])
            pr.seq("g", f, "agd")

        return emit_tile, emit_ag

    def phaseD1(sd_, dq=None):
        V = pr.chained("v")
        Cc = pr.chained("c")
        xdB = sd_["xdB"]; wcolD = sd_["wcolD"]; silB = sd_["silB"]; hv = sd_["hv"]
        agbase = pr.cnt["agd"]
        pr.dma("g", ap3(xdB[:, :], 0, [[1024, 32], [1, 1024]]),
               bass.AP(x3d, 0, [[NT, 128], [128 * NT, 32], [1, 1024]]), "woo")
        pr.wait("t", "woo")
        pr.wait("g", "agd", agbase)
        prevc = {}
        pr.sem("d1mm")
        d1mmb = pr.cnt["d1mm"]
        for tg in range(86):
            j, tt = tg // 11, tg % 11
            sl = tg % 2
            base = sl * 8192
            if dq:
                dq(tg)
            if tg >= 2:
                pr.wait("g", "d1mm", d1mmb + tg - 1)
            for wi in (1, 2):
                pr.dma("g", ap3(wcolD[:, :], base + (wi - 1) * 4096, [[128, 32], [1, 128]]),
                       bass.AP(agw_out[wi], j * D * SH12 + tt * 128,
                               [[SH12, 128], [128 * SH12, 32], [1, 128]]), "a2q%d" % sl)
            pr.wait("t", "a2q%d" % sl)
            if tg >= 1:
                pc_, pv_ = prevc[tg - 1]
                pr.wait("t", "c_self", pc_)
                pr.wait("t", "v_self", pv_)
            def f(t, base=base):
                last = None
                for wi in range(2):
                    for k in range(32):
                        for g2 in range(2):
                            last = t.matmul(PS[2 * wi + g2][:, 0:512],
                                            wcolD[:, base + wi * 4096 + k * 128:
                                                  base + wi * 4096 + (k + 1) * 128],
                                            xdB[:, k * 1024 + g2 * 512:k * 1024 + (g2 + 1) * 512],
                                            start=(k == 0), stop=(k == 31))
                return last
            pr.seq("t", f, "d1mm")
            pr.wait("c", "d1mm")
            if tg >= 2:
                pr.wait("c", "v_self", prevc[tg - 2][1])
            sc_ = None
            for g2 in range(2):
                sc_ = Cc.activation(silB[:, sl * 1024 + g2 * 512:sl * 1024 + (g2 + 1) * 512],
                                    PS[g2][:, 0:512], AF.Silu)
            pr.wait("v", "c_self", sc_)
            pr.wait("v", "bra" if sl == 0 else "cra")
            hvc = None
            for g2 in range(2):
                hvc = V.tensor_tensor(hv[:, sl * 1024 + g2 * 512:sl * 1024 + (g2 + 1) * 512],
                                      silB[:, sl * 1024 + g2 * 512:sl * 1024 + (g2 + 1) * 512],
                                      PS[2 + g2][:, 0:512], OP.mult)
            prevc[tg] = (sc_, hvc)
            pr.wait("s", "v_self", hvc)
            pr.dma("s", bass.AP(hdr, tg * 128 * NT, [[NT, 128], [1, 1024]]),
                   hv[:, sl * 1024:(sl + 1) * 1024], "bra" if sl == 0 else "cra")
        pr.wait("s", "bra")
        pr.wait("s", "cra")

    def phaseD2(sd_):
        V = pr.chained("v")
        Cc = pr.chained("c")
        hres = sd_["hres"]; resD = sd_["resD"]; w3col = sd_["w3col"]
        pr.wait("g", "bra")
        pr.wait("g", "cra")
        pr.wait("g", "agd")
        cmmb = {"v": 0}

        def em_load(s):
            m, hk = s // 2, s % 2
            j, dj = (m * 128) // 512, (m * 128) % 512
            nm = "kld" if hk == 0 else "kld2"
            if s >= 2:
                pr.wait("g", "cmm", cmmb["v"] + s - 1)
            pr.dma("g", ap3(w3col[:, :], hk * 5504, [[128, 43], [1, 128]]),
                   bass.AP(agw_out[3], j * HID * SH3 + dj + hk * 43 * 128 * SH3,
                           [[SH3, 128], [128 * SH3, 43], [1, 128]]), nm)

        for H in range(2):
            t0 = H * 512
            if H == 1:
                pr.wait("s", "bxo")
                pr.wait("g", "cmm")
                pr.wait("t", "v_self")
            pr.dma("g", ap3(hres[:, :], 0, [[512, 86], [1, 512]]),
                   bass.AP(hdr, t0, [[NT, 128], [128 * NT, 86], [1, 512]]), "cxb")
            pr.dma("s", ap3(resD[:, :], 0, [[512, 32], [1, 512]]),
                   bass.AP(x3d, t0, [[NT, 128], [128 * NT, 32], [1, 512]]), "bxb")
            pr.wait("t", "cxb")
            pr.wait("v", "bxb")
            cmmb["v"] = pr.cnt["cmm"]
            em_load(0)
            em_load(1)
            addc = {}
            for m in range(32):
                b = m % 2
                if m >= 2:
                    pr.wait("t", "v_self", addc[m - 2])
                for hk in range(2):
                    s = 2 * m + hk
                    pr.wait("t", "kld" if hk == 0 else "kld2")
                    def f(t, b=b, hk=hk):
                        last = None
                        for k in range(43):
                            last = t.matmul(PS[b][:, 0:512],
                                            w3col[:, hk * 5504 + k * 128:hk * 5504 + (k + 1) * 128],
                                            hres[:, (43 * hk + k) * 512:(43 * hk + k + 1) * 512],
                                            start=(hk == 0 and k == 0), stop=(hk == 1 and k == 42))
                        return last
                    pr.seq("t", f, "cmm")
                    if s + 2 < 64:
                        em_load(s + 2)
                pr.wait("v", "cmm")
                addc[m] = V.tensor_tensor(resD[:, m * 512:(m + 1) * 512],
                                          resD[:, m * 512:(m + 1) * 512],
                                          PS[b][:, 0:512], OP.add)
            emit_ln(resD, V, Cc, "l2h%d" % H)
            lv = pr.cnt["v_self"]
            pr.wait("s", "v_self", lv)
            pr.dma("s", bass.AP(out, t0, [[NT, 128], [128 * NT, 32], [1, 512]]),
                   ap3(resD[:, :], 0, [[512, 32], [1, 512]]), "bxo")
        pr.wait("s", "bxo")
        pr.wait("g", "bxo")
        pr.wait("t", "bxo")

    from contextlib import ExitStack
    if debug.get("upto", 99) >= 1:
        with ExitStack() as stk:
            mk = lambda n, s, dt: stk.enter_context(nc.sbuf_tensor(n, s, dt))
            sa = dict(
                kst=mk("pa_kst", [128, 8192], BF16),
                kbart=mk("pa_kbart", [128, 8192], BF16),
                ropekt=mk("pa_ropekt", [128, 8192], BF16),
                vnat=mk("pa_vnat", [128, 8192], BF16),
                ebt=mk("pa_ebt", [128, 8192], BF16),
                haloT=mk("pa_haloT", [128, 1024], BF16),
                vhalo=mk("pa_vhalo", [64, 1024], BF16),
                rkhalo=mk("pa_rkhalo", [128, 512], BF16),
                Sst=mk("pa_Sst", [128, 1024], F32),
                Zrun=mk("pa_Zrun", [128, 1024], F32),
                Drun=mk("pa_Drun", [128, 8], F32),
                rkcols=mk("pa_rkcols", [128, 64], F32),
                ebendF=mk("pa_ebendF", [128, 64], F32),
                cosb=mk("pa_cosb", [128, NT + 64], BF16),
                sinb=mk("pa_sinb", [128, NT + 64], BF16),
                wkF=mk("pa_wkF", [128, 1024], F32),
            )
            with ExitStack() as stk2:
                mk2 = lambda n, s, dt: stk2.enter_context(nc.sbuf_tensor(n, s, dt))
                sa.update(
                    xb=mk2("pa_xb", [128, 16384], BF16),
                    xbh=mk2("pa_xbh", [128, 2048], BF16),
                    kvT=mk2("pa_kvT", [128, 8192], BF16),
                    wcol=mk2("pa_wcol", [128, 8192], BF16),
                    qtmp=mk2("pa_qtmp", [128, 1024], BF16),
                    crow=mk2("pa_crow", [1, 2048], F32),
                    rkraw=mk2("pa_rkraw", [128, 16], F32),
                )
                phaseA(sa)
            pr.barrier()
            with ExitStack() as stk2:
                mk2 = lambda n, s, dt: stk2.enter_context(nc.sbuf_tensor(n, s, dt))
                sa.update(
                    qch=mk2("pa_qch", [128, 8192], BF16),
                    oT=mk2("pa_oT", [128, 4096], BF16),
                    Scast=mk2("pa_Scast", [128, 128], BF16),
                    cosb4=mk2("pa_cosb4", [128, 4096], BF16),
                    sinb4=mk2("pa_sinb4", [128, 4096], BF16),
                    e12=mk2("pa_e12", [128, 1024], BF16),
                    ebt4=mk2("pa_ebt4", [128, 512], BF16),
                    a2row=mk2("pa_a2row", [1, 1024], F32),
                    rqb4=mk2("pa_rqb4", [1, 1024], BF16),
                )
                phaseA2(sa)
            pr.barrier()
        with ExitStack() as stk:
            mk = lambda n, s, dt: stk.enter_context(nc.sbuf_tensor(n, s, dt))
            phaseWo(dict(oTf=mk("pw_oTf", [128, 32768], BF16),
                         wcol=mk("pw_wcol", [128, 8192], BF16),
                         aev=mk("pw_aev", [128, 2048], F32)))
        pr.barrier()
        with ExitStack() as stk:
            mk = lambda n, s, dt: stk.enter_context(nc.sbuf_tensor(n, s, dt))
            global_stat[0] = mk("pa3_stat", [1, 3072], F32)
            global_statB[0] = mk("pa3_statB", [128, 1024], F32)
            phaseA3(dict(resA=mk("pa3_res", [128, 16384], F32),
                         xroll=mk("pa3_xr", [128, 1024], F32)))
        pr.barrier()

    if debug.get("upto", 99) >= 2:
        with ExitStack() as stk:
            mk = lambda n, s, dt: stk.enter_context(nc.sbuf_tensor(n, s, dt))
            global_stat[0] = mk("pb_stat", [1, 3072], F32)
            global_statB[0] = mk("pb_statB", [128, 1024], F32)
            phaseB(dict(resB=mk("pb_res", [128, 16384], F32),
                        xbB=mk("pb_xb", [128, 16384], BF16),
                        AfB=mk("pb_Af", [128, 4096], BF16),
                        BwB=mk("pb_Bw", [128, 4096], BF16),
                        rtT=mk("pb_rt", [128, 256], F32),
                        lg8=mk("pb_lg8", [8, 512], F32),
                        lgT=mk("pb_lgT", [128, 32], F32),
                        eqT=mk("pb_eqT", [128, 32], F32),
                        mxc=mk("pb_mxc", [128, 4], F32),
                        rw8=mk("pb_rw8", [8, 512], F32),
                        rwXs=mk("pb_rwXs", [128, 512], F32),
                        whB=mk("pb_whB", [128, 512], BF16)))
        pr.barrier()

    if debug.get("upto", 99) >= 3:
        with ExitStack() as stk:
            mk = lambda n, s, dt: stk.enter_context(nc.sbuf_tensor(n, s, dt))
            global_stat[0] = mk("pc_stat", [1, 3072], F32)
            global_statB[0] = mk("pc_statB", [128, 1024], F32)
            phaseC(dict(resC=mk("pc_res", [128, 16384], F32),
                        xcB=mk("pc_xc", [128, 16384], BF16),
                        TbB=mk("pc_Tb", [128, 4096], BF16),
                        HB=mk("pc_HB", [128, 4096], BF16)))
        pr.barrier()

    if debug.get("upto", 99) >= 4:
        with ExitStack() as stk:
            mk = lambda n, s, dt: stk.enter_context(nc.sbuf_tensor(n, s, dt))
            dq_emit, dq_ag = mk_dq(dict(qt=mk("pd_qt", [128, 2816], BF16),
                                        msk=mk("pd_msk", [128, 2816], mybir.dt.uint8),
                                        acc=mk("pd_acc", [128, 1408], BF16),
                                        sct=mk("pd_sct", [128, 2816], BF16),
                                        wout=mk("pd_wout", [128, 2816], BF16)))
            for _ in range(32):
                dq_emit()
            dq_ag(1)
            for _ in range(32):
                dq_emit()
            dq_ag(2)

            def d1_hook(tg):
                if tg < 43:
                    dq_emit()
                    dq_emit()
                elif tg == 45:
                    dq_ag(3)
            phaseD1(dict(xdB=mk("pd1_x", [128, 32768], BF16),
                         wcolD=mk("pd1_w", [128, 16384], BF16),
                         silB=mk("pd1_s", [128, 2048], BF16),
                         hv=mk("pd1_h", [128, 2048], BF16)), dq=d1_hook)
        pr.barrier()
        with ExitStack() as stk:
            mk = lambda n, s, dt: stk.enter_context(nc.sbuf_tensor(n, s, dt))
            global_stat[0] = mk("pd2_stat", [1, 3072], F32)
            global_statB[0] = mk("pd2_statB", [128, 1024], F32)
            phaseD2(dict(hres=mk("pd2_hr", [128, 44032], BF16),
                         resD=mk("pd2_res", [128, 16384], F32),
                         w3col=mk("pd2_w3", [128, 11008], BF16)))
        pr.barrier()
    with nc.Block() as block:
        pr.emit(block)
    return nc, C, dbg_outs


def host_inputs(inputs, core):
    C = _host_constants()
    beta, sq = core // 4, core % 4
    s0 = sq * NT
    x = inputs["L0_attn_hidden_states"]
    m = {}
    m["xT"] = np.ascontiguousarray(x[beta, s0:s0 + NT, :].T)
    if sq == 0:
        m["xTh"] = np.zeros((D, 64), np.float32)
        m["mask1c0"] = np.zeros((64, 512), np.float32).astype(BF)
    else:
        m["xTh"] = np.ascontiguousarray(x[beta, s0 - 64:s0, :].T)
        jj1 = np.arange(64)[:, None]; ii1 = np.arange(128)[None, :]
        m["mask1c0"] = np.tile(np.where(jj1 >= ii1, 1.0, 0.0), (1, 4)).astype(np.float32).astype(BF)
    m["Wq"] = inputs["L0_attn_Wq"]
    m["Wk"] = inputs["L0_attn_Wk"]
    m["Wv"] = inputs["L0_attn_Wv"]
    m["Wo"] = inputs["L0_attn_Wo"]
    m["routeT"] = np.ascontiguousarray(inputs["L0_ffn_route_w"].T)
    m["Aflat"] = np.ascontiguousarray(inputs["L0_ffn_A"].reshape(128, D).T)
    m["BwT"] = np.ascontiguousarray(inputs["L0_ffn_Bw"].transpose(0, 2, 1).reshape(128, D))
    m["lamr"] = np.ascontiguousarray(inputs["L1_attn_Lambda_ri"].T)
    m["prow"] = np.ascontiguousarray(inputs["L1_attn_p_ri"].T.reshape(1, 128))
    m["qrow"] = np.ascontiguousarray(inputs["L1_attn_q_ri"].T.reshape(1, 128))
    m["VcTre"] = np.ascontiguousarray(inputs["L1_attn_Vc_ri"][:, :, 0].T)
    m["VcTim"] = np.ascontiguousarray(inputs["L1_attn_Vc_ri"][:, :, 1].T)
    m["Ctr"] = np.ascontiguousarray(inputs["L1_attn_Ct_ri"][0].T.reshape(1, 128))
    m["Bp"] = inputs["L1_attn_Bp"]
    m["Dlog"] = np.concatenate([inputs["L1_attn_D"], inputs["L1_attn_log_step"],
                                np.zeros(2, np.float32)]).reshape(1, 4).astype(np.float32)
    w1t = np.zeros((D, HID_PAD), np.int32); w1t[:, :HID] = inputs["L1_ffn_w1_q"].T
    w2t = np.zeros((D, HID_PAD), np.int32); w2t[:, :HID] = inputs["L1_ffn_w2_q"].T
    s1t = np.zeros((64, HID_PAD), np.float32); s1t[:, :HID] = inputs["L1_ffn_w1_scale"].T
    s2t = np.zeros((64, HID_PAD), np.float32); s2t[:, :HID] = inputs["L1_ffn_w2_scale"].T
    m["w1qT"] = np.ascontiguousarray(w1t[:, core * SH12:(core + 1) * SH12])
    m["w2qT"] = np.ascontiguousarray(w2t[:, core * SH12:(core + 1) * SH12])
    m["s1T"] = np.ascontiguousarray(s1t[:, core * SH12:(core + 1) * SH12])
    m["s2T"] = np.ascontiguousarray(s2t[:, core * SH12:(core + 1) * SH12])
    w3t = inputs["L1_ffn_w3_q"].T
    s3t = inputs["L1_ffn_w3_scale"].T
    m["w3qT"] = np.ascontiguousarray(w3t[:, core * SH3:(core + 1) * SH3])
    m["s3T"] = np.ascontiguousarray(s3t[:, core * SH3:(core + 1) * SH3])
    cos, sin = _host_rope(s0)
    sgn = np.where(np.arange(128) < 64, -1.0, 1.0).astype(np.float32)
    m["cosTt"] = np.ascontiguousarray(cos.T).astype(BF)
    cT = cos.T; sT = (sin * sgn[None, :]).T
    c4 = np.zeros((128, 4096), np.float32); s4 = np.zeros((128, 4096), np.float32)
    for ci in range(8):
        for r in range(4):
            c4[:, ci * 512 + r * 128:ci * 512 + (r + 1) * 128] = cT[:, 64 + ci * 128:64 + (ci + 1) * 128]
            s4[:, ci * 512 + r * 128:ci * 512 + (r + 1) * 128] = sT[:, 64 + ci * 128:64 + (ci + 1) * 128]
    m["cosT4"] = c4.astype(BF); m["sinT4"] = s4.astype(BF)
    m["sinTt"] = np.ascontiguousarray((sin * sgn[None, :]).T).astype(BF)
    for k, v in C.items():
        m["c_" + k] = np.asarray(v)
    mvec = np.zeros(8, np.float32)
    for j in range(8):
        if j // 4 == beta and j < core:
            mvec[j] = 1.0
    sm = np.zeros((128, 16), np.float32)
    sm[:, 0:8] = mvec[None, :]
    sm[:, 8:16] = 1.0 - mvec[None, :]
    m["scanm"] = sm
    return m


def kernel(**inputs):
    nc, C, dbg = build_program()
    in_maps = [host_inputs(inputs, c) for c in range(NC)]
    res = run_bass_kernel_spmd(nc, in_maps, core_ids=list(range(NC)))
    full = np.zeros((2, 4096, D), np.float32)
    for c in range(NC):
        beta, sq = c // 4, c % 4
        full[beta, sq * NT:(sq + 1) * NT, :] = res.results[c]["out"].T
    return full
